# version 01
# baseline (speedup 1.0000x reference)
import numpy as np
import ml_dtypes
import sys

sys.path.insert(0, "/opt/trn_rl_repo")
import concourse.bass as bass
import concourse.mybir as mybir
from concourse.bass_utils import run_bass_kernel_spmd
from contextlib import ExitStack

F32 = mybir.dt.float32
BF16 = mybir.dt.bfloat16
AF = mybir.ActivationFunctionType
ALU = mybir.AluOpType
AX = mybir.AxisListType

T = 4096
D = 1024
H = 4
DK = 256
DV = 512
E = 16
HID = 512
S2 = 2048
CB = 128
NCH = T // CB
NT = 512
NTILES = T // NT
NTA = 256
NTILESA = T // NTA
EPS = 1e-5
GAMMA = [1.0 - 2.0 ** (-5.0 - h) for h in range(H)]
GC = [g ** CB for g in GAMMA]

PHASES = ["C", "D", "Ep", "A1", "B"]


def _host_constants():
    c = {}
    inv_freq = 1.0 / 10000.0 ** (np.arange(0, DK, 2, dtype=np.float64) / DK)
    ang = np.arange(T, dtype=np.float64)[None, :] * inv_freq[:, None]
    c["cos_t"] = np.cos(ang).astype(np.float32)
    c["sin_t"] = np.sin(ang).astype(np.float32)
    i = np.arange(CB, dtype=np.float64)
    dmt = np.zeros((H, CB, CB), np.float32)
    crs = np.zeros((H, CB, CB), np.float32)
    kdc = np.zeros((CB, H), np.float32)
    for h in range(H):
        g = GAMMA[h]
        diff = i[:, None] - i[None, :]
        m = np.where(diff >= 0, g ** np.maximum(diff, 0), 0.0)
        dmt[h] = m.T.astype(np.float32)
        crs[h] = np.tile((g ** (i + 1.0)).astype(np.float32)[None, :], (CB, 1))
        kdc[:, h] = (g ** (CB - 1.0 - i)).astype(np.float32)
    c["dmaskt"] = dmt
    c["crossb"] = crs
    c["kdec_col"] = kdc
    c["ident_f"] = np.eye(128, dtype=np.float32)
    c["ident_b"] = np.eye(128).astype(ml_dtypes.bfloat16)
    c["ones_b"] = np.ones((1, 128), ml_dtypes.bfloat16)
    return c


def _host_layouts(inputs):
    d = {}
    d["wq"] = np.ascontiguousarray(inputs["L0_attn_Wq"], np.float32)
    d["wk"] = np.ascontiguousarray(inputs["L0_attn_Wk"], np.float32)
    d["wv"] = np.ascontiguousarray(inputs["L0_attn_Wv"], np.float32)
    d["wg"] = np.ascontiguousarray(inputs["L0_attn_Wg"], np.float32)
    d["wo"] = np.ascontiguousarray(inputs["L0_attn_Wo"], np.float32)
    d["sfc1t"] = np.ascontiguousarray(np.asarray(inputs["L0_ffn_sfc1"]).T, np.float32)
    d["sfc2t"] = np.ascontiguousarray(np.asarray(inputs["L0_ffn_sfc2"]).T, np.float32)
    d["sfc3t"] = np.ascontiguousarray(np.asarray(inputs["L0_ffn_sfc3"]).T, np.float32)
    d["w1t"] = np.ascontiguousarray(np.asarray(inputs["L0_ffn_w1"]).transpose(0, 2, 1), np.float32)
    d["w2t"] = np.ascontiguousarray(np.asarray(inputs["L0_ffn_w2"]).transpose(0, 2, 1), np.float32)
    d["gatet"] = np.ascontiguousarray(np.asarray(inputs["L0_ffn_gate_w"]).T, np.float32)
    d["lnw_bc"] = np.ascontiguousarray(
        np.broadcast_to(np.asarray(inputs["L0_attn_ln_w"])[None, :], (128, D)), np.float32)
    d["lnb_col"] = np.ascontiguousarray(
        np.asarray(inputs["L0_attn_ln_b"]).reshape(D // 128, 128).T, np.float32)
    d["gnw_col"] = np.ascontiguousarray(
        np.asarray(inputs["L0_attn_g_norm_w"]).reshape(DV // 128, 128).T, np.float32)
    return d


def mm(eng, out, lhsT, rhs, start, stop):
    return eng.matmul(out=out, lhsT=lhsT, rhs=rhs, start=start, stop=stop)


def tp(eng, out, in_, ident, start, stop):
    return eng.matmul(out=out, lhsT=in_, rhs=ident, is_transpose=True,
                      start=start, stop=stop)


def build_program(debug_outputs=(), phases=PHASES):
    nc = bass.Bass("TRN2", target_bir_lowering=False, debug=False,
                   detect_race_conditions=False)

    def dram_in(name, shape, dt=F32):
        return nc.dram_tensor(name, list(shape), dt, kind="ExternalInput").ap()

    def scratch(name, shape, dt):
        kind = "ExternalOutput" if name in debug_outputs else "Internal"
        return nc.dram_tensor(name, list(shape), dt, kind=kind).ap()

    x_d = dram_in("x", (T, D))
    wq_d = dram_in("wq", (D, D))
    wk_d = dram_in("wk", (D, D))
    wv_d = dram_in("wv", (D, 2 * D))
    wg_d = dram_in("wg", (D, 2 * D))
    wo_d = dram_in("wo", (2 * D, D))
    sfc1t_d = dram_in("sfc1t", (D, S2))
    sfc2t_d = dram_in("sfc2t", (D, S2))
    sfc3t_d = dram_in("sfc3t", (S2, D))
    w1t_d = dram_in("w1t", (E, D, HID))
    w2t_d = dram_in("w2t", (E, HID, D))
    gatet_d = dram_in("gatet", (D, E))
    lnwbc_d = dram_in("lnw_bc", (128, D))
    lnbcol_d = dram_in("lnb_col", (128, 8))
    gnwcol_d = dram_in("gnw_col", (128, 4))
    cos_d = dram_in("cos_t", (128, T))
    sin_d = dram_in("sin_t", (128, T))
    dmaskt_d = dram_in("dmaskt", (H, CB, CB))
    crossb_d = dram_in("crossb", (H, CB, CB))
    kdeccol_d = dram_in("kdec_col", (CB, H))
    identf_d = dram_in("ident_f", (128, 128))
    identb_d = dram_in("ident_b", (128, 128), BF16)
    onesb_d = dram_in("ones_b", (1, 128), BF16)

    out_d = nc.dram_tensor("out", [T, D], F32, kind="ExternalOutput").ap()

    xnfm_d = scratch("xnfm", (8, 128, T), BF16)
    sh_d = scratch("sh", (T, D), F32)
    moeps_d = scratch("moeps", (T, D), F32)
    gf_d = scratch("gf", (T, D), BF16)
    qp_d = scratch("qp", (8, 128, T), BF16)
    kp_d = scratch("kp", (8, 128, T), BF16)
    vtm_d = scratch("vtm", (T, 2 * D), BF16)
    dispt_d = scratch("dispt", (16, T), BF16)
    sgtm_d = scratch("sgtm", (T, 2 * D), BF16)
    dbg = {}
    if "dispdbg" in debug_outputs:
        dbg["dispdbg"] = nc.dram_tensor("dispdbg", [16, T], BF16,
                                        kind="ExternalOutput").ap()
    if "statsdbg" in debug_outputs:
        dbg["statsdbg"] = nc.dram_tensor("statsdbg", [128, 4 * NCH], F32,
                                         kind="ExternalOutput").ap()

    es = ExitStack()

    identf = nc.alloc_sbuf_tensor("identf", [128, 128], F32)
    identb = nc.alloc_sbuf_tensor("identb", [128, 128], BF16)
    onesb = nc.alloc_sbuf_tensor("onesb", [1, 128], BF16)
    negm1 = nc.alloc_sbuf_tensor("negm1", [128, NCH], F32)
    rstd1 = nc.alloc_sbuf_tensor("rstd1", [128, NCH], F32)
    negm2 = nc.alloc_sbuf_tensor("negm2", [128, NCH], F32)
    rstd2 = nc.alloc_sbuf_tensor("rstd2", [128, NCH], F32)
    lnbcol = nc.alloc_sbuf_tensor("lnbcol", [128, 8], F32)
    lnbcol_b = nc.alloc_sbuf_tensor("lnbcol_b", [128, 8], BF16)

    def mkphase(fn):
        with ExitStack() as ph:
            aa = lambda n, s, dt: ph.enter_context(nc.sbuf_tensor(n, s, dt))
            pp = lambda n, s, dt: ph.enter_context(nc.psum_tensor(n, s, dt))
            return fn(aa, pp)

    C = mkphase(lambda a, p: dict(
        sfc1b=a("C_sfc1b", [128, 8 * S2], BF16),
        sfc2b=a("C_sfc2b", [128, 8 * S2], BF16),
        sfc3b=a("C_sfc3b", [128, 16 * D], BF16),
        gateb=a("C_gateb", [128, 8 * E], BF16),
        stg=a("C_stg", [128, 2 * 2048], F32),
        x=a("C_x", [128, 4 * D], F32),
        xn=a("C_xn", [128, 4 * D], F32),
        fm=a("C_fm", [128, 8 * NT], BF16),
        hs=a("C_hs", [128, 16 * NT], BF16),
        scr=a("C_scr", [128, 3 * D], F32),
        silu=a("C_silu", [128, 2 * NT], F32),
        cols=a("C_cols", [128, 16], F32),
        lg=a("C_lg", [128, 16], F32),
        lg2=a("C_lg2", [128, 16], F32),
        eq1=a("C_eq1", [128, 16], F32),
        eq2=a("C_eq2", [128, 16], F32),
        disp=a("C_disp", [128, 16], F32),
        dispT=a("C_dispT", [16, T], BF16),
        mcol=a("C_mcol", [128, 8], F32),
        P=[p(f"C_P{i}", [128, 512], F32) for i in range(6)],
        TPB=[p(f"C_TPB{i}", [128, 512], F32) for i in range(2)],
    ))

    DE = mkphase(lambda a, p: dict(
        w1b=a("D_w1b", [128, 8 * 8 * 512], BF16),
        w2b=a("D_w2b", [128, 8 * 4 * 1024], BF16),
        stg=a("D_stg", [128, 2 * 1024], F32),
        fm=a("D_fm", [128, 8 * NT], BF16),
        hsc=a("D_hsc", [128, 8 * 4 * 512], BF16),
        stream=a("D_stream", [128, 2 * D], F32),
        outh=a("D_outh", [128, 2 * D], F32),
        gfh=a("D_gfh", [128, 2 * D], BF16),
        silu=a("D_silu", [128, 2 * 512], F32),
        dprow=a("D_dprow", [1, 2 * 512], BF16),
        P=[p(f"D_P{i}", [128, 512], F32) for i in range(6)],
    ))

    B = mkphase(lambda a, p: dict(
        wob=a("B_wob", [128, 16 * D], BF16),
        stg=a("B_stg", [128, 2 * D], F32),
        gnw=a("B_gnw", [128, 4], F32),
        dmask=a("B_dmask", [128, 4 * CB], F32),
        cross=a("B_cross", [128, 4 * CB], F32),
        kdec=a("B_kdec", [128, 4], F32),
        Sf=a("B_Sf", [128, 2 * H * 2 * DV], F32),
        Sb=a("B_Sb", [128, H * 2 * DV], BF16),
        q=a("B_q", [128, 2 * 8 * CB], BF16),
        k=a("B_k", [128, 2 * 8 * CB], BF16),
        v=a("B_v", [128, 2 * 2048], BF16),
        sgs=a("B_sgs", [128, 2 * 2048], BF16),
        xs=a("B_xs", [128, 2 * D], F32),
        gfs=a("B_gfs", [128, 2 * D], BF16),
        xn=a("B_xn", [128, D], F32),
        q2=a("B_q2", [128, 2 * CB], BF16),
        k2=a("B_k2", [128, DK], BF16),
        at=a("B_at", [128, CB], BF16),
        onf=a("B_onf", [128, DV], F32),
        og=a("B_og", [128, H * DV], BF16),
        ogfm=a("B_ogfm", [128, 16 * 128], BF16),
        sq=a("B_sq", [128, DV], F32),
        cols=a("B_cols", [128, 16], F32),
        so=a("B_so", [128, D], F32),
        t1=a("B_t1", [128, D], F32),
        outt=a("B_outt", [128, 2 * D], F32),
        P=[p(f"B_P{i}", [128, 512], F32) for i in range(6)],
        KTB=p("B_KTB", [128, 1024], BF16),
        OGB=p("B_OGB", [128, 1024], BF16),
    ))

    sem_names = [
        "s_in", "s_cst", "s_oxn", "s_osh0", "s_osh1",
        "s_wstg0", "s_wstg1", "s_wstgf",
        "c_st1", "c_c1", "c_sq1", "c_xn", "c_st2", "c_c2", "c_sq2",
        "c_tp", "c_fme", "c_ab", "c_silu", "c_hs", "c_s3", "c_s3e",
        "c_lg", "c_lge", "c_disp", "c_dtp", "c_dtpe",
        "d_wst0", "d_wst1", "d_wstf", "d_fm", "d_st0", "d_st1",
        "d_dp0", "d_dp1", "d_h", "d_bc", "d_silu", "d_hsc", "d_acc", "d_ev",
        "d_gf", "d_out0", "d_out1",
        "a_wst0", "a_wst1", "a_wstf", "a_cst", "a_x0", "a_x1", "a_cs0", "a_cs1", "a_xn",
        "a_nrm", "a_tp", "a_nfm", "a_qk", "a_qke", "a_rope", "a_vg", "a_vge",
        "a_oq", "a_ov", "a_pre", "a_pree",
        "b_cst", "b_wst0", "b_wst1", "b_wstf",
        "b_iq0", "b_iq1", "b_ik0", "b_ik1", "b_iv0", "b_iv1",
        "b_isg0", "b_isg1", "b_ix0", "b_ix1", "b_igf0", "b_igf1",
        "b_pe", "b_mask", "b_q2", "b_k2", "b_sq", "b_rc1", "b_rc2", "b_rc3",
        "b_on", "b_og", "b_stt", "b_sbf", "b_ogt", "b_oge", "b_wo", "b_woev",
        "b_comb", "b_oo0", "b_oo1",
    ]

    class _S:
        pass
    sems = _S()
    for n in sem_names:
        setattr(sems, n, es.enter_context(nc.semaphore(n)))

    C_WCHUNKS = 8 + 8 + 16 + 1

    def c_sync(eng):
        srcs = []
        for kc in range(8):
            srcs.append((sfc1t_d[kc * 128:(kc + 1) * 128, :], 2048))
        for kc in range(8):
            srcs.append((sfc2t_d[kc * 128:(kc + 1) * 128, :], 2048))
        for sc in range(16):
            srcs.append((sfc3t_d[sc * 128:(sc + 1) * 128, :], 1024))
        srcs.append((gatet_d.rearrange("(kc p) e -> p kc e", p=128), 128))
        for k, (src, w) in enumerate(srcs):
            if k >= 2:
                eng.wait_ge(sems.s_wstgf, k - 1)
            dst = C["stg"][:, (k % 2) * 2048:(k % 2) * 2048 + w]
            if len(src.shape) == 3:
                dst = dst.rearrange("p (kc e) -> p kc e", kc=src.shape[1])
            eng.dma_start(out=dst, in_=src).then_inc(
                [sems.s_wstg0, sems.s_wstg1][k % 2], 16)
        eng.dma_start(out=identf[:, :], in_=identf_d).then_inc(sems.s_cst, 16)
        eng.dma_start(out=identb[:, :], in_=identb_d).then_inc(sems.s_cst, 16)
        eng.dma_start(out=onesb[:, :], in_=onesb_d).then_inc(sems.s_cst, 16)
        eng.dma_start(out=lnbcol[:, :], in_=lnbcol_d).then_inc(sems.s_cst, 16)
        for i in range(NTILES):
            if i >= 1:
                eng.wait_ge(sems.c_xn, i)
            eng.dma_start(
                out=C["x"][:, :].rearrange("p (s d) -> p s d", s=4),
                in_=x_d[i * NT:(i + 1) * NT, :].rearrange("(s p) d -> p s d", p=128),
            ).then_inc(sems.s_in, 16)

    def c_pool(eng):
        for k in range(C_WCHUNKS):
            eng.wait_ge([sems.s_wstg0, sems.s_wstg1][k % 2], 16 * (k // 2 + 1))
            slot = C["stg"][:, (k % 2) * 2048:(k % 2) * 2048 + 2048]
            if k < 8:
                dst, src = C["sfc1b"][:, k * 2048:(k + 1) * 2048], slot[:, 0:2048]
            elif k < 16:
                dst, src = C["sfc2b"][:, (k - 8) * 2048:(k - 7) * 2048], slot[:, 0:2048]
            elif k < 32:
                dst, src = C["sfc3b"][:, (k - 16) * 1024:(k - 15) * 1024], slot[:, 0:1024]
            else:
                dst, src = C["gateb"][:, :], slot[:, 0:128]
            eng.tensor_copy(out=dst, in_=src).then_inc(sems.s_wstgf, 1)
        eng.wait_ge(sems.s_cst, 4 * 16)
        eng.tensor_copy(out=lnbcol_b[:, :], in_=lnbcol[:, :]).then_inc(sems.s_wstgf, 1)

    def c_act(eng):
        for i in range(NTILES):
            eng.wait_ge(sems.s_in, 16 * (i + 1))
            for st in range(4):
                eng.activation(out=C["scr"][:, 0:D], in_=C["x"][:, st * D:(st + 1) * D],
                               func=AF.Square, accum_out=C["cols"][:, st:st + 1])
                a = eng.activation(out=C["scr"][:, 0:D], in_=C["x"][:, st * D:(st + 1) * D],
                                   func=AF.Identity, accum_out=C["cols"][:, 4 + st:4 + st + 1])
            a.then_inc(sems.c_st1, 1)
            eng.wait_ge(sems.c_c1, i + 1)
            eng.activation(out=C["mcol"][:, 0:4], in_=C["cols"][:, 8:12],
                           func=AF.Sqrt).then_inc(sems.c_sq1, 1)
            eng.wait_ge(sems.c_xn, i + 1)
            for st in range(4):
                eng.activation(out=C["scr"][:, 0:D], in_=C["xn"][:, st * D:(st + 1) * D],
                               func=AF.Square, accum_out=C["cols"][:, st:st + 1])
                a = eng.activation(out=C["scr"][:, 0:D], in_=C["xn"][:, st * D:(st + 1) * D],
                                   func=AF.Identity, accum_out=C["cols"][:, 4 + st:4 + st + 1])
            a.then_inc(sems.c_st2, 1)
            eng.wait_ge(sems.c_c2, i + 1)
            eng.activation(out=C["mcol"][:, 4:8], in_=C["cols"][:, 8:12],
                           func=AF.Sqrt).then_inc(sems.c_sq2, 1)
            for sc in range(16):
                Ksc = i * 16 + sc
                eng.wait_ge(sems.c_ab, Ksc + 1)
                if Ksc >= 2:
                    eng.wait_ge(sems.c_hs, Ksc - 1)
                eng.activation(out=C["silu"][:, (sc % 2) * NT:(sc % 2 + 1) * NT],
                               in_=C["P"][sc % 2][:, :],
                               func=AF.Silu).then_inc(sems.c_silu, 1)
            for mc in range(4):
                Km = i * 4 + mc
                eng.wait_ge(sems.c_disp, 2 * Km + 1)
                eng.activation(out=C["mcol"][:, 3:4], in_=C["mcol"][:, 2:3],
                               func=AF.Tanh, scale=0.5).then_inc(sems.c_dtp, 1)

    def c_vector(eng):
        for i in range(NTILES):
            eng.wait_ge(sems.c_st1, i + 1)
            nm = negm1[:, i * 4:(i + 1) * 4]
            eng.tensor_scalar(out=nm, in0=C["cols"][:, 4:8], scalar1=-1.0 / D,
                              scalar2=None, op0=ALU.mult)
            eng.drain()
            eng.tensor_mul(out=C["cols"][:, 12:16], in0=nm, in1=nm)
            eng.drain()
            eng.scalar_tensor_tensor(out=C["cols"][:, 8:12], in0=C["cols"][:, 0:4],
                                     scalar=1.0 / D, in1=C["cols"][:, 12:16],
                                     op0=ALU.mult, op1=ALU.subtract)
            eng.drain()
            eng.tensor_scalar(out=C["cols"][:, 8:12], in0=C["cols"][:, 8:12],
                              scalar1=EPS, scalar2=None,
                              op0=ALU.add).then_inc(sems.c_c1, 1)
            eng.wait_ge(sems.c_sq1, i + 1)
            eng.reciprocal(out=rstd1[:, i * 4:(i + 1) * 4], in_=C["mcol"][:, 0:4])
            eng.drain()
            for st in range(4):
                c = i * 4 + st
                a = eng.tensor_scalar(out=C["xn"][:, st * D:(st + 1) * D],
                                      in0=C["x"][:, st * D:(st + 1) * D],
                                      scalar1=negm1[:, c:c + 1], scalar2=rstd1[:, c:c + 1],
                                      op0=ALU.add, op1=ALU.mult)
            a.then_inc(sems.c_xn, 1)
            eng.wait_ge(sems.c_st2, i + 1)
            nm2 = negm2[:, i * 4:(i + 1) * 4]
            eng.tensor_scalar(out=nm2, in0=C["cols"][:, 4:8], scalar1=-1.0 / D,
                              scalar2=None, op0=ALU.mult)
            eng.drain()
            eng.tensor_mul(out=C["cols"][:, 12:16], in0=nm2, in1=nm2)
            eng.drain()
            eng.scalar_tensor_tensor(out=C["cols"][:, 8:12], in0=C["cols"][:, 0:4],
                                     scalar=1.0 / D, in1=C["cols"][:, 12:16],
                                     op0=ALU.mult, op1=ALU.subtract)
            eng.drain()
            eng.tensor_scalar(out=C["cols"][:, 8:12], in0=C["cols"][:, 8:12],
                              scalar1=EPS, scalar2=None,
                              op0=ALU.add).then_inc(sems.c_c2, 1)
            eng.wait_ge(sems.c_sq2, i + 1)
            eng.reciprocal(out=rstd2[:, i * 4:(i + 1) * 4], in_=C["mcol"][:, 4:8])
            for g in range(8):
                K = i * 8 + g
                st, gg = g // 2, g % 2
                eng.wait_ge(sems.c_tp, K + 1)
                if i > 0 and g == 0:
                    eng.wait_ge(sems.s_oxn, 16 * i)
                eng.tensor_copy(
                    out=C["fm"][:, :].rearrange("p (kc t) -> p kc t", kc=8)
                        [:, gg * 4:(gg + 1) * 4, st * 128:(st + 1) * 128],
                    in_=C["TPB"][K % 2][:, :].rearrange("p (s f) -> p s f", s=4)
                ).then_inc(sems.c_fme, 1)
            for sc in range(16):
                Ksc = i * 16 + sc
                eng.wait_ge(sems.c_silu, Ksc + 1)
                eng.tensor_mul(out=C["hs"][:, sc * NT:(sc + 1) * NT],
                               in0=C["silu"][:, (sc % 2) * NT:(sc % 2 + 1) * NT],
                               in1=C["P"][2 + sc % 2][:, :]).then_inc(sems.c_hs, 1)
            for g in range(8):
                K = i * 8 + g
                mc, ng = g // 2, g % 2
                eng.wait_ge(sems.c_s3, K + 1)
                if ng == 0:
                    occ = 2 * i + mc // 2
                    if occ > 0:
                        eng.wait_ge([sems.s_osh0, sems.s_osh1][mc % 2], 16 * occ)
                off = (1 + mc % 2) * D + ng * 512
                eng.tensor_copy(out=C["scr"][:, off:off + 512],
                                in_=C["P"][4 + ng][:, :]).then_inc(sems.c_s3e, 1)
            for mc in range(4):
                Km = i * 4 + mc
                eng.wait_ge(sems.c_lg, Km + 1)
                eng.tensor_copy(out=C["lg"][:, :],
                                in_=C["P"][4][:, 0:16]).then_inc(sems.c_lge, 1)
                eng.drain()
                eng.reduce_max(out=C["mcol"][:, 0:1], in_=C["lg"][:, :], axis=AX.X)
                eng.drain()
                eng.tensor_scalar(out=C["eq1"][:, :], in0=C["lg"][:, :],
                                  scalar1=C["mcol"][:, 0:1], scalar2=None, op0=ALU.is_ge)
                eng.drain()
                eng.scalar_tensor_tensor(out=C["lg2"][:, :], in0=C["eq1"][:, :],
                                         scalar=-1e30, in1=C["lg"][:, :],
                                         op0=ALU.mult, op1=ALU.add)
                eng.drain()
                eng.reduce_max(out=C["mcol"][:, 1:2], in_=C["lg2"][:, :], axis=AX.X)
                eng.drain()
                eng.tensor_scalar(out=C["eq2"][:, :], in0=C["lg2"][:, :],
                                  scalar1=C["mcol"][:, 1:2], scalar2=None, op0=ALU.is_ge)
                eng.tensor_sub(out=C["mcol"][:, 2:3], in0=C["mcol"][:, 0:1],
                               in1=C["mcol"][:, 1:2]).then_inc(sems.c_disp, 1)
                eng.wait_ge(sems.c_dtp, Km + 1)
                eng.tensor_scalar(out=C["mcol"][:, 4:5], in0=C["mcol"][:, 3:4],
                                  scalar1=0.5, scalar2=0.5, op0=ALU.mult, op1=ALU.add)
                eng.drain()
                eng.tensor_scalar(out=C["mcol"][:, 5:6], in0=C["mcol"][:, 4:5],
                                  scalar1=-1.0, scalar2=1.0, op0=ALU.mult, op1=ALU.add)
                eng.tensor_scalar(out=C["disp"][:, :], in0=C["eq1"][:, :],
                                  scalar1=C["mcol"][:, 4:5], scalar2=None, op0=ALU.mult)
                eng.drain()
                eng.scalar_tensor_tensor(out=C["disp"][:, :], in0=C["eq2"][:, :],
                                         scalar=C["mcol"][:, 5:6], in1=C["disp"][:, :],
                                         op0=ALU.mult,
                                         op1=ALU.add).then_inc(sems.c_disp, 1)
                eng.wait_ge(sems.c_dtpe, Km + 1)
                eng.tensor_copy(out=C["dispT"][0:16, Km * 128:(Km + 1) * 128],
                                in_=C["P"][5][0:16, 0:128]).then_inc(sems.c_lge, 1)

    def c_tensor(eng):
        eng.wait_ge(sems.s_wstgf, C_WCHUNKS + 1)
        for i in range(NTILES):
            eng.wait_ge(sems.c_xn, i + 1)
            for g in range(8):
                K = i * 8 + g
                st, gg = g // 2, g % 2
                if K >= 2:
                    eng.wait_ge(sems.c_fme, K - 1)
                for s in range(4):
                    kc = gg * 4 + s
                    a = tp(eng, C["TPB"][K % 2][:, s * 128:(s + 1) * 128],
                           C["xn"][:, st * D + kc * 128: st * D + (kc + 1) * 128],
                           identf[:, :], start=(s == 0), stop=(s == 3))
                a.then_inc(sems.c_tp, 1)
            eng.wait_ge(sems.c_fme, (i + 1) * 8)
            fmv = C["fm"][:, :].rearrange("p (kc t) -> p kc t", kc=8)
            for sc in range(16):
                Ksc = i * 16 + sc
                if Ksc >= 2:
                    eng.wait_ge(sems.c_silu, Ksc - 1)
                    eng.wait_ge(sems.c_hs, Ksc - 1)
                for kc in range(8):
                    mm(eng, C["P"][sc % 2][:, :],
                       C["sfc1b"][:, kc * 2048 + sc * 128: kc * 2048 + (sc + 1) * 128],
                       fmv[:, kc, :], start=(kc == 0), stop=(kc == 7))
                for kc in range(8):
                    a = mm(eng, C["P"][2 + sc % 2][:, :],
                           C["sfc2b"][:, kc * 2048 + sc * 128: kc * 2048 + (sc + 1) * 128],
                           fmv[:, kc, :], start=(kc == 0), stop=(kc == 7))
                a.then_inc(sems.c_ab, 1)
            eng.wait_ge(sems.c_hs, (i + 1) * 16)
            hsv = C["hs"][:, :].rearrange("p (sc t) -> p sc t", sc=16)
            for g in range(8):
                K = i * 8 + g
                mc, ng = g // 2, g % 2
                if K >= 2:
                    eng.wait_ge(sems.c_s3e, K - 1)
                for sc in range(16):
                    a = mm(eng, C["P"][4 + ng][:, :],
                           hsv[:, sc, mc * 128:(mc + 1) * 128],
                           C["sfc3b"][:, sc * D + ng * 512: sc * D + ng * 512 + 512],
                           start=(sc == 0), stop=(sc == 15))
                a.then_inc(sems.c_s3, 1)
            for mc in range(4):
                Km = i * 4 + mc
                if mc == 0:
                    eng.wait_ge(sems.c_s3e, i * 8 + 8)
                if Km >= 1:
                    eng.wait_ge(sems.c_lge, 2 * Km - 1)
                for kc in range(8):
                    a = mm(eng, C["P"][4][:, 0:16],
                           fmv[:, kc, mc * 128:(mc + 1) * 128],
                           C["gateb"][:, kc * 16:(kc + 1) * 16],
                           start=(kc == 0), stop=(kc == 7))
                a.then_inc(sems.c_lg, 1)
                eng.wait_ge(sems.c_disp, 2 * Km + 2)
                if Km >= 1:
                    eng.wait_ge(sems.c_lge, 2 * Km)
                tp(eng, C["P"][5][0:16, 0:128], C["disp"][:, :], identf[:, :],
                   start=True, stop=True).then_inc(sems.c_dtpe, 1)

    def c_gpsimd_out(eng):
        for i in range(NTILES):
            eng.wait_ge(sems.c_fme, (i + 1) * 8)
            eng.dma_start(
                out=xnfm_d[:, :, i * NT:(i + 1) * NT].rearrange("kc p t -> p kc t"),
                in_=C["fm"][:, :].rearrange("p (kc t) -> p kc t", kc=8),
            ).then_inc(sems.s_oxn, 16)
            for mc in range(4):
                eng.wait_ge(sems.c_s3e, i * 8 + 2 * (mc + 1))
                off = (1 + mc % 2) * D
                eng.dma_start(
                    out=sh_d[i * NT + mc * 128: i * NT + (mc + 1) * 128, :],
                    in_=C["scr"][:, off:off + D],
                ).then_inc([sems.s_osh0, sems.s_osh1][mc % 2], 16)
        nxn = 16 * NTILES
        eng.wait_ge(sems.c_lge, 2 * 4 * NTILES)
        eng.dma_start(out=dispt_d, in_=C["dispT"][:, :]).then_inc(sems.s_oxn, 16)
        nxn += 16
        if dbg:
            eng.wait_ge(sems.c_lge, 2 * 4 * NTILES)
        if "dispdbg" in dbg:
            eng.dma_start(out=dbg["dispdbg"], in_=C["dispT"][:, :]).then_inc(sems.s_oxn, 16)
            nxn += 16
        if "statsdbg" in dbg:
            eng.dma_start(out=dbg["statsdbg"][:, 0:NCH], in_=negm1[:, :]).then_inc(sems.s_oxn, 16)
            eng.dma_start(out=dbg["statsdbg"][:, NCH:2 * NCH], in_=rstd1[:, :]).then_inc(sems.s_oxn, 16)
            eng.dma_start(out=dbg["statsdbg"][:, 2 * NCH:3 * NCH], in_=negm2[:, :]).then_inc(sems.s_oxn, 16)
            eng.dma_start(out=dbg["statsdbg"][:, 3 * NCH:4 * NCH], in_=rstd2[:, :]).then_inc(sems.s_oxn, 16)
            nxn += 64
        eng.wait_ge(sems.s_oxn, nxn)
        eng.wait_ge(sems.s_osh0, 16 * 2 * NTILES)
        eng.wait_ge(sems.s_osh1, 16 * 2 * NTILES)

    DE_TOT = dict(wst=512, wstf=64, fm=16 * NTILES, st=16 * 2 * NTILES,
                  dp=16 * 4 * NTILES, h=32 * NTILES, bc=8 * NTILES,
                  silu=32 * NTILES, hsc=32 * NTILES, acc=8 * NTILES,
                  ev=4 * NTILES, gf=4 * NTILES, out=16 * 2 * NTILES)

    def de_bases(pi):
        b = {k: pi * v for k, v in DE_TOT.items()}
        b["gf"] = 0
        return b

    def de_sync(eng, bs, ge0, stream_d):
        chunks = []
        for e in range(8):
            ge = ge0 + e
            for q4 in range(4):
                chunks.append((w1t_d[ge, q4 * 256:(q4 + 1) * 256, :]
                               .rearrange("(kc p) h -> p kc h", p=128), 2))
            for q4 in range(4):
                chunks.append((w2t_d[ge, q4 * 128:(q4 + 1) * 128, :], 1))
        for k, (sap, nkc) in enumerate(chunks):
            if k >= 2:
                eng.wait_ge(sems.d_wstf, bs["wstf"] + k - 1)
            dst = DE["stg"][:, (k % 2) * 1024:(k % 2) * 1024 + 1024]
            if nkc > 1:
                dst = dst.rearrange("p (kc w) -> p kc w", kc=nkc)
            eng.dma_start(out=dst, in_=sap).then_inc(
                [sems.d_wst0, sems.d_wst1][k % 2], 16)
        for i in range(NTILES):
            w = bs["h"] + 32 * i
            if w > 0:
                eng.wait_ge(sems.d_h, w)
            eng.dma_start(
                out=DE["fm"][:, :].rearrange("p (kc t) -> p kc t", kc=8),
                in_=xnfm_d[:, :, i * NT:(i + 1) * NT].rearrange("kc p t -> p kc t"),
            ).then_inc(sems.d_fm, 16)
            for e in range(8):
                w = bs["bc"] + 8 * i + e - 1
                if w > 0:
                    eng.wait_ge(sems.d_bc, w)
                eng.dma_start(
                    out=DE["dprow"][0:1, (e % 2) * 512:(e % 2) * 512 + 512],
                    in_=dispt_d[ge0 + e:ge0 + e + 1, i * NT:(i + 1) * NT],
                ).then_inc([sems.d_dp0, sems.d_dp1][e % 2], 16)
            for mc in range(4):
                w = bs["ev"] + 4 * i + mc - 1
                if w > 0:
                    eng.wait_ge(sems.d_ev, w)
                eng.dma_start(
                    out=DE["stream"][:, (mc % 2) * D:(mc % 2) * D + D],
                    in_=stream_d[i * NT + mc * 128: i * NT + (mc + 1) * 128, :],
                ).then_inc([sems.d_st0, sems.d_st1][mc % 2], 16)
        eng.wait_ge(sems.d_wst0, bs["wst"] + 16 * 32)
        eng.wait_ge(sems.d_wst1, bs["wst"] + 16 * 32)
        eng.wait_ge(sems.d_fm, bs["fm"] + 16 * NTILES)
        eng.wait_ge(sems.d_dp0, bs["dp"] + 16 * 4 * NTILES)
        eng.wait_ge(sems.d_dp1, bs["dp"] + 16 * 4 * NTILES)
        eng.wait_ge(sems.d_st0, bs["st"] + 16 * 2 * NTILES)
        eng.wait_ge(sems.d_st1, bs["st"] + 16 * 2 * NTILES)

    def de_pool(eng, bs, is_e):
        for k in range(64):
            eng.wait_ge([sems.d_wst0, sems.d_wst1][k % 2],
                        bs["wst"] + 16 * (k // 2 + 1))
            e8 = k // 8
            kind = k % 8
            if kind < 4:
                dst = DE["w1b"][:, e8 * 4096 + kind * 1024: e8 * 4096 + kind * 1024 + 1024]
            else:
                dst = DE["w2b"][:, e8 * 4096 + (kind - 4) * 1024: e8 * 4096 + (kind - 4) * 1024 + 1024]
            eng.tensor_copy(out=dst,
                            in_=DE["stg"][:, (k % 2) * 1024:(k % 2) * 1024 + 1024]
                            ).then_inc(sems.d_wstf, 1)
        for i in range(NTILES):
            for mc in range(4):
                if is_e:
                    eng.wait_ge(sems.d_gf, bs["gf"] + 4 * i + mc + 1)
                    srcb = DE["gfh"][:, (mc % 2) * D:(mc % 2) * D + D]
                    dstd = gf_d[i * NT + mc * 128: i * NT + (mc + 1) * 128, :]
                else:
                    eng.wait_ge(sems.d_ev, bs["ev"] + 4 * i + mc + 1)
                    srcb = DE["outh"][:, (mc % 2) * D:(mc % 2) * D + D]
                    dstd = moeps_d[i * NT + mc * 128: i * NT + (mc + 1) * 128, :]
                eng.dma_start(out=dstd, in_=srcb).then_inc(
                    [sems.d_out0, sems.d_out1][mc % 2], 16)
        eng.wait_ge(sems.d_out0, bs["out"] + 16 * 2 * NTILES)
        eng.wait_ge(sems.d_out1, bs["out"] + 16 * 2 * NTILES)

    def de_tensor(eng, bs, is_e):
        eng.wait_ge(sems.d_wstf, bs["wstf"] + 64)
        fmv = DE["fm"][:, :].rearrange("p (kc t) -> p kc t", kc=8)
        hscv = DE["hsc"][:, :].rearrange("p (e hc t) -> p e hc t", e=8, hc=4)
        for i in range(NTILES):
            eng.wait_ge(sems.d_fm, bs["fm"] + 16 * (i + 1))
            for e in range(8):
                Ke = 8 * i + e
                w = bs["silu"] + 4 * Ke
                if w > 0:
                    eng.wait_ge(sems.d_silu, w)
                for hc in range(4):
                    for kc in range(8):
                        a = mm(eng, DE["P"][hc][:, :],
                               DE["w1b"][:, e * 4096 + kc * 512 + hc * 128:
                                         e * 4096 + kc * 512 + (hc + 1) * 128],
                               fmv[:, kc, :], start=(kc == 0), stop=(kc == 7))
                    a.then_inc(sems.d_h, 1)
                eng.wait_ge([sems.d_dp0, sems.d_dp1][e % 2],
                            bs["dp"] + 16 * (4 * i + e // 2 + 1))
                w = bs["hsc"] + 4 * (Ke - 1)
                if w > 0:
                    eng.wait_ge(sems.d_hsc, w)
                mm(eng, DE["P"][4 + e % 2][:, :], onesb[0:1, :],
                   DE["dprow"][0:1, (e % 2) * 512:(e % 2) * 512 + 512],
                   start=True, stop=True).then_inc(sems.d_bc, 1)
            eng.wait_ge(sems.d_hsc, bs["hsc"] + 32 * (i + 1))
            for mc in range(4):
                if mc >= 1:
                    eng.wait_ge(sems.d_ev, bs["ev"] + 4 * i + mc)
                for ng in range(2):
                    for e in range(8):
                        for kc in range(4):
                            a = mm(eng, DE["P"][4 + ng][:, :],
                                   hscv[:, e, kc, mc * 128:(mc + 1) * 128],
                                   DE["w2b"][:, e * 4096 + kc * 1024 + ng * 512:
                                             e * 4096 + kc * 1024 + ng * 512 + 512],
                                   start=(e == 0 and kc == 0),
                                   stop=(e == 7 and kc == 3))
                    a.then_inc(sems.d_acc, 1)

    def de_act(eng, bs, is_e):
        for i in range(NTILES):
            for e in range(8):
                for hc in range(4):
                    ghc = 32 * i + 4 * e + hc
                    eng.wait_ge(sems.d_h, bs["h"] + ghc + 1)
                    w = bs["hsc"] + ghc - 1
                    if w > 0:
                        eng.wait_ge(sems.d_hsc, w)
                    eng.activation(out=DE["silu"][:, (ghc % 2) * 512:(ghc % 2) * 512 + 512],
                                   in_=DE["P"][hc][:, :],
                                   func=AF.Silu).then_inc(sems.d_silu, 1)
            if is_e:
                for mc in range(4):
                    eng.wait_ge(sems.d_ev, bs["ev"] + 4 * i + mc + 1)
                    w = bs["out"] + 16 * (2 * i + mc // 2)
                    if w > 0:
                        eng.wait_ge([sems.d_out0, sems.d_out1][mc % 2], w)
                    eng.activation(out=DE["gfh"][:, (mc % 2) * D:(mc % 2) * D + D],
                                   in_=DE["outh"][:, (mc % 2) * D:(mc % 2) * D + D],
                                   func=AF.Silu).then_inc(sems.d_gf, 1)

    def de_vector(eng, bs, is_e):
        hscv = DE["hsc"][:, :].rearrange("p (e hc t) -> p e hc t", e=8, hc=4)
        for i in range(NTILES):
            for e in range(8):
                for hc in range(4):
                    ghc = 32 * i + 4 * e + hc
                    eng.wait_ge(sems.d_silu, bs["silu"] + ghc + 1)
                    eng.wait_ge(sems.d_bc, bs["bc"] + 8 * i + e + 1)
                    eng.tensor_mul(out=hscv[:, e, hc, :],
                                   in0=DE["silu"][:, (ghc % 2) * 512:(ghc % 2) * 512 + 512],
                                   in1=DE["P"][4 + e % 2][:, :]).then_inc(sems.d_hsc, 1)
            for mc in range(4):
                eng.wait_ge(sems.d_acc, bs["acc"] + 8 * i + 2 * mc + 2)
                eng.wait_ge([sems.d_st0, sems.d_st1][mc % 2],
                            bs["st"] + 16 * (2 * i + mc // 2 + 1))
                if is_e:
                    w = bs["gf"] + 4 * i + mc - 1
                    if w > 0:
                        eng.wait_ge(sems.d_gf, w)
                else:
                    w = bs["out"] + 16 * (2 * i + mc // 2)
                    if w > 0:
                        eng.wait_ge([sems.d_out0, sems.d_out1][mc % 2], w)
                off = (mc % 2) * D
                eng.tensor_add(out=DE["outh"][:, off:off + 512],
                               in0=DE["P"][4][:, :],
                               in1=DE["stream"][:, off:off + 512])
                eng.tensor_add(out=DE["outh"][:, off + 512:off + D],
                               in0=DE["P"][5][:, :],
                               in1=DE["stream"][:, off + 512:off + D]
                               ).then_inc(sems.d_ev, 1)

    A1 = mkphase(lambda a, p: dict(
        wqb=a("A_wqb", [128, 8 * D], BF16),
        wkb=a("A_wkb", [128, 8 * D], BF16),
        wvb=a("A_wvb", [128, 8 * 2048], BF16),
        wgb=a("A_wgb", [128, 8 * 2048], BF16),
        stg=a("A_stg", [128, 2 * 2048], F32),
        lnwbc=a("A_lnwbc", [128, D], F32),
        cs=a("A_cs", [128, 2 * 2 * NTA], F32),
        brow=a("A_brow", [1, 2048], F32),
        bqcol=a("A_bqcol", [128, 8], F32),
        bkcol=a("A_bkcol", [128, 8], F32),
        bvrow=a("A_bvrow", [1, 2048], BF16),
        bgrow=a("A_bgrow", [1, 2048], BF16),
        x=a("A_x", [128, 2 * 2 * D], F32),
        xn=a("A_xn", [128, 2 * D], F32),
        ntmp=a("A_ntmp", [128, D], F32),
        nrm=a("A_nrm", [128, 2 * D], BF16),
        nfm=a("A_nfm", [128, 8 * NTA], BF16),
        qf=a("A_qf", [128, 2 * NTA], F32),
        kf=a("A_kf", [128, 2 * NTA], F32),
        rt=a("A_rt", [128, 4 * NTA], F32),
        qp=a("A_qp", [128, 8 * NTA], BF16),
        kp=a("A_kp", [128, 8 * NTA], BF16),
        v=a("A_v", [128, 2 * 2048], BF16),
        sg=a("A_sg", [128, 2 * 2048], BF16),
        P=[p(f"A_P{i}", [128, 512], F32) for i in range(6)],
        TPB=[p(f"A_TPB{i}", [128, 1024], BF16) for i in range(2)],
    ))

    def a1_sync(eng):
        chunks = []
        for kc in range(8):
            chunks.append((wq_d[kc * 128:(kc + 1) * 128, :], 1024))
        for kc in range(8):
            chunks.append((wk_d[kc * 128:(kc + 1) * 128, :], 1024))
        for kc in range(8):
            chunks.append((wv_d[kc * 128:(kc + 1) * 128, :], 2048))
        for kc in range(8):
            chunks.append((wg_d[kc * 128:(kc + 1) * 128, :], 2048))
        for k, (sap, w) in enumerate(chunks):
            if k >= 2:
                eng.wait_ge(sems.a_wstf, k - 1)
            eng.dma_start(out=A1["stg"][:, (k % 2) * 2048:(k % 2) * 2048 + w],
                          in_=sap).then_inc([sems.a_wst0, sems.a_wst1][k % 2], 16)
        eng.dma_start(out=A1["lnwbc"][:, :], in_=lnwbc_d).then_inc(sems.a_cst, 16)
        for i in range(NTILESA):
            if i >= 2:
                eng.wait_ge(sems.a_xn, i - 1)
            eng.dma_start(
                out=A1["x"][:, (i % 2) * 2 * D:(i % 2) * 2 * D + 2 * D]
                    .rearrange("p (s d) -> p s d", s=2),
                in_=x_d[i * NTA:(i + 1) * NTA, :].rearrange("(s p) d -> p s d", p=128),
            ).then_inc([sems.a_x0, sems.a_x1][i % 2], 16)
            if i >= 2:
                eng.wait_ge(sems.a_rope, 8 * (i - 1))
            co = (i % 2) * 2 * NTA
            eng.dma_start(out=A1["cs"][:, co:co + NTA],
                          in_=cos_d[:, i * NTA:(i + 1) * NTA]).then_inc(
                              [sems.a_cs0, sems.a_cs1][i % 2], 16)
            eng.dma_start(out=A1["cs"][:, co + NTA:co + 2 * NTA],
                          in_=sin_d[:, i * NTA:(i + 1) * NTA]).then_inc(
                              [sems.a_cs0, sems.a_cs1][i % 2], 16)

    def a1_pool(eng):
        for k in range(32):
            eng.wait_ge([sems.a_wst0, sems.a_wst1][k % 2], 16 * (k // 2 + 1))
            s = A1["stg"][:, (k % 2) * 2048:]
            if k < 8:
                eng.tensor_scalar(out=A1["wqb"][:, k * D:(k + 1) * D],
                                  in0=s[:, 0:1024], scalar1=DK ** -0.5, scalar2=None,
                                  op0=ALU.mult).then_inc(sems.a_wstf, 1)
            elif k < 16:
                eng.tensor_copy(out=A1["wkb"][:, (k - 8) * D:(k - 7) * D],
                                in_=s[:, 0:1024]).then_inc(sems.a_wstf, 1)
            elif k < 24:
                eng.tensor_copy(out=A1["wvb"][:, (k - 16) * 2048:(k - 15) * 2048],
                                in_=s[:, 0:2048]).then_inc(sems.a_wstf, 1)
            else:
                eng.tensor_copy(out=A1["wgb"][:, (k - 24) * 2048:(k - 23) * 2048],
                                in_=s[:, 0:2048]).then_inc(sems.a_wstf, 1)
        for i in range(NTILESA):
            eng.wait_ge(sems.a_rope, 8 * (i + 1))
            eng.dma_start(
                out=qp_d[:, :, i * NTA:(i + 1) * NTA].rearrange("kc p t -> p kc t"),
                in_=A1["qp"][:, :].rearrange("p (kc t) -> p kc t", kc=8),
            ).then_inc(sems.a_oq, 16)
            eng.dma_start(
                out=kp_d[:, :, i * NTA:(i + 1) * NTA].rearrange("kc p t -> p kc t"),
                in_=A1["kp"][:, :].rearrange("p (kc t) -> p kc t", kc=8),
            ).then_inc(sems.a_oq, 16)
            eng.wait_ge(sems.a_vge, 16 * (i + 1))
            eng.dma_start(
                out=vtm_d[i * NTA:(i + 1) * NTA, :].rearrange("(s p) n -> p s n", p=128),
                in_=A1["v"][:, :].rearrange("p (s n) -> p s n", s=2),
            ).then_inc(sems.a_ov, 16)
            eng.dma_start(
                out=sgtm_d[i * NTA:(i + 1) * NTA, :].rearrange("(s p) n -> p s n", p=128),
                in_=A1["sg"][:, :].rearrange("p (s n) -> p s n", s=2),
            ).then_inc(sems.a_ov, 16)
        eng.wait_ge(sems.a_oq, 16 * 2 * NTILESA)
        eng.wait_ge(sems.a_ov, 16 * 2 * NTILESA)

    def a1_tensor(eng):
        rows = ([("q", A1["wqb"], 1024, hx) for hx in range(2)]
                + [("k", A1["wkb"], 1024, hx) for hx in range(2)]
                + [("v", A1["wvb"], 2048, hx) for hx in range(4)]
                + [("g", A1["wgb"], 2048, hx) for hx in range(4)])
        for r, (kind, wb, wid, hx) in enumerate(rows):
            need = {"q": 8, "k": 16, "v": 24, "g": 32}[kind]
            eng.wait_ge(sems.a_wstf, need)
            if r >= 2:
                eng.wait_ge(sems.a_pree, r - 1)
            bank = A1["P"][5 if r % 2 == 0 else 4]
            for kc in range(8):
                a = mm(eng, bank[0:1, 0:512],
                       lnbcol_b[:, kc:kc + 1],
                       wb[:, kc * wid + hx * 512: kc * wid + (hx + 1) * 512],
                       start=(kc == 0), stop=(kc == 7))
            a.then_inc(sems.a_pre, 1)
        for j in range(16):
            eng.wait_ge(sems.a_pree, 12 + j)
            tp(eng, A1["P"][3][:, 0:1], A1["brow"][0:1, j * 128:(j + 1) * 128],
               identf[0:1, 0:1], start=True, stop=True).then_inc(sems.a_pre, 1)
        eng.wait_ge(sems.a_pree, 28)
        nfmv = A1["nfm"][:, :].rearrange("p (kc t) -> p kc t", kc=8)
        for i in range(NTILESA):
            for st in range(2):
                gt = 2 * i + st
                eng.wait_ge(sems.a_nrm, gt + 1)
                if gt >= 2:
                    eng.wait_ge(sems.a_nfm, gt - 1)
                for kc in range(8):
                    a = tp(eng, A1["TPB"][gt % 2][:, kc * 128:(kc + 1) * 128],
                           A1["nrm"][:, st * D + kc * 128: st * D + (kc + 1) * 128],
                           identb[:, :], start=(kc == 0), stop=(kc == 7))
                a.then_inc(sems.a_tp, 1)
            eng.wait_ge(sems.a_nfm, 2 * (i + 1))
            for hp in range(4):
                for qk in range(2):
                    wb = A1["wqb"] if qk == 0 else A1["wkb"]
                    w = 8 * i + (hp - 1) * 2 + qk + 1
                    if w > 0:
                        eng.wait_ge(sems.a_qke, w)
                    for half in range(2):
                        dq = 2 * hp + half
                        bank = A1["P"][qk * 2 + half]
                        for kc in range(8):
                            a = mm(eng, bank[:, 0:NTA],
                                   wb[:, kc * D + dq * 128: kc * D + (dq + 1) * 128],
                                   nfmv[:, kc, :], start=(kc == 0), stop=(kc == 7))
                    a.then_inc(sems.a_qk, 1)
            for st in range(2):
                for vg in range(2):
                    wb = A1["wvb"] if vg == 0 else A1["wgb"]
                    br = A1["bvrow"] if vg == 0 else A1["bgrow"]
                    for ng in range(4):
                        gv = 16 * i + st * 8 + vg * 4 + ng
                        if gv >= 1:
                            eng.wait_ge(sems.a_vge, gv)
                        bank = A1["P"][4 + vg]
                        for kc in range(8):
                            mm(eng, bank[:, :],
                               nfmv[:, kc, st * 128:(st + 1) * 128],
                               wb[:, kc * 2048 + ng * 512: kc * 2048 + (ng + 1) * 512],
                               start=(kc == 0), stop=False)
                        mm(eng, bank[:, :], onesb[0:1, 0:128],
                           br[0:1, ng * 512:(ng + 1) * 512],
                           start=False, stop=True).then_inc(sems.a_vg, 1)

    def a1_act(eng):
        rows = (["q0", "q1", "k0", "k1"] + [f"v{j}" for j in range(4)]
                + [f"g{j}" for j in range(4)])
        for r, tag in enumerate(rows):
            eng.wait_ge(sems.a_pre, r + 1)
            bank = A1["P"][5 if r % 2 == 0 else 4]
            kind, hx = tag[0], int(tag[1])
            if kind == "q":
                dst = A1["brow"][0:1, hx * 512:(hx + 1) * 512]
            elif kind == "k":
                dst = A1["brow"][0:1, 1024 + hx * 512: 1024 + (hx + 1) * 512]
            elif kind == "v":
                dst = A1["bvrow"][0:1, hx * 512:(hx + 1) * 512]
            else:
                dst = A1["bgrow"][0:1, hx * 512:(hx + 1) * 512]
            eng.activation(out=dst, in_=bank[0:1, 0:512],
                           func=AF.Identity).then_inc(sems.a_pree, 1)
        for j in range(16):
            eng.wait_ge(sems.a_pre, 12 + j + 1)
            dst = A1["bqcol"][:, j:j + 1] if j < 8 else A1["bkcol"][:, j - 8:j - 7]
            eng.activation(out=dst, in_=A1["P"][3][:, 0:1],
                           func=AF.Identity).then_inc(sems.a_pree, 1)
        for i in range(NTILESA):
            for st in range(2):
                gt = 2 * i + st
                eng.wait_ge(sems.a_tp, gt + 1)
                eng.activation(out=A1["nfm"][:, :]
                               .rearrange("p (kc t) -> p kc t", kc=8)
                               [:, :, st * 128:(st + 1) * 128],
                               in_=A1["TPB"][gt % 2][:, :]
                               .rearrange("p (kc t) -> p kc t", kc=8),
                               func=AF.Identity).then_inc(sems.a_nfm, 1)
            for st in range(2):
                for vg in range(2):
                    for ng in range(4):
                        gv = 16 * i + st * 8 + vg * 4 + ng
                        eng.wait_ge(sems.a_vg, gv + 1)
                        if i > 0 and st == 0 and vg == 0 and ng == 0:
                            eng.wait_ge(sems.a_ov, 16 * 2 * i)
                        dstbuf = A1["v"] if vg == 0 else A1["sg"]
                        dst = dstbuf[:, st * 2048 + ng * 512: st * 2048 + (ng + 1) * 512]
                        eng.activation(out=dst, in_=A1["P"][4 + vg][:, :],
                                       func=AF.Identity if vg == 0 else AF.Silu
                                       ).then_inc(sems.a_vge, 1)

    def a1_vector(eng):
        for i in range(NTILESA):
            eng.wait_ge([sems.a_x0, sems.a_x1][i % 2], 16 * (i // 2 + 1))
            eng.wait_ge(sems.a_cst, 16)
            xoff = (i % 2) * 2 * D
            for st in range(2):
                c = 2 * i + st
                a = eng.tensor_scalar(out=A1["xn"][:, st * D:(st + 1) * D],
                                      in0=A1["x"][:, xoff + st * D: xoff + (st + 1) * D],
                                      scalar1=negm1[:, c:c + 1], scalar2=rstd1[:, c:c + 1],
                                      op0=ALU.add, op1=ALU.mult)
            a.then_inc(sems.a_xn, 1)
            for st in range(2):
                c = 2 * i + st
                eng.tensor_scalar(out=A1["ntmp"][:, :],
                                  in0=A1["xn"][:, st * D:(st + 1) * D],
                                  scalar1=negm2[:, c:c + 1], scalar2=rstd2[:, c:c + 1],
                                  op0=ALU.add, op1=ALU.mult)
                eng.tensor_mul(out=A1["nrm"][:, st * D:(st + 1) * D],
                               in0=A1["ntmp"][:, :],
                               in1=A1["lnwbc"][:, :]).then_inc(sems.a_nrm, 1)
            eng.wait_ge([sems.a_cs0, sems.a_cs1][i % 2], 32 * (i // 2 + 1))
            co = (i % 2) * 2 * NTA
            coss = A1["cs"][:, co:co + NTA]
            sins = A1["cs"][:, co + NTA:co + 2 * NTA]
            for hp in range(4):
                for qk in range(2):
                    gq = 8 * i + hp * 2 + qk
                    eng.wait_ge(sems.a_qk, gq + 1)
                    if i > 0 and hp == 0 and qk == 0:
                        eng.wait_ge(sems.a_oq, 16 * 2 * i)
                    bcol = A1["bqcol"] if qk == 0 else A1["bkcol"]
                    fbuf = A1["qf"] if qk == 0 else A1["kf"]
                    obuf = A1["qp"] if qk == 0 else A1["kp"]
                    pA = A1["P"][qk * 2][:, 0:NTA]
                    pB = A1["P"][qk * 2 + 1][:, 0:NTA]
                    eng.tensor_scalar(out=fbuf[:, 0:NTA], in0=pA,
                                      scalar1=bcol[:, 2 * hp:2 * hp + 1], scalar2=None,
                                      op0=ALU.add)
                    a = eng.tensor_scalar(out=fbuf[:, NTA:2 * NTA], in0=pB,
                                          scalar1=bcol[:, 2 * hp + 1:2 * hp + 2],
                                          scalar2=None, op0=ALU.add)
                    a.then_inc(sems.a_qke, 1)
                    fA = fbuf[:, 0:NTA]
                    fB = fbuf[:, NTA:2 * NTA]
                    rt = A1["rt"]
                    eng.tensor_mul(out=rt[:, 0:NTA], in0=fA, in1=coss)
                    eng.tensor_mul(out=rt[:, NTA:2 * NTA], in0=fB, in1=sins)
                    eng.tensor_mul(out=rt[:, 2 * NTA:3 * NTA], in0=fB, in1=coss)
                    eng.tensor_mul(out=rt[:, 3 * NTA:4 * NTA], in0=fA, in1=sins)
                    eng.tensor_sub(out=obuf[:, (2 * hp) * NTA:(2 * hp + 1) * NTA],
                                   in0=rt[:, 0:NTA], in1=rt[:, NTA:2 * NTA])
                    eng.tensor_add(out=obuf[:, (2 * hp + 1) * NTA:(2 * hp + 2) * NTA],
                                   in0=rt[:, 2 * NTA:3 * NTA],
                                   in1=rt[:, 3 * NTA:4 * NTA]).then_inc(sems.a_rope, 1)

    def a1_endstate():
        return [(sems.a_vg, 16 * NTILESA), (sems.a_rope, 8 * NTILESA),
                (sems.a_vge, 16 * NTILESA), (sems.a_wstf, 32),
                (sems.a_oq, 16 * 2 * NTILESA), (sems.a_ov, 16 * 2 * NTILESA),
                (sems.a_x0, 16 * NTILESA // 2), (sems.a_x1, 16 * NTILESA // 2),
                (sems.a_cs0, 16 * NTILESA), (sems.a_cs1, 16 * NTILESA),
                (sems.a_wst0, 16 * 16), (sems.a_wst1, 16 * 16),
                (sems.a_cst, 16), (sems.a_nfm, 2 * NTILESA),
                (sems.a_pree, 28), (sems.a_qke, 8 * NTILESA),
                (sems.a_xn, NTILESA), (sems.a_tp, 2 * NTILESA),
                (sems.a_qk, 8 * NTILESA), (sems.a_nrm, 2 * NTILESA)]

    def b_sync(eng):
        eng.dma_start(out=B["gnw"][:, :], in_=gnwcol_d).then_inc(sems.b_cst, 16)
        eng.dma_start(out=B["dmask"][:, :].rearrange("p (h i) -> p h i", h=H),
                      in_=dmaskt_d.rearrange("h p i -> p h i")).then_inc(sems.b_cst, 16)
        eng.dma_start(out=B["cross"][:, :].rearrange("p (h i) -> p h i", h=H),
                      in_=crossb_d.rearrange("h p i -> p h i")).then_inc(sems.b_cst, 16)
        eng.dma_start(out=B["kdec"][:, :], in_=kdeccol_d).then_inc(sems.b_cst, 16)
        for kc in range(16):
            if kc >= 2:
                eng.wait_ge(sems.b_wstf, kc - 1)
            eng.dma_start(out=B["stg"][:, (kc % 2) * D:(kc % 2) * D + D],
                          in_=wo_d[kc * 128:(kc + 1) * 128, :]).then_inc(
                              [sems.b_wst0, sems.b_wst1][kc % 2], 16)
        for n in range(NCH):
            sl = n % 2
            if n >= 2:
                eng.wait_ge(sems.b_stt, 4 * (n - 1))
            eng.dma_start(
                out=B["q"][:, sl * 1024:(sl + 1) * 1024]
                    .rearrange("p (kc t) -> p kc t", kc=8),
                in_=qp_d[:, :, n * CB:(n + 1) * CB].rearrange("kc p t -> p kc t"),
            ).then_inc([sems.b_iq0, sems.b_iq1][sl], 16)
            eng.dma_start(
                out=B["k"][:, sl * 1024:(sl + 1) * 1024]
                    .rearrange("p (kc t) -> p kc t", kc=8),
                in_=kp_d[:, :, n * CB:(n + 1) * CB].rearrange("kc p t -> p kc t"),
            ).then_inc([sems.b_ik0, sems.b_ik1][sl], 16)
            eng.dma_start(
                out=B["v"][:, sl * 2048:(sl + 1) * 2048],
                in_=vtm_d[n * CB:(n + 1) * CB, :],
            ).then_inc([sems.b_iv0, sems.b_iv1][sl], 16)
            if n >= 2:
                eng.wait_ge(sems.b_og, 4 * (n - 1))
            eng.dma_start(
                out=B["sgs"][:, sl * 2048:(sl + 1) * 2048],
                in_=sgtm_d[n * CB:(n + 1) * CB, :],
            ).then_inc([sems.b_isg0, sems.b_isg1][sl], 16)
            if n >= 2:
                eng.wait_ge(sems.b_comb, n - 1)
            eng.dma_start(
                out=B["xs"][:, sl * D:(sl + 1) * D],
                in_=x_d[n * CB:(n + 1) * CB, :],
            ).then_inc([sems.b_ix0, sems.b_ix1][sl], 16)
            eng.dma_start(
                out=B["gfs"][:, sl * D:(sl + 1) * D],
                in_=gf_d[n * CB:(n + 1) * CB, :],
            ).then_inc([sems.b_igf0, sems.b_igf1][sl], 16)

    def b_pool(eng):
        eng.wait_ge(sems.b_cst, 64)
        for kc in range(16):
            eng.wait_ge([sems.b_wst0, sems.b_wst1][kc % 2], 16 * (kc // 2 + 1))
            eng.tensor_scalar(out=B["wob"][:, kc * D:(kc + 1) * D],
                              in0=B["stg"][:, (kc % 2) * D:(kc % 2) * D + D],
                              scalar1=B["gnw"][:, kc % 4:kc % 4 + 1], scalar2=None,
                              op0=ALU.mult).then_inc(sems.b_wstf, 1)
        eng.memset(B["Sf"][:, :], 0.0)
        eng.memset(B["Sb"][:, :], 0.0).then_inc(sems.b_sbf, 4)
        for n in range(NCH):
            sl = n % 2
            for h in range(H):
                gk = 4 * n + h
                eng.wait_ge(sems.b_on, gk + 1)
                eng.wait_ge([sems.b_isg0, sems.b_isg1][sl], 16 * (n // 2 + 1))
                eng.tensor_mul(out=B["og"][:, h * DV:(h + 1) * DV],
                               in0=B["onf"][:, :],
                               in1=B["sgs"][:, sl * 2048 + h * DV: sl * 2048 + (h + 1) * DV]
                               ).then_inc(sems.b_og, 1)
            for h in range(H):
                gk = 4 * n + h
                eng.wait_ge(sems.b_stt, gk + 1)
                eng.tensor_copy(out=B["Sb"][:, h * 1024:(h + 1) * 1024],
                                in_=B["Sf"][:, (n % 2) * 4096 + h * 1024:
                                            (n % 2) * 4096 + (h + 1) * 1024]
                                ).then_inc(sems.b_sbf, 1)
            eng.wait_ge(sems.b_woev, n + 1)
            eng.wait_ge([sems.b_igf0, sems.b_igf1][sl], 16 * (n // 2 + 1))
            eng.wait_ge([sems.b_ix0, sems.b_ix1][sl], 16 * (n // 2 + 1))
            if n >= 2:
                eng.wait_ge([sems.b_oo0, sems.b_oo1][sl], 16 * (n // 2))
            eng.tensor_mul(out=B["t1"][:, :], in0=B["so"][:, :],
                           in1=B["gfs"][:, sl * D:(sl + 1) * D])
            eng.tensor_add(out=B["outt"][:, sl * D:(sl + 1) * D],
                           in0=B["t1"][:, :],
                           in1=B["xs"][:, sl * D:(sl + 1) * D]).then_inc(sems.b_comb, 1)
            eng.wait_ge(sems.b_comb, n + 1)
            eng.dma_start(out=out_d[n * CB:(n + 1) * CB, :],
                          in_=B["outt"][:, sl * D:(sl + 1) * D]).then_inc(
                              [sems.b_oo0, sems.b_oo1][sl], 16)
        eng.wait_ge(sems.b_oo0, 16 * NCH // 2)
        eng.wait_ge(sems.b_oo1, 16 * NCH // 2)

    def b_tensor(eng):
        eng.wait_ge(sems.b_wstf, 16)
        for n in range(NCH):
            sl = n % 2
            for h in range(H):
                gk = 4 * n + h
                eng.wait_ge([sems.b_iq0, sems.b_iq1][sl], 16 * (n // 2 + 1))
                eng.wait_ge([sems.b_ik0, sems.b_ik1][sl], 16 * (n // 2 + 1))
                if h == 0:
                    if n >= 1:
                        eng.wait_ge(sems.b_woev, n)
                if gk >= 1:
                    eng.wait_ge(sems.b_mask, gk)
                for c in range(2):
                    a = mm(eng, B["P"][4][:, 0:CB],
                           B["k"][:, sl * 1024 + (2 * h + c) * CB: sl * 1024 + (2 * h + c + 1) * CB],
                           B["q"][:, sl * 1024 + (2 * h + c) * CB: sl * 1024 + (2 * h + c + 1) * CB],
                           start=(c == 0), stop=(c == 1))
                a.then_inc(sems.b_pe, 1)
                ktb = B["KTB"] if h % 2 == 0 else B["OGB"]
                if gk >= 2:
                    eng.wait_ge(sems.b_k2, gk - 1)
                if h == 1:
                    eng.wait_ge(sems.b_oge, 2 * n)
                for c in range(2):
                    a = tp(eng, ktb[:, c * 128:(c + 1) * 128],
                           B["k"][:, sl * 1024 + (2 * h + c) * CB: sl * 1024 + (2 * h + c + 1) * CB],
                           identb[:, :], start=(c == 0), stop=(c == 1))
                a.then_inc(sems.b_pe, 1)
                eng.wait_ge(sems.b_mask, gk + 1)
                eng.wait_ge(sems.b_q2, gk + 1)
                eng.wait_ge(sems.b_sbf, 4 * n + h + 1)
                eng.wait_ge([sems.b_iv0, sems.b_iv1][sl], 16 * (n // 2 + 1))
                if gk >= 2:
                    eng.wait_ge(sems.b_on, gk - 1)
                mm(eng, B["P"][h % 2][:, :], B["at"][:, :],
                   B["v"][:, sl * 2048 + h * DV: sl * 2048 + (h + 1) * DV],
                   start=True, stop=False)
                for c in range(2):
                    mm(eng, B["P"][h % 2][:, :],
                       B["q2"][:, c * CB:(c + 1) * CB],
                       B["Sb"][:, h * 1024 + c * 512: h * 1024 + (c + 1) * 512],
                       start=False, stop=(c == 1))
                if gk >= 1:
                    eng.wait_ge(sems.b_stt, gk)
                eng.wait_ge(sems.b_k2, gk + 1)
                for c in range(2):
                    a = mm(eng, B["P"][2 + c][:, :],
                           B["k2"][:, c * CB:(c + 1) * CB],
                           B["v"][:, sl * 2048 + h * DV: sl * 2048 + (h + 1) * DV],
                           start=True, stop=True)
                a.then_inc(sems.b_pe, 1)
            for w in range(2):
                eng.wait_ge(sems.b_og, 4 * n + 2 * (w + 1))
                if w == 0:
                    eng.wait_ge(sems.b_k2, 4 * n + 4)
                if 2 * n + w >= 1:
                    eng.wait_ge(sems.b_oge, 2 * n + w)
                for j in range(8):
                    fc = w * 8 + j
                    a = tp(eng, B["OGB"][:, j * 128:(j + 1) * 128],
                           B["og"][:, fc * 128:(fc + 1) * 128],
                           identb[:, :], start=(j == 0), stop=(j == 7))
                a.then_inc(sems.b_ogt, 1)
            eng.wait_ge(sems.b_oge, 2 * n + 2)
            eng.wait_ge(sems.b_mask, 4 * n + 4)
            for ng in range(2):
                for fc in range(16):
                    a = mm(eng, B["P"][4 + ng][:, :],
                           B["ogfm"][:, fc * 128:(fc + 1) * 128],
                           B["wob"][:, fc * D + ng * 512: fc * D + ng * 512 + 512],
                           start=(fc == 0), stop=(fc == 15))
            a.then_inc(sems.b_wo, 1)

    def b_act(eng):
        eng.wait_ge(sems.b_cst, 64)
        for n in range(NCH):
            for h in range(H):
                gk = 4 * n + h
                eng.wait_ge(sems.b_pe, 3 * gk + 2)
                if gk >= 1:
                    eng.wait_ge(sems.b_stt, gk)
                ktb = B["KTB"] if h % 2 == 0 else B["OGB"]
                eng.activation(out=B["k2"][:, :],
                               in_=ktb[:, 0:256],
                               func=AF.Identity,
                               scale=B["kdec"][:, h:h + 1]).then_inc(sems.b_k2, 1)
                eng.wait_ge(sems.b_pe, 3 * gk + 3)
                eng.activation(out=B["sq"][:, :], in_=B["P"][h % 2][:, :],
                               func=AF.Square,
                               accum_out=B["cols"][:, h:h + 1]).then_inc(sems.b_sq, 1)
                eng.wait_ge(sems.b_rc1, gk + 1)
                eng.activation(out=B["cols"][:, 8 + h:9 + h], in_=B["cols"][:, 4 + h:5 + h],
                               func=AF.Sqrt).then_inc(sems.b_rc2, 1)
                eng.wait_ge(sems.b_rc3, gk + 1)
                if gk >= 1:
                    eng.wait_ge(sems.b_og, gk)
                eng.activation(out=B["onf"][:, :], in_=B["P"][h % 2][:, :],
                               func=AF.Identity,
                               scale=B["cols"][:, 12 + h:13 + h]).then_inc(sems.b_on, 1)
            for w in range(2):
                eng.wait_ge(sems.b_ogt, 2 * n + w + 1)
                if n >= 1 and w == 0:
                    eng.wait_ge(sems.b_wo, n)
                eng.activation(out=B["ogfm"][:, w * 1024:(w + 1) * 1024],
                               in_=B["OGB"][:, :],
                               func=AF.Identity).then_inc(sems.b_oge, 1)

    def b_vector(eng):
        eng.wait_ge(sems.b_cst, 64)
        for n in range(NCH):
            sl = n % 2
            eng.wait_ge([sems.b_ix0, sems.b_ix1][sl], 16 * (n // 2 + 1))
            eng.tensor_scalar(out=B["xn"][:, :], in0=B["xs"][:, sl * D:(sl + 1) * D],
                              scalar1=negm1[:, n:n + 1], scalar2=rstd1[:, n:n + 1],
                              op0=ALU.add, op1=ALU.mult)
            for h in range(H):
                gk = 4 * n + h
                eng.wait_ge(sems.b_pe, 3 * gk + 1)
                eng.tensor_mul(out=B["at"][:, :], in0=B["P"][4][:, 0:CB],
                               in1=B["dmask"][:, h * CB:(h + 1) * CB]
                               ).then_inc(sems.b_mask, 1)
                eng.wait_ge([sems.b_iq0, sems.b_iq1][sl], 16 * (n // 2 + 1))
                for c in range(2):
                    a = eng.tensor_mul(
                        out=B["q2"][:, c * CB:(c + 1) * CB],
                        in0=B["q"][:, sl * 1024 + (2 * h + c) * CB: sl * 1024 + (2 * h + c + 1) * CB],
                        in1=B["cross"][:, h * CB:(h + 1) * CB])
                a.then_inc(sems.b_q2, 1)
                eng.wait_ge(sems.b_sq, gk + 1)
                eng.tensor_scalar(out=B["cols"][:, 4 + h:5 + h], in0=B["cols"][:, h:h + 1],
                                  scalar1=1.0 / DV, scalar2=EPS,
                                  op0=ALU.mult, op1=ALU.add).then_inc(sems.b_rc1, 1)
                eng.wait_ge(sems.b_rc2, gk + 1)
                eng.reciprocal(out=B["cols"][:, 12 + h:13 + h],
                               in_=B["cols"][:, 8 + h:9 + h]).then_inc(sems.b_rc3, 1)
                eng.wait_ge(sems.b_pe, 3 * gk + 3)
                if n == 0:
                    eng.wait_ge(sems.b_sbf, 4)
                for c in range(2):
                    a = eng.scalar_tensor_tensor(
                        out=B["Sf"][:, (n % 2) * 4096 + h * 1024 + c * 512:
                                    (n % 2) * 4096 + h * 1024 + (c + 1) * 512],
                        in0=B["Sf"][:, ((n + 1) % 2) * 4096 + h * 1024 + c * 512:
                                    ((n + 1) % 2) * 4096 + h * 1024 + (c + 1) * 512],
                        scalar=GC[h], in1=B["P"][2 + c][:, :],
                        op0=ALU.mult, op1=ALU.add)
                a.then_inc(sems.b_stt, 1)
            eng.wait_ge(sems.b_wo, n + 1)
            if n >= 1:
                eng.wait_ge(sems.b_comb, n)
            eng.tensor_add(out=B["so"][:, 0:512], in0=B["P"][4][:, :],
                           in1=B["xn"][:, 0:512])
            eng.tensor_add(out=B["so"][:, 512:D], in0=B["P"][5][:, :],
                           in1=B["xn"][:, 512:D]).then_inc(sems.b_woev, 1)

    def c_endstate():
        nxn = 16 * NTILES + 16
        if "dispdbg" in dbg:
            nxn += 16
        if "statsdbg" in dbg:
            nxn += 64
        return [(sems.c_dtpe, 4 * NTILES), (sems.c_lge, 8 * NTILES),
                (sems.c_dtp, 4 * NTILES), (sems.s_wstgf, C_WCHUNKS + 1),
                (sems.s_oxn, nxn),
                (sems.s_osh0, 16 * 2 * NTILES), (sems.s_osh1, 16 * 2 * NTILES),
                (sems.s_wstg0, 16 * 17), (sems.s_wstg1, 16 * 16),
                (sems.s_cst, 64), (sems.s_in, 16 * NTILES)]

    def de_endstate(pi):
        t = DE_TOT
        out = [(sems.d_acc, pi * t["acc"]), (sems.d_ev, pi * t["ev"]),
               (sems.d_silu, pi * t["silu"]), (sems.d_wstf, pi * t["wstf"]),
               (sems.d_out0, pi * t["out"]), (sems.d_out1, pi * t["out"]),
               (sems.d_wst0, pi * t["wst"]), (sems.d_wst1, pi * t["wst"]),
               (sems.d_fm, pi * t["fm"]),
               (sems.d_dp0, pi * t["dp"]), (sems.d_dp1, pi * t["dp"]),
               (sems.d_st0, pi * t["st"]), (sems.d_st1, pi * t["st"]),
               (sems.d_hsc, pi * t["hsc"]), (sems.d_bc, pi * t["bc"]),
               (sems.d_h, pi * t["h"])]
        if pi >= 2:
            out.append((sems.d_gf, t["gf"]))
        return out

    def wait_list(eng, lst):
        for s, v in lst:
            eng.wait_ge(s, v)

    with nc.Block() as block:
        @block.sync
        def _(eng):
            if "C" in phases:
                c_sync(eng)
            if "D" in phases:
                wait_list(eng, c_endstate())
                de_sync(eng, de_bases(0), 0, sh_d)
            if "Ep" in phases:
                wait_list(eng, de_endstate(1))
                de_sync(eng, de_bases(1), 8, moeps_d)
            if "A1" in phases:
                wait_list(eng, de_endstate(2))
                a1_sync(eng)
            if "B" in phases:
                wait_list(eng, a1_endstate())
                b_sync(eng)

        @block.tensor
        def _(eng):
            if "C" in phases:
                c_tensor(eng)
            if "D" in phases:
                wait_list(eng, c_endstate())
                de_tensor(eng, de_bases(0), False)
            if "Ep" in phases:
                wait_list(eng, de_endstate(1))
                de_tensor(eng, de_bases(1), True)
            if "A1" in phases:
                wait_list(eng, de_endstate(2))
                a1_tensor(eng)
            if "B" in phases:
                wait_list(eng, a1_endstate())
                b_tensor(eng)

        @block.vector
        def _(eng):
            if "C" in phases:
                c_vector(eng)
            if "D" in phases:
                wait_list(eng, c_endstate())
                de_vector(eng, de_bases(0), False)
            if "Ep" in phases:
                wait_list(eng, de_endstate(1))
                de_vector(eng, de_bases(1), True)
            if "A1" in phases:
                wait_list(eng, de_endstate(2))
                a1_vector(eng)
            if "B" in phases:
                wait_list(eng, a1_endstate())
                b_vector(eng)

        @block.scalar
        def _(eng):
            if "C" in phases:
                c_act(eng)
            if "D" in phases:
                wait_list(eng, c_endstate())
                de_act(eng, de_bases(0), False)
            if "Ep" in phases:
                wait_list(eng, de_endstate(1))
                de_act(eng, de_bases(1), True)
            if "A1" in phases:
                wait_list(eng, de_endstate(2))
                a1_act(eng)
            if "B" in phases:
                wait_list(eng, a1_endstate())
                b_act(eng)

        @block.gpsimd
        def _(eng):
            if "C" in phases:
                c_pool(eng)
                c_gpsimd_out(eng)
            if "D" in phases:
                wait_list(eng, c_endstate())
                de_pool(eng, de_bases(0), False)
            if "Ep" in phases:
                wait_list(eng, de_endstate(1))
                de_pool(eng, de_bases(1), True)
            if "A1" in phases:
                wait_list(eng, de_endstate(2))
                a1_pool(eng)
            if "B" in phases:
                wait_list(eng, a1_endstate())
                b_pool(eng)

    es.close()
    return nc


_consts = None


def _in_maps(inputs):
    global _consts
    if _consts is None:
        _consts = _host_constants()
    lay = _host_layouts(inputs)
    x = np.ascontiguousarray(np.asarray(inputs["L0_attn_x"], dtype=np.float32))
    base = dict(lay)
    base.update(_consts)
    maps = []
    for c in range(8):
        m = dict(base)
        m["x"] = np.ascontiguousarray(x[c])
        maps.append(m)
    return maps


def kernel(**inputs):
    nc = build_program()
    maps = _in_maps(inputs)
    res = run_bass_kernel_spmd(nc, maps, core_ids=list(range(8)))
    return np.stack([res.results[c]["out"] for c in range(8)], axis=0)


# version 02
# speedup vs baseline: 1.2285x; 1.2285x over previous
import numpy as np
import ml_dtypes
import sys

sys.path.insert(0, "/opt/trn_rl_repo")
import concourse.bass as bass
import concourse.mybir as mybir
from concourse.bass_utils import run_bass_kernel_spmd
from contextlib import ExitStack

F32 = mybir.dt.float32
BF16 = mybir.dt.bfloat16
AF = mybir.ActivationFunctionType
ALU = mybir.AluOpType
AX = mybir.AxisListType

T = 4096
D = 1024
H = 4
DK = 256
DV = 512
E = 16
HID = 512
S2 = 2048
CB = 128
NCH = T // CB
NT = 512
NTILES = T // NT
NTA = 256
NTILESA = T // NTA
EPS = 1e-5
GAMMA = [1.0 - 2.0 ** (-5.0 - h) for h in range(H)]
GC = [g ** CB for g in GAMMA]

PHASES = ["C", "D", "Ep", "A1", "B"]


def _host_constants():
    c = {}
    inv_freq = 1.0 / 10000.0 ** (np.arange(0, DK, 2, dtype=np.float64) / DK)
    ang = np.arange(T, dtype=np.float64)[None, :] * inv_freq[:, None]
    c["cos_t"] = np.cos(ang).astype(np.float32)
    c["sin_t"] = np.sin(ang).astype(np.float32)
    i = np.arange(CB, dtype=np.float64)
    dmt = np.zeros((H, CB, CB), np.float32)
    crs = np.zeros((H, CB, CB), np.float32)
    kdc = np.zeros((CB, H), np.float32)
    for h in range(H):
        g = GAMMA[h]
        diff = i[:, None] - i[None, :]
        m = np.where(diff >= 0, g ** np.maximum(diff, 0), 0.0)
        dmt[h] = m.T.astype(np.float32)
        crs[h] = np.tile((g ** (i + 1.0)).astype(np.float32)[None, :], (CB, 1))
        kdc[:, h] = (g ** (CB - 1.0 - i)).astype(np.float32)
    c["dmaskt"] = dmt
    c["crossb"] = crs
    c["kdec_col"] = kdc
    c["ident_f"] = np.eye(128, dtype=np.float32)
    c["ident_b"] = np.eye(128).astype(ml_dtypes.bfloat16)
    c["ones_b"] = np.ones((1, 128), ml_dtypes.bfloat16)
    return c


def _host_layouts(inputs):
    d = {}
    d["wq"] = np.ascontiguousarray(inputs["L0_attn_Wq"], np.float32)
    d["wk"] = np.ascontiguousarray(inputs["L0_attn_Wk"], np.float32)
    d["wv"] = np.ascontiguousarray(inputs["L0_attn_Wv"], np.float32)
    d["wg"] = np.ascontiguousarray(inputs["L0_attn_Wg"], np.float32)
    d["wo"] = np.ascontiguousarray(inputs["L0_attn_Wo"], np.float32)
    d["sfc1t"] = np.ascontiguousarray(np.asarray(inputs["L0_ffn_sfc1"]).T, np.float32)
    d["sfc2t"] = np.ascontiguousarray(np.asarray(inputs["L0_ffn_sfc2"]).T, np.float32)
    d["sfc3t"] = np.ascontiguousarray(np.asarray(inputs["L0_ffn_sfc3"]).T, np.float32)
    d["w1t"] = np.ascontiguousarray(np.asarray(inputs["L0_ffn_w1"]).transpose(0, 2, 1), np.float32)
    d["w2t"] = np.ascontiguousarray(np.asarray(inputs["L0_ffn_w2"]).transpose(0, 2, 1), np.float32)
    d["gatet"] = np.ascontiguousarray(np.asarray(inputs["L0_ffn_gate_w"]).T, np.float32)
    d["lnw_bc"] = np.ascontiguousarray(
        np.broadcast_to(np.asarray(inputs["L0_attn_ln_w"])[None, :], (128, D)), np.float32)
    d["lnb_col"] = np.ascontiguousarray(
        np.asarray(inputs["L0_attn_ln_b"]).reshape(D // 128, 128).T, np.float32)
    d["gnw_col"] = np.ascontiguousarray(
        np.asarray(inputs["L0_attn_g_norm_w"]).reshape(DV // 128, 128).T, np.float32)
    return d


def mm(eng, out, lhsT, rhs, start, stop):
    return eng.matmul(out=out, lhsT=lhsT, rhs=rhs, start=start, stop=stop)


def tp(eng, out, in_, ident, start, stop):
    return eng.matmul(out=out, lhsT=in_, rhs=ident, is_transpose=True,
                      start=start, stop=stop)


def build_program(debug_outputs=(), phases=PHASES):
    nc = bass.Bass("TRN2", target_bir_lowering=False, debug=False,
                   detect_race_conditions=False)

    def dram_in(name, shape, dt=F32):
        return nc.dram_tensor(name, list(shape), dt, kind="ExternalInput").ap()

    def scratch(name, shape, dt):
        kind = "ExternalOutput" if name in debug_outputs else "Internal"
        return nc.dram_tensor(name, list(shape), dt, kind=kind).ap()

    x_d = dram_in("x", (T, D))
    wq_d = dram_in("wq", (D, D))
    wk_d = dram_in("wk", (D, D))
    wv_d = dram_in("wv", (D, 2 * D))
    wg_d = dram_in("wg", (D, 2 * D))
    wo_d = dram_in("wo", (2 * D, D))
    sfc1t_d = dram_in("sfc1t", (D, S2))
    sfc2t_d = dram_in("sfc2t", (D, S2))
    sfc3t_d = dram_in("sfc3t", (S2, D))
    w1t_d = dram_in("w1t", (E, D, HID))
    w2t_d = dram_in("w2t", (E, HID, D))
    gatet_d = dram_in("gatet", (D, E))
    lnwbc_d = dram_in("lnw_bc", (128, D))
    lnbcol_d = dram_in("lnb_col", (128, 8))
    gnwcol_d = dram_in("gnw_col", (128, 4))
    cos_d = dram_in("cos_t", (128, T))
    sin_d = dram_in("sin_t", (128, T))
    dmaskt_d = dram_in("dmaskt", (H, CB, CB))
    crossb_d = dram_in("crossb", (H, CB, CB))
    kdeccol_d = dram_in("kdec_col", (CB, H))
    identf_d = dram_in("ident_f", (128, 128))
    identb_d = dram_in("ident_b", (128, 128), BF16)
    onesb_d = dram_in("ones_b", (1, 128), BF16)

    out_d = nc.dram_tensor("out", [T, D], F32, kind="ExternalOutput").ap()

    xnfm_d = scratch("xnfm", (8, 128, T), BF16)
    sh_d = scratch("sh", (T, D), F32)
    moeps_d = scratch("moeps", (T, D), F32)
    gf_d = scratch("gf", (T, D), BF16)
    qp_d = scratch("qp", (8, 128, T), BF16)
    kp_d = scratch("kp", (8, 128, T), BF16)
    vtm_d = scratch("vtm", (T, 2 * D), BF16)
    dispt_d = scratch("dispt", (16, T), BF16)
    sgtm_d = scratch("sgtm", (T, 2 * D), BF16)
    dbg = {}
    if "dispdbg" in debug_outputs:
        dbg["dispdbg"] = nc.dram_tensor("dispdbg", [16, T], BF16,
                                        kind="ExternalOutput").ap()
    if "statsdbg" in debug_outputs:
        dbg["statsdbg"] = nc.dram_tensor("statsdbg", [128, 4 * NCH], F32,
                                         kind="ExternalOutput").ap()

    es = ExitStack()

    identf = nc.alloc_sbuf_tensor("identf", [128, 128], F32)
    identb = nc.alloc_sbuf_tensor("identb", [128, 128], BF16)
    onesb = nc.alloc_sbuf_tensor("onesb", [1, 128], BF16)
    negm1 = nc.alloc_sbuf_tensor("negm1", [128, NCH], F32)
    rstd1 = nc.alloc_sbuf_tensor("rstd1", [128, NCH], F32)
    negm2 = nc.alloc_sbuf_tensor("negm2", [128, NCH], F32)
    rstd2 = nc.alloc_sbuf_tensor("rstd2", [128, NCH], F32)
    lnbcol = nc.alloc_sbuf_tensor("lnbcol", [128, 8], F32)
    lnbcol_b = nc.alloc_sbuf_tensor("lnbcol_b", [128, 8], BF16)

    def mkphase(fn):
        with ExitStack() as ph:
            aa = lambda n, s, dt: ph.enter_context(nc.sbuf_tensor(n, s, dt))
            pp = lambda n, s, dt: ph.enter_context(nc.psum_tensor(n, s, dt))
            return fn(aa, pp)

    C = mkphase(lambda a, p: dict(
        sfc1b=a("C_sfc1b", [128, 8 * S2], BF16),
        sfc2b=a("C_sfc2b", [128, 8 * S2], BF16),
        sfc3b=a("C_sfc3b", [128, 16 * D], BF16),
        gateb=a("C_gateb", [128, 8 * E], BF16),
        stg=a("C_stg", [128, 2 * 2048], F32),
        x=a("C_x", [128, 4 * D], F32),
        xn=a("C_xn", [128, 4 * D], F32),
        fm=a("C_fm", [128, 8 * NT], BF16),
        hs=a("C_hs", [128, 16 * NT], BF16),
        scr=a("C_scr", [128, 3 * D], F32),
        silu=a("C_silu", [128, 2 * NT], F32),
        cols=a("C_cols", [128, 16], F32),
        lg=a("C_lg", [128, 16], F32),
        lg2=a("C_lg2", [128, 16], F32),
        eq1=a("C_eq1", [128, 16], F32),
        eq2=a("C_eq2", [128, 16], F32),
        disp=a("C_disp", [128, 16], F32),
        dispT=a("C_dispT", [16, T], BF16),
        mcol=a("C_mcol", [128, 8], F32),
        P=[p(f"C_P{i}", [128, 512], F32) for i in range(6)],
        TPB=[p(f"C_TPB{i}", [128, 512], F32) for i in range(2)],
    ))

    DE = mkphase(lambda a, p: dict(
        w1b=a("D_w1b", [128, 8 * 8 * 512], BF16),
        w2b=a("D_w2b", [128, 8 * 4 * 1024], BF16),
        stg=a("D_stg", [128, 2 * 1024], F32),
        fm=a("D_fm", [128, 8 * NT], BF16),
        hsc=a("D_hsc", [128, 8 * 4 * 512], BF16),
        stream=a("D_stream", [128, 2 * D], F32),
        outh=a("D_outh", [128, 2 * D], F32),
        gfh=a("D_gfh", [128, 2 * D], BF16),
        silu=a("D_silu", [128, 2 * 512], F32),
        dprow=a("D_dprow", [1, 2 * 512], BF16),
        P=[p(f"D_P{i}", [128, 512], F32) for i in range(8)],
    ))

    B = mkphase(lambda a, p: dict(
        wob=a("B_wob", [128, 16 * D], BF16),
        stg=a("B_stg", [128, 2 * D], F32),
        gnw=a("B_gnw", [128, 4], F32),
        dmask=a("B_dmask", [128, 4 * CB], F32),
        cross=a("B_cross", [128, 4 * CB], F32),
        kdec=a("B_kdec", [128, 4], F32),
        Sf=a("B_Sf", [128, 2 * H * 2 * DV], F32),
        Sb=a("B_Sb", [128, H * 2 * DV], BF16),
        q=a("B_q", [128, 2 * 8 * CB], BF16),
        k=a("B_k", [128, 2 * 8 * CB], BF16),
        v=a("B_v", [128, 2 * 2048], BF16),
        sgs=a("B_sgs", [128, 2 * 2048], BF16),
        xs=a("B_xs", [128, 2 * D], F32),
        gfs=a("B_gfs", [128, 2 * D], BF16),
        xn=a("B_xn", [128, D], F32),
        q2=a("B_q2", [128, 4 * 2 * CB], BF16),
        k2=a("B_k2", [128, 4 * DK], BF16),
        at=a("B_at", [128, 4 * CB], BF16),
        onf=a("B_onf", [128, DV], F32),
        og=a("B_og", [128, H * DV], BF16),
        ogfm=a("B_ogfm", [128, 16 * 128], BF16),
        sq=a("B_sq", [128, DV], F32),
        cols=a("B_cols", [128, 16], F32),
        so=a("B_so", [128, D], F32),
        t1=a("B_t1", [128, D], F32),
        outt=a("B_outt", [128, 2 * D], F32),
        P=[p(f"B_P{i}", [128, 512], F32) for i in range(6)],
        KTB=p("B_KTB", [128, 1024], BF16),
        OGB=p("B_OGB", [128, 1024], BF16),
    ))

    sem_names = [
        "s_in", "s_cst", "s_oxn", "s_osh0", "s_osh1",
        "s_wstg0", "s_wstg1", "s_wstgf",
        "c_st1", "c_c1", "c_sq1", "c_xn", "c_st2", "c_c2", "c_sq2",
        "c_tp", "c_fme", "c_ab", "c_silu", "c_hs", "c_s3", "c_s3e",
        "c_lg", "c_lge", "c_disp", "c_dtp", "c_dtpe",
        "d_wst0", "d_wst1", "d_wstf", "d_fm", "d_st0", "d_st1",
        "d_dp0", "d_dp1", "d_h", "d_bc", "d_silu", "d_hsc", "d_acc", "d_ev",
        "d_gf", "d_out0", "d_out1",
        "a_wst0", "a_wst1", "a_wstf", "a_cst", "a_x0", "a_x1", "a_cs0", "a_cs1", "a_xn",
        "a_nrm", "a_tp", "a_nfm", "a_qk", "a_qke", "a_rope", "a_vg", "a_vge",
        "a_oq", "a_ov", "a_pre", "a_pree",
        "b_cst", "b_wst0", "b_wst1", "b_wstf",
        "b_iq0", "b_iq1", "b_ik0", "b_ik1", "b_iv0", "b_iv1",
        "b_isg0", "b_isg1", "b_ix0", "b_ix1", "b_igf0", "b_igf1",
        "b_pe", "b_mask", "b_q2", "b_k2", "b_sq", "b_rc1", "b_rc2", "b_rc3",
        "b_on", "b_og", "b_stt", "b_sbf", "b_ogt", "b_oge", "b_wo", "b_woev",
        "b_comb", "b_oo0", "b_oo1",
    ]

    class _S:
        pass
    sems = _S()
    for n in sem_names:
        setattr(sems, n, es.enter_context(nc.semaphore(n)))

    C_WCHUNKS = 8 + 8 + 16 + 1

    def c_sync(eng):
        srcs = []
        for kc in range(8):
            srcs.append((sfc1t_d[kc * 128:(kc + 1) * 128, :], 2048))
        for kc in range(8):
            srcs.append((sfc2t_d[kc * 128:(kc + 1) * 128, :], 2048))
        for sc in range(16):
            srcs.append((sfc3t_d[sc * 128:(sc + 1) * 128, :], 1024))
        srcs.append((gatet_d.rearrange("(kc p) e -> p kc e", p=128), 128))
        for k, (src, w) in enumerate(srcs):
            if k >= 2:
                eng.wait_ge(sems.s_wstgf, k - 1)
            dst = C["stg"][:, (k % 2) * 2048:(k % 2) * 2048 + w]
            if len(src.shape) == 3:
                dst = dst.rearrange("p (kc e) -> p kc e", kc=src.shape[1])
            eng.dma_start(out=dst, in_=src).then_inc(
                [sems.s_wstg0, sems.s_wstg1][k % 2], 16)
        eng.dma_start(out=identf[:, :], in_=identf_d).then_inc(sems.s_cst, 16)
        eng.dma_start(out=identb[:, :], in_=identb_d).then_inc(sems.s_cst, 16)
        eng.dma_start(out=onesb[:, :], in_=onesb_d).then_inc(sems.s_cst, 16)
        eng.dma_start(out=lnbcol[:, :], in_=lnbcol_d).then_inc(sems.s_cst, 16)
        for i in range(NTILES):
            if i >= 1:
                eng.wait_ge(sems.c_xn, i)
            eng.dma_start(
                out=C["x"][:, :].rearrange("p (s d) -> p s d", s=4),
                in_=x_d[i * NT:(i + 1) * NT, :].rearrange("(s p) d -> p s d", p=128),
            ).then_inc(sems.s_in, 16)

    def c_pool(eng):
        for k in range(C_WCHUNKS):
            eng.wait_ge([sems.s_wstg0, sems.s_wstg1][k % 2], 16 * (k // 2 + 1))
            slot = C["stg"][:, (k % 2) * 2048:(k % 2) * 2048 + 2048]
            if k < 8:
                dst, src = C["sfc1b"][:, k * 2048:(k + 1) * 2048], slot[:, 0:2048]
            elif k < 16:
                dst, src = C["sfc2b"][:, (k - 8) * 2048:(k - 7) * 2048], slot[:, 0:2048]
            elif k < 32:
                dst, src = C["sfc3b"][:, (k - 16) * 1024:(k - 15) * 1024], slot[:, 0:1024]
            else:
                dst, src = C["gateb"][:, :], slot[:, 0:128]
            eng.tensor_copy(out=dst, in_=src).then_inc(sems.s_wstgf, 1)
        eng.wait_ge(sems.s_cst, 4 * 16)
        eng.tensor_copy(out=lnbcol_b[:, :], in_=lnbcol[:, :]).then_inc(sems.s_wstgf, 1)

    def c_act(eng):
        for i in range(NTILES):
            eng.wait_ge(sems.s_in, 16 * (i + 1))
            for st in range(4):
                eng.activation(out=C["scr"][:, 0:D], in_=C["x"][:, st * D:(st + 1) * D],
                               func=AF.Square, accum_out=C["cols"][:, st:st + 1])
                a = eng.activation(out=C["scr"][:, 0:D], in_=C["x"][:, st * D:(st + 1) * D],
                                   func=AF.Identity, accum_out=C["cols"][:, 4 + st:4 + st + 1])
            a.then_inc(sems.c_st1, 1)
            eng.wait_ge(sems.c_c1, i + 1)
            eng.activation(out=C["mcol"][:, 0:4], in_=C["cols"][:, 8:12],
                           func=AF.Sqrt).then_inc(sems.c_sq1, 1)
            eng.wait_ge(sems.c_xn, i + 1)
            for st in range(4):
                eng.activation(out=C["scr"][:, 0:D], in_=C["xn"][:, st * D:(st + 1) * D],
                               func=AF.Square, accum_out=C["cols"][:, st:st + 1])
                a = eng.activation(out=C["scr"][:, 0:D], in_=C["xn"][:, st * D:(st + 1) * D],
                                   func=AF.Identity, accum_out=C["cols"][:, 4 + st:4 + st + 1])
            a.then_inc(sems.c_st2, 1)
            eng.wait_ge(sems.c_c2, i + 1)
            eng.activation(out=C["mcol"][:, 4:8], in_=C["cols"][:, 8:12],
                           func=AF.Sqrt).then_inc(sems.c_sq2, 1)
            for sc in range(16):
                Ksc = i * 16 + sc
                eng.wait_ge(sems.c_ab, Ksc + 1)
                if Ksc >= 2:
                    eng.wait_ge(sems.c_hs, Ksc - 1)
                eng.activation(out=C["silu"][:, (sc % 2) * NT:(sc % 2 + 1) * NT],
                               in_=C["P"][sc % 2][:, :],
                               func=AF.Silu).then_inc(sems.c_silu, 1)
            for mc in range(4):
                Km = i * 4 + mc
                eng.wait_ge(sems.c_disp, 2 * Km + 1)
                eng.activation(out=C["mcol"][:, 3:4], in_=C["mcol"][:, 2:3],
                               func=AF.Tanh, scale=0.5).then_inc(sems.c_dtp, 1)

    def c_vector(eng):
        for i in range(NTILES):
            eng.wait_ge(sems.c_st1, i + 1)
            nm = negm1[:, i * 4:(i + 1) * 4]
            eng.tensor_scalar(out=nm, in0=C["cols"][:, 4:8], scalar1=-1.0 / D,
                              scalar2=None, op0=ALU.mult)
            eng.drain()
            eng.tensor_mul(out=C["cols"][:, 12:16], in0=nm, in1=nm)
            eng.drain()
            eng.scalar_tensor_tensor(out=C["cols"][:, 8:12], in0=C["cols"][:, 0:4],
                                     scalar=1.0 / D, in1=C["cols"][:, 12:16],
                                     op0=ALU.mult, op1=ALU.subtract)
            eng.drain()
            eng.tensor_scalar(out=C["cols"][:, 8:12], in0=C["cols"][:, 8:12],
                              scalar1=EPS, scalar2=None,
                              op0=ALU.add).then_inc(sems.c_c1, 1)
            eng.wait_ge(sems.c_sq1, i + 1)
            eng.reciprocal(out=rstd1[:, i * 4:(i + 1) * 4], in_=C["mcol"][:, 0:4])
            eng.drain()
            for st in range(4):
                c = i * 4 + st
                a = eng.tensor_scalar(out=C["xn"][:, st * D:(st + 1) * D],
                                      in0=C["x"][:, st * D:(st + 1) * D],
                                      scalar1=negm1[:, c:c + 1], scalar2=rstd1[:, c:c + 1],
                                      op0=ALU.add, op1=ALU.mult)
            a.then_inc(sems.c_xn, 1)
            eng.wait_ge(sems.c_st2, i + 1)
            nm2 = negm2[:, i * 4:(i + 1) * 4]
            eng.tensor_scalar(out=nm2, in0=C["cols"][:, 4:8], scalar1=-1.0 / D,
                              scalar2=None, op0=ALU.mult)
            eng.drain()
            eng.tensor_mul(out=C["cols"][:, 12:16], in0=nm2, in1=nm2)
            eng.drain()
            eng.scalar_tensor_tensor(out=C["cols"][:, 8:12], in0=C["cols"][:, 0:4],
                                     scalar=1.0 / D, in1=C["cols"][:, 12:16],
                                     op0=ALU.mult, op1=ALU.subtract)
            eng.drain()
            eng.tensor_scalar(out=C["cols"][:, 8:12], in0=C["cols"][:, 8:12],
                              scalar1=EPS, scalar2=None,
                              op0=ALU.add).then_inc(sems.c_c2, 1)
            eng.wait_ge(sems.c_sq2, i + 1)
            eng.reciprocal(out=rstd2[:, i * 4:(i + 1) * 4], in_=C["mcol"][:, 4:8])
            for g in range(8):
                K = i * 8 + g
                st, gg = g // 2, g % 2
                eng.wait_ge(sems.c_tp, K + 1)
                if i > 0 and g == 0:
                    eng.wait_ge(sems.s_oxn, 16 * i)
                eng.tensor_copy(
                    out=C["fm"][:, :].rearrange("p (kc t) -> p kc t", kc=8)
                        [:, gg * 4:(gg + 1) * 4, st * 128:(st + 1) * 128],
                    in_=C["TPB"][K % 2][:, :].rearrange("p (s f) -> p s f", s=4)
                ).then_inc(sems.c_fme, 1)
            for sc in range(16):
                Ksc = i * 16 + sc
                eng.wait_ge(sems.c_silu, Ksc + 1)
                eng.tensor_mul(out=C["hs"][:, sc * NT:(sc + 1) * NT],
                               in0=C["silu"][:, (sc % 2) * NT:(sc % 2 + 1) * NT],
                               in1=C["P"][2 + sc % 2][:, :]).then_inc(sems.c_hs, 1)
            for g in range(8):
                K = i * 8 + g
                mc, ng = g // 2, g % 2
                eng.wait_ge(sems.c_s3, K + 1)
                if ng == 0:
                    occ = 2 * i + mc // 2
                    if occ > 0:
                        eng.wait_ge([sems.s_osh0, sems.s_osh1][mc % 2], 16 * occ)
                off = (1 + mc % 2) * D + ng * 512
                eng.tensor_copy(out=C["scr"][:, off:off + 512],
                                in_=C["P"][4 + ng][:, :]).then_inc(sems.c_s3e, 1)
            for mc in range(4):
                Km = i * 4 + mc
                eng.wait_ge(sems.c_lg, Km + 1)
                eng.tensor_copy(out=C["lg"][:, :],
                                in_=C["P"][4][:, 0:16]).then_inc(sems.c_lge, 1)
                eng.drain()
                eng.reduce_max(out=C["mcol"][:, 0:1], in_=C["lg"][:, :], axis=AX.X)
                eng.drain()
                eng.tensor_scalar(out=C["eq1"][:, :], in0=C["lg"][:, :],
                                  scalar1=C["mcol"][:, 0:1], scalar2=None, op0=ALU.is_ge)
                eng.drain()
                eng.scalar_tensor_tensor(out=C["lg2"][:, :], in0=C["eq1"][:, :],
                                         scalar=-1e30, in1=C["lg"][:, :],
                                         op0=ALU.mult, op1=ALU.add)
                eng.drain()
                eng.reduce_max(out=C["mcol"][:, 1:2], in_=C["lg2"][:, :], axis=AX.X)
                eng.drain()
                eng.tensor_scalar(out=C["eq2"][:, :], in0=C["lg2"][:, :],
                                  scalar1=C["mcol"][:, 1:2], scalar2=None, op0=ALU.is_ge)
                eng.tensor_sub(out=C["mcol"][:, 2:3], in0=C["mcol"][:, 0:1],
                               in1=C["mcol"][:, 1:2]).then_inc(sems.c_disp, 1)
                eng.wait_ge(sems.c_dtp, Km + 1)
                eng.tensor_scalar(out=C["mcol"][:, 4:5], in0=C["mcol"][:, 3:4],
                                  scalar1=0.5, scalar2=0.5, op0=ALU.mult, op1=ALU.add)
                eng.drain()
                eng.tensor_scalar(out=C["mcol"][:, 5:6], in0=C["mcol"][:, 4:5],
                                  scalar1=-1.0, scalar2=1.0, op0=ALU.mult, op1=ALU.add)
                eng.tensor_scalar(out=C["disp"][:, :], in0=C["eq1"][:, :],
                                  scalar1=C["mcol"][:, 4:5], scalar2=None, op0=ALU.mult)
                eng.drain()
                eng.scalar_tensor_tensor(out=C["disp"][:, :], in0=C["eq2"][:, :],
                                         scalar=C["mcol"][:, 5:6], in1=C["disp"][:, :],
                                         op0=ALU.mult,
                                         op1=ALU.add).then_inc(sems.c_disp, 1)
                eng.wait_ge(sems.c_dtpe, Km + 1)
                eng.tensor_copy(out=C["dispT"][0:16, Km * 128:(Km + 1) * 128],
                                in_=C["P"][5][0:16, 0:128]).then_inc(sems.c_lge, 1)

    def c_tensor(eng):
        eng.wait_ge(sems.s_wstgf, C_WCHUNKS + 1)
        for i in range(NTILES):
            eng.wait_ge(sems.c_xn, i + 1)
            for g in range(8):
                K = i * 8 + g
                st, gg = g // 2, g % 2
                if K >= 2:
                    eng.wait_ge(sems.c_fme, K - 1)
                for s in range(4):
                    kc = gg * 4 + s
                    a = tp(eng, C["TPB"][K % 2][:, s * 128:(s + 1) * 128],
                           C["xn"][:, st * D + kc * 128: st * D + (kc + 1) * 128],
                           identf[:, :], start=(s == 0), stop=(s == 3))
                a.then_inc(sems.c_tp, 1)
            eng.wait_ge(sems.c_fme, (i + 1) * 8)
            fmv = C["fm"][:, :].rearrange("p (kc t) -> p kc t", kc=8)
            for sc in range(16):
                Ksc = i * 16 + sc
                if Ksc >= 2:
                    eng.wait_ge(sems.c_silu, Ksc - 1)
                    eng.wait_ge(sems.c_hs, Ksc - 1)
                for kc in range(8):
                    mm(eng, C["P"][sc % 2][:, :],
                       C["sfc1b"][:, kc * 2048 + sc * 128: kc * 2048 + (sc + 1) * 128],
                       fmv[:, kc, :], start=(kc == 0), stop=(kc == 7))
                for kc in range(8):
                    a = mm(eng, C["P"][2 + sc % 2][:, :],
                           C["sfc2b"][:, kc * 2048 + sc * 128: kc * 2048 + (sc + 1) * 128],
                           fmv[:, kc, :], start=(kc == 0), stop=(kc == 7))
                a.then_inc(sems.c_ab, 1)
            eng.wait_ge(sems.c_hs, (i + 1) * 16)
            hsv = C["hs"][:, :].rearrange("p (sc t) -> p sc t", sc=16)
            for g in range(8):
                K = i * 8 + g
                mc, ng = g // 2, g % 2
                if K >= 2:
                    eng.wait_ge(sems.c_s3e, K - 1)
                for sc in range(16):
                    a = mm(eng, C["P"][4 + ng][:, :],
                           hsv[:, sc, mc * 128:(mc + 1) * 128],
                           C["sfc3b"][:, sc * D + ng * 512: sc * D + ng * 512 + 512],
                           start=(sc == 0), stop=(sc == 15))
                a.then_inc(sems.c_s3, 1)
            for mc in range(4):
                Km = i * 4 + mc
                if mc == 0:
                    eng.wait_ge(sems.c_s3e, i * 8 + 8)
                if Km >= 1:
                    eng.wait_ge(sems.c_lge, 2 * Km - 1)
                for kc in range(8):
                    a = mm(eng, C["P"][4][:, 0:16],
                           fmv[:, kc, mc * 128:(mc + 1) * 128],
                           C["gateb"][:, kc * 16:(kc + 1) * 16],
                           start=(kc == 0), stop=(kc == 7))
                a.then_inc(sems.c_lg, 1)
                eng.wait_ge(sems.c_disp, 2 * Km + 2)
                if Km >= 1:
                    eng.wait_ge(sems.c_lge, 2 * Km)
                tp(eng, C["P"][5][0:16, 0:128], C["disp"][:, :], identf[:, :],
                   start=True, stop=True).then_inc(sems.c_dtpe, 1)

    def c_gpsimd_out(eng):
        for i in range(NTILES):
            eng.wait_ge(sems.c_fme, (i + 1) * 8)
            eng.dma_start(
                out=xnfm_d[:, :, i * NT:(i + 1) * NT].rearrange("kc p t -> p kc t"),
                in_=C["fm"][:, :].rearrange("p (kc t) -> p kc t", kc=8),
            ).then_inc(sems.s_oxn, 16)
            for mc in range(4):
                eng.wait_ge(sems.c_s3e, i * 8 + 2 * (mc + 1))
                off = (1 + mc % 2) * D
                eng.dma_start(
                    out=sh_d[i * NT + mc * 128: i * NT + (mc + 1) * 128, :],
                    in_=C["scr"][:, off:off + D],
                ).then_inc([sems.s_osh0, sems.s_osh1][mc % 2], 16)
        nxn = 16 * NTILES
        eng.wait_ge(sems.c_lge, 2 * 4 * NTILES)
        eng.dma_start(out=dispt_d, in_=C["dispT"][:, :]).then_inc(sems.s_oxn, 16)
        nxn += 16
        if dbg:
            eng.wait_ge(sems.c_lge, 2 * 4 * NTILES)
        if "dispdbg" in dbg:
            eng.dma_start(out=dbg["dispdbg"], in_=C["dispT"][:, :]).then_inc(sems.s_oxn, 16)
            nxn += 16
        if "statsdbg" in dbg:
            eng.dma_start(out=dbg["statsdbg"][:, 0:NCH], in_=negm1[:, :]).then_inc(sems.s_oxn, 16)
            eng.dma_start(out=dbg["statsdbg"][:, NCH:2 * NCH], in_=rstd1[:, :]).then_inc(sems.s_oxn, 16)
            eng.dma_start(out=dbg["statsdbg"][:, 2 * NCH:3 * NCH], in_=negm2[:, :]).then_inc(sems.s_oxn, 16)
            eng.dma_start(out=dbg["statsdbg"][:, 3 * NCH:4 * NCH], in_=rstd2[:, :]).then_inc(sems.s_oxn, 16)
            nxn += 64
        eng.wait_ge(sems.s_oxn, nxn)
        eng.wait_ge(sems.s_osh0, 16 * 2 * NTILES)
        eng.wait_ge(sems.s_osh1, 16 * 2 * NTILES)

    DE_TOT = dict(wst=512, wstf=64, fm=16 * NTILES, st=16 * 2 * NTILES,
                  dp=16 * 4 * NTILES, h=32 * NTILES, bc=8 * NTILES,
                  silu=32 * NTILES, hsc=32 * NTILES, acc=8 * NTILES,
                  ev=4 * NTILES, gf=4 * NTILES, out=16 * 2 * NTILES)

    def de_bases(pi):
        b = {k: pi * v for k, v in DE_TOT.items()}
        b["gf"] = 0
        return b

    def de_sync(eng, bs, ge0, stream_d):
        chunks = []
        for e in range(8):
            ge = ge0 + e
            for q4 in range(4):
                chunks.append((w1t_d[ge, q4 * 256:(q4 + 1) * 256, :]
                               .rearrange("(kc p) h -> p kc h", p=128), 2))
            for q4 in range(4):
                chunks.append((w2t_d[ge, q4 * 128:(q4 + 1) * 128, :], 1))
        for k, (sap, nkc) in enumerate(chunks):
            if k >= 2:
                eng.wait_ge(sems.d_wstf, bs["wstf"] + k - 1)
            dst = DE["stg"][:, (k % 2) * 1024:(k % 2) * 1024 + 1024]
            if nkc > 1:
                dst = dst.rearrange("p (kc w) -> p kc w", kc=nkc)
            eng.dma_start(out=dst, in_=sap).then_inc(
                [sems.d_wst0, sems.d_wst1][k % 2], 16)
        for i in range(NTILES):
            w = bs["h"] + 32 * i
            if w > 0:
                eng.wait_ge(sems.d_h, w)
            eng.dma_start(
                out=DE["fm"][:, :].rearrange("p (kc t) -> p kc t", kc=8),
                in_=xnfm_d[:, :, i * NT:(i + 1) * NT].rearrange("kc p t -> p kc t"),
            ).then_inc(sems.d_fm, 16)
            for e in range(8):
                w = bs["bc"] + 8 * i + e - 1
                if w > 0:
                    eng.wait_ge(sems.d_bc, w)
                eng.dma_start(
                    out=DE["dprow"][0:1, (e % 2) * 512:(e % 2) * 512 + 512],
                    in_=dispt_d[ge0 + e:ge0 + e + 1, i * NT:(i + 1) * NT],
                ).then_inc([sems.d_dp0, sems.d_dp1][e % 2], 16)
            for mc in range(4):
                w = bs["ev"] + 4 * i + mc - 1
                if w > 0:
                    eng.wait_ge(sems.d_ev, w)
                eng.dma_start(
                    out=DE["stream"][:, (mc % 2) * D:(mc % 2) * D + D],
                    in_=stream_d[i * NT + mc * 128: i * NT + (mc + 1) * 128, :],
                ).then_inc([sems.d_st0, sems.d_st1][mc % 2], 16)
        eng.wait_ge(sems.d_wst0, bs["wst"] + 16 * 32)
        eng.wait_ge(sems.d_wst1, bs["wst"] + 16 * 32)
        eng.wait_ge(sems.d_fm, bs["fm"] + 16 * NTILES)
        eng.wait_ge(sems.d_dp0, bs["dp"] + 16 * 4 * NTILES)
        eng.wait_ge(sems.d_dp1, bs["dp"] + 16 * 4 * NTILES)
        eng.wait_ge(sems.d_st0, bs["st"] + 16 * 2 * NTILES)
        eng.wait_ge(sems.d_st1, bs["st"] + 16 * 2 * NTILES)

    def de_pool(eng, bs, is_e):
        for k in range(64):
            eng.wait_ge([sems.d_wst0, sems.d_wst1][k % 2],
                        bs["wst"] + 16 * (k // 2 + 1))
            e8 = k // 8
            kind = k % 8
            if kind < 4:
                dst = DE["w1b"][:, e8 * 4096 + kind * 1024: e8 * 4096 + kind * 1024 + 1024]
            else:
                dst = DE["w2b"][:, e8 * 4096 + (kind - 4) * 1024: e8 * 4096 + (kind - 4) * 1024 + 1024]
            eng.tensor_copy(out=dst,
                            in_=DE["stg"][:, (k % 2) * 1024:(k % 2) * 1024 + 1024]
                            ).then_inc(sems.d_wstf, 1)
        for i in range(NTILES):
            for mc in range(4):
                if is_e:
                    eng.wait_ge(sems.d_gf, bs["gf"] + 4 * i + mc + 1)
                    srcb = DE["gfh"][:, (mc % 2) * D:(mc % 2) * D + D]
                    dstd = gf_d[i * NT + mc * 128: i * NT + (mc + 1) * 128, :]
                else:
                    eng.wait_ge(sems.d_ev, bs["ev"] + 4 * i + mc + 1)
                    srcb = DE["outh"][:, (mc % 2) * D:(mc % 2) * D + D]
                    dstd = moeps_d[i * NT + mc * 128: i * NT + (mc + 1) * 128, :]
                eng.dma_start(out=dstd, in_=srcb).then_inc(
                    [sems.d_out0, sems.d_out1][mc % 2], 16)
        eng.wait_ge(sems.d_out0, bs["out"] + 16 * 2 * NTILES)
        eng.wait_ge(sems.d_out1, bs["out"] + 16 * 2 * NTILES)

    def de_tensor(eng, bs, is_e):
        fmv = DE["fm"][:, :].rearrange("p (kc t) -> p kc t", kc=8)
        hscv = DE["hsc"][:, :].rearrange("p (e hc t) -> p e hc t", e=8, hc=4)
        for i in range(NTILES):
            eng.wait_ge(sems.d_fm, bs["fm"] + 16 * (i + 1))
            for e in range(8):
                Ke = 8 * i + e
                if i == 0:
                    eng.wait_ge(sems.d_wstf, bs["wstf"] + 8 * e + 4)
                w = bs["silu"] + 4 * Ke
                if w > 0:
                    eng.wait_ge(sems.d_silu, w)
                for hc in range(4):
                    for kc in range(8):
                        a = mm(eng, DE["P"][hc][:, :],
                               DE["w1b"][:, e * 4096 + kc * 512 + hc * 128:
                                         e * 4096 + kc * 512 + (hc + 1) * 128],
                               fmv[:, kc, :], start=(kc == 0), stop=(kc == 7))
                    a.then_inc(sems.d_h, 1)
                eng.wait_ge([sems.d_dp0, sems.d_dp1][e % 2],
                            bs["dp"] + 16 * (4 * i + e // 2 + 1))
                w = bs["hsc"] + 4 * (Ke - 1)
                if w > 0:
                    eng.wait_ge(sems.d_hsc, w)
                if e < 2 and (bs["ev"] + 4 * i) > 0:
                    eng.wait_ge(sems.d_ev, bs["ev"] + 4 * i)
                mm(eng, DE["P"][4 + e % 2][:, :], onesb[0:1, :],
                   DE["dprow"][0:1, (e % 2) * 512:(e % 2) * 512 + 512],
                   start=True, stop=True).then_inc(sems.d_bc, 1)
            eng.wait_ge(sems.d_hsc, bs["hsc"] + 32 * (i + 1))
            if i == 0:
                eng.wait_ge(sems.d_wstf, bs["wstf"] + 64)
            for mc in range(4):
                w = bs["ev"] + 4 * i + mc - 1
                if w > 0:
                    eng.wait_ge(sems.d_ev, w)
                ob = 6 if mc % 2 == 0 else 4
                for ng in range(2):
                    for e in range(8):
                        for kc in range(4):
                            a = mm(eng, DE["P"][ob + ng][:, :],
                                   hscv[:, e, kc, mc * 128:(mc + 1) * 128],
                                   DE["w2b"][:, e * 4096 + kc * 1024 + ng * 512:
                                             e * 4096 + kc * 1024 + ng * 512 + 512],
                                   start=(e == 0 and kc == 0),
                                   stop=(e == 7 and kc == 3))
                    a.then_inc(sems.d_acc, 1)

    def de_act(eng, bs, is_e):
        for i in range(NTILES):
            for e in range(8):
                for hc in range(4):
                    ghc = 32 * i + 4 * e + hc
                    eng.wait_ge(sems.d_h, bs["h"] + ghc + 1)
                    w = bs["hsc"] + ghc - 1
                    if w > 0:
                        eng.wait_ge(sems.d_hsc, w)
                    eng.activation(out=DE["silu"][:, (ghc % 2) * 512:(ghc % 2) * 512 + 512],
                                   in_=DE["P"][hc][:, :],
                                   func=AF.Silu).then_inc(sems.d_silu, 1)
            if is_e:
                for mc in range(4):
                    eng.wait_ge(sems.d_ev, bs["ev"] + 4 * i + mc + 1)
                    w = bs["out"] + 16 * (2 * i + mc // 2)
                    if w > 0:
                        eng.wait_ge([sems.d_out0, sems.d_out1][mc % 2], w)
                    eng.activation(out=DE["gfh"][:, (mc % 2) * D:(mc % 2) * D + D],
                                   in_=DE["outh"][:, (mc % 2) * D:(mc % 2) * D + D],
                                   func=AF.Silu).then_inc(sems.d_gf, 1)

    def de_vector(eng, bs, is_e):
        hscv = DE["hsc"][:, :].rearrange("p (e hc t) -> p e hc t", e=8, hc=4)
        for i in range(NTILES):
            for e in range(8):
                for hc in range(4):
                    ghc = 32 * i + 4 * e + hc
                    eng.wait_ge(sems.d_silu, bs["silu"] + ghc + 1)
                    eng.wait_ge(sems.d_bc, bs["bc"] + 8 * i + e + 1)
                    eng.tensor_mul(out=hscv[:, e, hc, :],
                                   in0=DE["silu"][:, (ghc % 2) * 512:(ghc % 2) * 512 + 512],
                                   in1=DE["P"][4 + e % 2][:, :]).then_inc(sems.d_hsc, 1)
            for mc in range(4):
                eng.wait_ge(sems.d_acc, bs["acc"] + 8 * i + 2 * mc + 2)
                eng.wait_ge([sems.d_st0, sems.d_st1][mc % 2],
                            bs["st"] + 16 * (2 * i + mc // 2 + 1))
                if is_e:
                    w = bs["gf"] + 4 * i + mc - 1
                    if w > 0:
                        eng.wait_ge(sems.d_gf, w)
                else:
                    w = bs["out"] + 16 * (2 * i + mc // 2)
                    if w > 0:
                        eng.wait_ge([sems.d_out0, sems.d_out1][mc % 2], w)
                off = (mc % 2) * D
                ob = 6 if mc % 2 == 0 else 4
                eng.tensor_add(out=DE["outh"][:, off:off + 512],
                               in0=DE["P"][ob][:, :],
                               in1=DE["stream"][:, off:off + 512])
                eng.tensor_add(out=DE["outh"][:, off + 512:off + D],
                               in0=DE["P"][ob + 1][:, :],
                               in1=DE["stream"][:, off + 512:off + D]
                               ).then_inc(sems.d_ev, 1)

    A1 = mkphase(lambda a, p: dict(
        wqb=a("A_wqb", [128, 8 * D], BF16),
        wkb=a("A_wkb", [128, 8 * D], BF16),
        wvb=a("A_wvb", [128, 8 * 2048], BF16),
        wgb=a("A_wgb", [128, 8 * 2048], BF16),
        stg=a("A_stg", [128, 2 * 2048], F32),
        lnwbc=a("A_lnwbc", [128, D], F32),
        cs=a("A_cs", [128, 2 * 2 * NTA], F32),
        brow=a("A_brow", [1, 2048], F32),
        bqcol=a("A_bqcol", [128, 8], F32),
        bkcol=a("A_bkcol", [128, 8], F32),
        bvrow=a("A_bvrow", [1, 2048], BF16),
        bgrow=a("A_bgrow", [1, 2048], BF16),
        x=a("A_x", [128, 2 * 2 * D], F32),
        xn=a("A_xn", [128, 2 * D], F32),
        ntmp=a("A_ntmp", [128, D], F32),
        nrm=a("A_nrm", [128, 2 * D], BF16),
        nfm=a("A_nfm", [128, 8 * NTA], BF16),
        qf=a("A_qf", [128, 2 * NTA], F32),
        kf=a("A_kf", [128, 2 * NTA], F32),
        rt=a("A_rt", [128, 4 * NTA], F32),
        qp=a("A_qp", [128, 8 * NTA], BF16),
        kp=a("A_kp", [128, 8 * NTA], BF16),
        v=a("A_v", [128, 2 * 2048], BF16),
        sg=a("A_sg", [128, 2 * 2048], BF16),
        P=[p(f"A_P{i}", [128, 512], F32) for i in range(6)],
        TPB=[p(f"A_TPB{i}", [128, 1024], BF16) for i in range(2)],
    ))

    def a1_sync(eng):
        chunks = []
        for kc in range(8):
            chunks.append((wq_d[kc * 128:(kc + 1) * 128, :], 1024))
        for kc in range(8):
            chunks.append((wk_d[kc * 128:(kc + 1) * 128, :], 1024))
        for kc in range(8):
            chunks.append((wv_d[kc * 128:(kc + 1) * 128, :], 2048))
        for kc in range(8):
            chunks.append((wg_d[kc * 128:(kc + 1) * 128, :], 2048))
        for k, (sap, w) in enumerate(chunks):
            if k >= 2:
                eng.wait_ge(sems.a_wstf, k - 1)
            eng.dma_start(out=A1["stg"][:, (k % 2) * 2048:(k % 2) * 2048 + w],
                          in_=sap).then_inc([sems.a_wst0, sems.a_wst1][k % 2], 16)
        eng.dma_start(out=A1["lnwbc"][:, :], in_=lnwbc_d).then_inc(sems.a_cst, 16)
        for i in range(NTILESA):
            if i >= 2:
                eng.wait_ge(sems.a_xn, i - 1)
            eng.dma_start(
                out=A1["x"][:, (i % 2) * 2 * D:(i % 2) * 2 * D + 2 * D]
                    .rearrange("p (s d) -> p s d", s=2),
                in_=x_d[i * NTA:(i + 1) * NTA, :].rearrange("(s p) d -> p s d", p=128),
            ).then_inc([sems.a_x0, sems.a_x1][i % 2], 16)
            if i >= 2:
                eng.wait_ge(sems.a_rope, 8 * (i - 1))
            co = (i % 2) * 2 * NTA
            eng.dma_start(out=A1["cs"][:, co:co + NTA],
                          in_=cos_d[:, i * NTA:(i + 1) * NTA]).then_inc(
                              [sems.a_cs0, sems.a_cs1][i % 2], 16)
            eng.dma_start(out=A1["cs"][:, co + NTA:co + 2 * NTA],
                          in_=sin_d[:, i * NTA:(i + 1) * NTA]).then_inc(
                              [sems.a_cs0, sems.a_cs1][i % 2], 16)

    def a1_pool(eng):
        for k in range(32):
            eng.wait_ge([sems.a_wst0, sems.a_wst1][k % 2], 16 * (k // 2 + 1))
            s = A1["stg"][:, (k % 2) * 2048:]
            if k < 8:
                eng.tensor_scalar(out=A1["wqb"][:, k * D:(k + 1) * D],
                                  in0=s[:, 0:1024], scalar1=DK ** -0.5, scalar2=None,
                                  op0=ALU.mult).then_inc(sems.a_wstf, 1)
            elif k < 16:
                eng.tensor_copy(out=A1["wkb"][:, (k - 8) * D:(k - 7) * D],
                                in_=s[:, 0:1024]).then_inc(sems.a_wstf, 1)
            elif k < 24:
                eng.tensor_copy(out=A1["wvb"][:, (k - 16) * 2048:(k - 15) * 2048],
                                in_=s[:, 0:2048]).then_inc(sems.a_wstf, 1)
            else:
                eng.tensor_copy(out=A1["wgb"][:, (k - 24) * 2048:(k - 23) * 2048],
                                in_=s[:, 0:2048]).then_inc(sems.a_wstf, 1)
        for i in range(NTILESA):
            eng.wait_ge(sems.a_rope, 8 * (i + 1))
            eng.dma_start(
                out=qp_d[:, :, i * NTA:(i + 1) * NTA].rearrange("kc p t -> p kc t"),
                in_=A1["qp"][:, :].rearrange("p (kc t) -> p kc t", kc=8),
            ).then_inc(sems.a_oq, 16)
            eng.dma_start(
                out=kp_d[:, :, i * NTA:(i + 1) * NTA].rearrange("kc p t -> p kc t"),
                in_=A1["kp"][:, :].rearrange("p (kc t) -> p kc t", kc=8),
            ).then_inc(sems.a_oq, 16)
            eng.wait_ge(sems.a_vge, 16 * (i + 1))
            eng.dma_start(
                out=vtm_d[i * NTA:(i + 1) * NTA, :].rearrange("(s p) n -> p s n", p=128),
                in_=A1["v"][:, :].rearrange("p (s n) -> p s n", s=2),
            ).then_inc(sems.a_ov, 16)
            eng.dma_start(
                out=sgtm_d[i * NTA:(i + 1) * NTA, :].rearrange("(s p) n -> p s n", p=128),
                in_=A1["sg"][:, :].rearrange("p (s n) -> p s n", s=2),
            ).then_inc(sems.a_ov, 16)
        eng.wait_ge(sems.a_oq, 16 * 2 * NTILESA)
        eng.wait_ge(sems.a_ov, 16 * 2 * NTILESA)

    def a1_tensor(eng):
        rows = ([("q", A1["wqb"], 1024, hx) for hx in range(2)]
                + [("k", A1["wkb"], 1024, hx) for hx in range(2)]
                + [("v", A1["wvb"], 2048, hx) for hx in range(4)]
                + [("g", A1["wgb"], 2048, hx) for hx in range(4)])
        for r, (kind, wb, wid, hx) in enumerate(rows):
            need = {"q": 8, "k": 16, "v": 24, "g": 32}[kind]
            eng.wait_ge(sems.a_wstf, need)
            if r >= 2:
                eng.wait_ge(sems.a_pree, r - 1)
            bank = A1["P"][5 if r % 2 == 0 else 4]
            for kc in range(8):
                a = mm(eng, bank[0:1, 0:512],
                       lnbcol_b[:, kc:kc + 1],
                       wb[:, kc * wid + hx * 512: kc * wid + (hx + 1) * 512],
                       start=(kc == 0), stop=(kc == 7))
            a.then_inc(sems.a_pre, 1)
        for j in range(16):
            eng.wait_ge(sems.a_pree, 12 + j)
            tp(eng, A1["P"][3][:, 0:1], A1["brow"][0:1, j * 128:(j + 1) * 128],
               identf[0:1, 0:1], start=True, stop=True).then_inc(sems.a_pre, 1)
        eng.wait_ge(sems.a_pree, 28)
        nfmv = A1["nfm"][:, :].rearrange("p (kc t) -> p kc t", kc=8)
        for i in range(NTILESA):
            for st in range(2):
                gt = 2 * i + st
                eng.wait_ge(sems.a_nrm, gt + 1)
                if gt >= 2:
                    eng.wait_ge(sems.a_nfm, gt - 1)
                for kc in range(8):
                    a = tp(eng, A1["TPB"][gt % 2][:, kc * 128:(kc + 1) * 128],
                           A1["nrm"][:, st * D + kc * 128: st * D + (kc + 1) * 128],
                           identb[:, :], start=(kc == 0), stop=(kc == 7))
                a.then_inc(sems.a_tp, 1)
            eng.wait_ge(sems.a_nfm, 2 * (i + 1))
            for hp in range(4):
                for qk in range(2):
                    wb = A1["wqb"] if qk == 0 else A1["wkb"]
                    w = 8 * i + (hp - 1) * 2 + qk + 1
                    if w > 0:
                        eng.wait_ge(sems.a_qke, w)
                    for half in range(2):
                        dq = 2 * hp + half
                        bank = A1["P"][qk * 2 + half]
                        for kc in range(8):
                            a = mm(eng, bank[:, 0:NTA],
                                   wb[:, kc * D + dq * 128: kc * D + (dq + 1) * 128],
                                   nfmv[:, kc, :], start=(kc == 0), stop=(kc == 7))
                    a.then_inc(sems.a_qk, 1)
            for st in range(2):
                for vg in range(2):
                    wb = A1["wvb"] if vg == 0 else A1["wgb"]
                    br = A1["bvrow"] if vg == 0 else A1["bgrow"]
                    for ng in range(4):
                        gv = 16 * i + st * 8 + vg * 4 + ng
                        if gv >= 1:
                            eng.wait_ge(sems.a_vge, gv)
                        bank = A1["P"][4 + vg]
                        for kc in range(8):
                            mm(eng, bank[:, :],
                               nfmv[:, kc, st * 128:(st + 1) * 128],
                               wb[:, kc * 2048 + ng * 512: kc * 2048 + (ng + 1) * 512],
                               start=(kc == 0), stop=False)
                        mm(eng, bank[:, :], onesb[0:1, 0:128],
                           br[0:1, ng * 512:(ng + 1) * 512],
                           start=False, stop=True).then_inc(sems.a_vg, 1)

    def a1_act(eng):
        rows = (["q0", "q1", "k0", "k1"] + [f"v{j}" for j in range(4)]
                + [f"g{j}" for j in range(4)])
        for r, tag in enumerate(rows):
            eng.wait_ge(sems.a_pre, r + 1)
            bank = A1["P"][5 if r % 2 == 0 else 4]
            kind, hx = tag[0], int(tag[1])
            if kind == "q":
                dst = A1["brow"][0:1, hx * 512:(hx + 1) * 512]
            elif kind == "k":
                dst = A1["brow"][0:1, 1024 + hx * 512: 1024 + (hx + 1) * 512]
            elif kind == "v":
                dst = A1["bvrow"][0:1, hx * 512:(hx + 1) * 512]
            else:
                dst = A1["bgrow"][0:1, hx * 512:(hx + 1) * 512]
            eng.activation(out=dst, in_=bank[0:1, 0:512],
                           func=AF.Identity).then_inc(sems.a_pree, 1)
        for j in range(16):
            eng.wait_ge(sems.a_pre, 12 + j + 1)
            dst = A1["bqcol"][:, j:j + 1] if j < 8 else A1["bkcol"][:, j - 8:j - 7]
            eng.activation(out=dst, in_=A1["P"][3][:, 0:1],
                           func=AF.Identity).then_inc(sems.a_pree, 1)
        for i in range(NTILESA):
            for st in range(2):
                gt = 2 * i + st
                eng.wait_ge(sems.a_tp, gt + 1)
                eng.activation(out=A1["nfm"][:, :]
                               .rearrange("p (kc t) -> p kc t", kc=8)
                               [:, :, st * 128:(st + 1) * 128],
                               in_=A1["TPB"][gt % 2][:, :]
                               .rearrange("p (kc t) -> p kc t", kc=8),
                               func=AF.Identity).then_inc(sems.a_nfm, 1)
            for st in range(2):
                for vg in range(2):
                    for ng in range(4):
                        gv = 16 * i + st * 8 + vg * 4 + ng
                        eng.wait_ge(sems.a_vg, gv + 1)
                        if i > 0 and st == 0 and vg == 0 and ng == 0:
                            eng.wait_ge(sems.a_ov, 16 * 2 * i)
                        dstbuf = A1["v"] if vg == 0 else A1["sg"]
                        dst = dstbuf[:, st * 2048 + ng * 512: st * 2048 + (ng + 1) * 512]
                        eng.activation(out=dst, in_=A1["P"][4 + vg][:, :],
                                       func=AF.Identity if vg == 0 else AF.Silu
                                       ).then_inc(sems.a_vge, 1)

    def a1_vector(eng):
        for i in range(NTILESA):
            eng.wait_ge([sems.a_x0, sems.a_x1][i % 2], 16 * (i // 2 + 1))
            eng.wait_ge(sems.a_cst, 16)
            xoff = (i % 2) * 2 * D
            for st in range(2):
                c = 2 * i + st
                a = eng.tensor_scalar(out=A1["xn"][:, st * D:(st + 1) * D],
                                      in0=A1["x"][:, xoff + st * D: xoff + (st + 1) * D],
                                      scalar1=negm1[:, c:c + 1], scalar2=rstd1[:, c:c + 1],
                                      op0=ALU.add, op1=ALU.mult)
            a.then_inc(sems.a_xn, 1)
            for st in range(2):
                c = 2 * i + st
                eng.tensor_scalar(out=A1["ntmp"][:, :],
                                  in0=A1["xn"][:, st * D:(st + 1) * D],
                                  scalar1=negm2[:, c:c + 1], scalar2=rstd2[:, c:c + 1],
                                  op0=ALU.add, op1=ALU.mult)
                eng.tensor_mul(out=A1["nrm"][:, st * D:(st + 1) * D],
                               in0=A1["ntmp"][:, :],
                               in1=A1["lnwbc"][:, :]).then_inc(sems.a_nrm, 1)
            eng.wait_ge([sems.a_cs0, sems.a_cs1][i % 2], 32 * (i // 2 + 1))
            co = (i % 2) * 2 * NTA
            coss = A1["cs"][:, co:co + NTA]
            sins = A1["cs"][:, co + NTA:co + 2 * NTA]
            for hp in range(4):
                for qk in range(2):
                    gq = 8 * i + hp * 2 + qk
                    eng.wait_ge(sems.a_qk, gq + 1)
                    if i > 0 and hp == 0 and qk == 0:
                        eng.wait_ge(sems.a_oq, 16 * 2 * i)
                    bcol = A1["bqcol"] if qk == 0 else A1["bkcol"]
                    fbuf = A1["qf"] if qk == 0 else A1["kf"]
                    obuf = A1["qp"] if qk == 0 else A1["kp"]
                    pA = A1["P"][qk * 2][:, 0:NTA]
                    pB = A1["P"][qk * 2 + 1][:, 0:NTA]
                    eng.tensor_scalar(out=fbuf[:, 0:NTA], in0=pA,
                                      scalar1=bcol[:, 2 * hp:2 * hp + 1], scalar2=None,
                                      op0=ALU.add)
                    a = eng.tensor_scalar(out=fbuf[:, NTA:2 * NTA], in0=pB,
                                          scalar1=bcol[:, 2 * hp + 1:2 * hp + 2],
                                          scalar2=None, op0=ALU.add)
                    a.then_inc(sems.a_qke, 1)
                    fA = fbuf[:, 0:NTA]
                    fB = fbuf[:, NTA:2 * NTA]
                    rt = A1["rt"]
                    eng.tensor_mul(out=rt[:, 0:NTA], in0=fA, in1=coss)
                    eng.tensor_mul(out=rt[:, NTA:2 * NTA], in0=fB, in1=sins)
                    eng.tensor_mul(out=rt[:, 2 * NTA:3 * NTA], in0=fB, in1=coss)
                    eng.tensor_mul(out=rt[:, 3 * NTA:4 * NTA], in0=fA, in1=sins)
                    eng.tensor_sub(out=obuf[:, (2 * hp) * NTA:(2 * hp + 1) * NTA],
                                   in0=rt[:, 0:NTA], in1=rt[:, NTA:2 * NTA])
                    eng.tensor_add(out=obuf[:, (2 * hp + 1) * NTA:(2 * hp + 2) * NTA],
                                   in0=rt[:, 2 * NTA:3 * NTA],
                                   in1=rt[:, 3 * NTA:4 * NTA]).then_inc(sems.a_rope, 1)

    def a1_endstate():
        return [(sems.a_vg, 16 * NTILESA), (sems.a_rope, 8 * NTILESA),
                (sems.a_vge, 16 * NTILESA), (sems.a_wstf, 32),
                (sems.a_oq, 16 * 2 * NTILESA), (sems.a_ov, 16 * 2 * NTILESA),
                (sems.a_x0, 16 * NTILESA // 2), (sems.a_x1, 16 * NTILESA // 2),
                (sems.a_cs0, 16 * NTILESA), (sems.a_cs1, 16 * NTILESA),
                (sems.a_wst0, 16 * 16), (sems.a_wst1, 16 * 16),
                (sems.a_cst, 16), (sems.a_nfm, 2 * NTILESA),
                (sems.a_pree, 28), (sems.a_qke, 8 * NTILESA),
                (sems.a_xn, NTILESA), (sems.a_tp, 2 * NTILESA),
                (sems.a_qk, 8 * NTILESA), (sems.a_nrm, 2 * NTILESA)]

    def b_sync(eng):
        eng.dma_start(out=B["gnw"][:, :], in_=gnwcol_d).then_inc(sems.b_cst, 16)
        eng.dma_start(out=B["dmask"][:, :].rearrange("p (h i) -> p h i", h=H),
                      in_=dmaskt_d.rearrange("h p i -> p h i")).then_inc(sems.b_cst, 16)
        eng.dma_start(out=B["cross"][:, :].rearrange("p (h i) -> p h i", h=H),
                      in_=crossb_d.rearrange("h p i -> p h i")).then_inc(sems.b_cst, 16)
        eng.dma_start(out=B["kdec"][:, :], in_=kdeccol_d).then_inc(sems.b_cst, 16)
        for kc in range(16):
            if kc >= 2:
                eng.wait_ge(sems.b_wstf, kc - 1)
            eng.dma_start(out=B["stg"][:, (kc % 2) * D:(kc % 2) * D + D],
                          in_=wo_d[kc * 128:(kc + 1) * 128, :]).then_inc(
                              [sems.b_wst0, sems.b_wst1][kc % 2], 16)
        for n in range(NCH):
            sl = n % 2
            if n >= 2:
                eng.wait_ge(sems.b_stt, 4 * (n - 1))
            eng.dma_start(
                out=B["q"][:, sl * 1024:(sl + 1) * 1024]
                    .rearrange("p (kc t) -> p kc t", kc=8),
                in_=qp_d[:, :, n * CB:(n + 1) * CB].rearrange("kc p t -> p kc t"),
            ).then_inc([sems.b_iq0, sems.b_iq1][sl], 16)
            eng.dma_start(
                out=B["k"][:, sl * 1024:(sl + 1) * 1024]
                    .rearrange("p (kc t) -> p kc t", kc=8),
                in_=kp_d[:, :, n * CB:(n + 1) * CB].rearrange("kc p t -> p kc t"),
            ).then_inc([sems.b_ik0, sems.b_ik1][sl], 16)
            eng.dma_start(
                out=B["v"][:, sl * 2048:(sl + 1) * 2048],
                in_=vtm_d[n * CB:(n + 1) * CB, :],
            ).then_inc([sems.b_iv0, sems.b_iv1][sl], 16)
            if n >= 2:
                eng.wait_ge(sems.b_og, 4 * (n - 1))
            eng.dma_start(
                out=B["sgs"][:, sl * 2048:(sl + 1) * 2048],
                in_=sgtm_d[n * CB:(n + 1) * CB, :],
            ).then_inc([sems.b_isg0, sems.b_isg1][sl], 16)
            if n >= 2:
                eng.wait_ge(sems.b_comb, n - 1)
            eng.dma_start(
                out=B["xs"][:, sl * D:(sl + 1) * D],
                in_=x_d[n * CB:(n + 1) * CB, :],
            ).then_inc([sems.b_ix0, sems.b_ix1][sl], 16)
            eng.dma_start(
                out=B["gfs"][:, sl * D:(sl + 1) * D],
                in_=gf_d[n * CB:(n + 1) * CB, :],
            ).then_inc([sems.b_igf0, sems.b_igf1][sl], 16)

    def b_pool(eng):
        eng.wait_ge(sems.b_cst, 64)
        for kc in range(16):
            eng.wait_ge([sems.b_wst0, sems.b_wst1][kc % 2], 16 * (kc // 2 + 1))
            eng.tensor_scalar(out=B["wob"][:, kc * D:(kc + 1) * D],
                              in0=B["stg"][:, (kc % 2) * D:(kc % 2) * D + D],
                              scalar1=B["gnw"][:, kc % 4:kc % 4 + 1], scalar2=None,
                              op0=ALU.mult).then_inc(sems.b_wstf, 1)
        eng.memset(B["Sf"][:, :], 0.0)
        eng.memset(B["Sb"][:, :], 0.0).then_inc(sems.b_sbf, 4)
        for n in range(NCH):
            sl = n % 2
            for h in range(H):
                gk = 4 * n + h
                eng.wait_ge(sems.b_on, gk + 1)
                eng.wait_ge([sems.b_isg0, sems.b_isg1][sl], 16 * (n // 2 + 1))
                if n >= 1:
                    eng.wait_ge(sems.b_ogt, 2 * n - 1 if h < 2 else 2 * n)
                eng.tensor_mul(out=B["og"][:, h * DV:(h + 1) * DV],
                               in0=B["onf"][:, :],
                               in1=B["sgs"][:, sl * 2048 + h * DV: sl * 2048 + (h + 1) * DV]
                               ).then_inc(sems.b_og, 1)
            for h in range(H):
                gk = 4 * n + h
                eng.wait_ge(sems.b_stt, gk + 1)
                eng.tensor_copy(out=B["Sb"][:, h * 1024:(h + 1) * 1024],
                                in_=B["Sf"][:, (n % 2) * 4096 + h * 1024:
                                            (n % 2) * 4096 + (h + 1) * 1024]
                                ).then_inc(sems.b_sbf, 1)
            eng.wait_ge(sems.b_woev, n + 1)
            eng.wait_ge([sems.b_igf0, sems.b_igf1][sl], 16 * (n // 2 + 1))
            eng.wait_ge([sems.b_ix0, sems.b_ix1][sl], 16 * (n // 2 + 1))
            if n >= 2:
                eng.wait_ge([sems.b_oo0, sems.b_oo1][sl], 16 * (n // 2))
            eng.tensor_mul(out=B["t1"][:, :], in0=B["so"][:, :],
                           in1=B["gfs"][:, sl * D:(sl + 1) * D])
            eng.tensor_add(out=B["outt"][:, sl * D:(sl + 1) * D],
                           in0=B["t1"][:, :],
                           in1=B["xs"][:, sl * D:(sl + 1) * D]).then_inc(sems.b_comb, 1)
            eng.wait_ge(sems.b_comb, n + 1)
            eng.dma_start(out=out_d[n * CB:(n + 1) * CB, :],
                          in_=B["outt"][:, sl * D:(sl + 1) * D]).then_inc(
                              [sems.b_oo0, sems.b_oo1][sl], 16)
        eng.wait_ge(sems.b_oo0, 16 * NCH // 2)
        eng.wait_ge(sems.b_oo1, 16 * NCH // 2)

    def b_tensor(eng):
        for n in range(NCH):
            sl = n % 2
            eng.wait_ge([sems.b_iq0, sems.b_iq1][sl], 16 * (n // 2 + 1))
            eng.wait_ge([sems.b_ik0, sems.b_ik1][sl], 16 * (n // 2 + 1))
            for h in range(H):
                gk = 4 * n + h
                if h == 0 and n >= 1:
                    eng.wait_ge(sems.b_stt, 4 * n)
                if h >= 2:
                    eng.wait_ge(sems.b_mask, 4 * n + h - 1)
                for c in range(2):
                    a = mm(eng, B["P"][2 + h % 2][:, 0:CB],
                           B["k"][:, sl * 1024 + (2 * h + c) * CB: sl * 1024 + (2 * h + c + 1) * CB],
                           B["q"][:, sl * 1024 + (2 * h + c) * CB: sl * 1024 + (2 * h + c + 1) * CB],
                           start=(c == 0), stop=(c == 1))
                a.then_inc(sems.b_pe, 1)
            for h in range(H):
                gk = 4 * n + h
                ktb = B["KTB"] if h % 2 == 0 else B["OGB"]
                if gk >= 2:
                    eng.wait_ge(sems.b_k2, gk - 1)
                if h == 1:
                    eng.wait_ge(sems.b_oge, 2 * n)
                for c in range(2):
                    a = tp(eng, ktb[:, c * 128:(c + 1) * 128],
                           B["k"][:, sl * 1024 + (2 * h + c) * CB: sl * 1024 + (2 * h + c + 1) * CB],
                           identb[:, :], start=(c == 0), stop=(c == 1))
                a.then_inc(sems.b_pe, 1)
            eng.wait_ge([sems.b_iv0, sems.b_iv1][sl], 16 * (n // 2 + 1))
            for h in range(H):
                gk = 4 * n + h
                eng.wait_ge(sems.b_mask, gk + 1)
                eng.wait_ge(sems.b_q2, gk + 1)
                eng.wait_ge(sems.b_sbf, 4 * n + h + 1)
                if gk >= 2:
                    eng.wait_ge(sems.b_on, gk - 1)
                mm(eng, B["P"][h % 2][:, :], B["at"][:, h * CB:(h + 1) * CB],
                   B["v"][:, sl * 2048 + h * DV: sl * 2048 + (h + 1) * DV],
                   start=True, stop=False)
                for c in range(2):
                    mm(eng, B["P"][h % 2][:, :],
                       B["q2"][:, (2 * h + c) * CB:(2 * h + c + 1) * CB],
                       B["Sb"][:, h * 1024 + c * 512: h * 1024 + (c + 1) * 512],
                       start=False, stop=(c == 1))
                if h == 0:
                    eng.wait_ge(sems.b_mask, 4 * n + 4)
                if gk >= 1:
                    eng.wait_ge(sems.b_stt, gk)
                eng.wait_ge(sems.b_k2, gk + 1)
                for c in range(2):
                    a = mm(eng, B["P"][2 + c][:, :],
                           B["k2"][:, h * DK + c * CB: h * DK + (c + 1) * CB],
                           B["v"][:, sl * 2048 + h * DV: sl * 2048 + (h + 1) * DV],
                           start=True, stop=True)
                a.then_inc(sems.b_pe, 1)
            for w in range(2):
                eng.wait_ge(sems.b_og, 4 * n + 2 * (w + 1))
                if w == 0:
                    eng.wait_ge(sems.b_k2, 4 * n + 4)
                if 2 * n + w >= 1:
                    eng.wait_ge(sems.b_oge, 2 * n + w)
                for j in range(8):
                    fc = w * 8 + j
                    a = tp(eng, B["OGB"][:, j * 128:(j + 1) * 128],
                           B["og"][:, fc * 128:(fc + 1) * 128],
                           identb[:, :], start=(j == 0), stop=(j == 7))
                a.then_inc(sems.b_ogt, 1)
            eng.wait_ge(sems.b_oge, 2 * n + 2)
            if n == 0:
                eng.wait_ge(sems.b_wstf, 16)
            else:
                eng.wait_ge(sems.b_woev, n)
            for ng in range(2):
                for fc in range(16):
                    a = mm(eng, B["P"][4 + ng][:, :],
                           B["ogfm"][:, fc * 128:(fc + 1) * 128],
                           B["wob"][:, fc * D + ng * 512: fc * D + ng * 512 + 512],
                           start=(fc == 0), stop=(fc == 15))
            a.then_inc(sems.b_wo, 1)

    def b_act(eng):
        eng.wait_ge(sems.b_cst, 64)
        for n in range(NCH):
            for h in range(H):
                gk = 4 * n + h
                eng.wait_ge(sems.b_pe, 12 * n + 4 + h + 1)
                ktb = B["KTB"] if h % 2 == 0 else B["OGB"]
                eng.activation(out=B["k2"][:, h * DK:(h + 1) * DK],
                               in_=ktb[:, 0:256],
                               func=AF.Identity,
                               scale=B["kdec"][:, h:h + 1]).then_inc(sems.b_k2, 1)
            for h in range(H):
                gk = 4 * n + h
                eng.wait_ge(sems.b_pe, 12 * n + 8 + h + 1)
                eng.activation(out=B["sq"][:, :], in_=B["P"][h % 2][:, :],
                               func=AF.Square,
                               accum_out=B["cols"][:, h:h + 1]).then_inc(sems.b_sq, 1)
                eng.wait_ge(sems.b_rc1, gk + 1)
                eng.activation(out=B["cols"][:, 8 + h:9 + h], in_=B["cols"][:, 4 + h:5 + h],
                               func=AF.Sqrt).then_inc(sems.b_rc2, 1)
                eng.wait_ge(sems.b_rc3, gk + 1)
                if gk >= 1:
                    eng.wait_ge(sems.b_og, gk)
                eng.activation(out=B["onf"][:, :], in_=B["P"][h % 2][:, :],
                               func=AF.Identity,
                               scale=B["cols"][:, 12 + h:13 + h]).then_inc(sems.b_on, 1)
            for w in range(2):
                eng.wait_ge(sems.b_ogt, 2 * n + w + 1)
                if n >= 1 and w == 0:
                    eng.wait_ge(sems.b_wo, n)
                eng.activation(out=B["ogfm"][:, w * 1024:(w + 1) * 1024],
                               in_=B["OGB"][:, :],
                               func=AF.Identity).then_inc(sems.b_oge, 1)

    def b_vector(eng):
        eng.wait_ge(sems.b_cst, 64)
        for n in range(NCH):
            sl = n % 2
            eng.wait_ge([sems.b_ix0, sems.b_ix1][sl], 16 * (n // 2 + 1))
            eng.tensor_scalar(out=B["xn"][:, :], in0=B["xs"][:, sl * D:(sl + 1) * D],
                              scalar1=negm1[:, n:n + 1], scalar2=rstd1[:, n:n + 1],
                              op0=ALU.add, op1=ALU.mult)
            for h in range(H):
                gk = 4 * n + h
                eng.wait_ge(sems.b_pe, 12 * n + h + 1)
                eng.tensor_mul(out=B["at"][:, h * CB:(h + 1) * CB],
                               in0=B["P"][2 + h % 2][:, 0:CB],
                               in1=B["dmask"][:, h * CB:(h + 1) * CB]
                               ).then_inc(sems.b_mask, 1)
            eng.wait_ge([sems.b_iq0, sems.b_iq1][sl], 16 * (n // 2 + 1))
            for h in range(H):
                for c in range(2):
                    a = eng.tensor_mul(
                        out=B["q2"][:, (2 * h + c) * CB:(2 * h + c + 1) * CB],
                        in0=B["q"][:, sl * 1024 + (2 * h + c) * CB: sl * 1024 + (2 * h + c + 1) * CB],
                        in1=B["cross"][:, h * CB:(h + 1) * CB])
                a.then_inc(sems.b_q2, 1)
            for h in range(H):
                gk = 4 * n + h
                eng.wait_ge(sems.b_sq, gk + 1)
                eng.tensor_scalar(out=B["cols"][:, 4 + h:5 + h], in0=B["cols"][:, h:h + 1],
                                  scalar1=1.0 / DV, scalar2=EPS,
                                  op0=ALU.mult, op1=ALU.add).then_inc(sems.b_rc1, 1)
                eng.wait_ge(sems.b_rc2, gk + 1)
                eng.reciprocal(out=B["cols"][:, 12 + h:13 + h],
                               in_=B["cols"][:, 8 + h:9 + h]).then_inc(sems.b_rc3, 1)
                eng.wait_ge(sems.b_pe, 12 * n + 8 + h + 1)
                if n == 0:
                    eng.wait_ge(sems.b_sbf, 4)
                for c in range(2):
                    a = eng.scalar_tensor_tensor(
                        out=B["Sf"][:, (n % 2) * 4096 + h * 1024 + c * 512:
                                    (n % 2) * 4096 + h * 1024 + (c + 1) * 512],
                        in0=B["Sf"][:, ((n + 1) % 2) * 4096 + h * 1024 + c * 512:
                                    ((n + 1) % 2) * 4096 + h * 1024 + (c + 1) * 512],
                        scalar=GC[h], in1=B["P"][2 + c][:, :],
                        op0=ALU.mult, op1=ALU.add)
                a.then_inc(sems.b_stt, 1)
            eng.wait_ge(sems.b_wo, n + 1)
            if n >= 1:
                eng.wait_ge(sems.b_comb, n)
            eng.tensor_add(out=B["so"][:, 0:512], in0=B["P"][4][:, :],
                           in1=B["xn"][:, 0:512])
            eng.tensor_add(out=B["so"][:, 512:D], in0=B["P"][5][:, :],
                           in1=B["xn"][:, 512:D]).then_inc(sems.b_woev, 1)

    def c_endstate():
        nxn = 16 * NTILES + 16
        if "dispdbg" in dbg:
            nxn += 16
        if "statsdbg" in dbg:
            nxn += 64
        return [(sems.c_dtpe, 4 * NTILES), (sems.c_lge, 8 * NTILES),
                (sems.c_dtp, 4 * NTILES), (sems.s_wstgf, C_WCHUNKS + 1),
                (sems.s_oxn, nxn),
                (sems.s_osh0, 16 * 2 * NTILES), (sems.s_osh1, 16 * 2 * NTILES),
                (sems.s_wstg0, 16 * 17), (sems.s_wstg1, 16 * 16),
                (sems.s_cst, 64), (sems.s_in, 16 * NTILES)]

    def de_endstate(pi):
        t = DE_TOT
        out = [(sems.d_acc, pi * t["acc"]), (sems.d_ev, pi * t["ev"]),
               (sems.d_silu, pi * t["silu"]), (sems.d_wstf, pi * t["wstf"]),
               (sems.d_out0, pi * t["out"]), (sems.d_out1, pi * t["out"]),
               (sems.d_wst0, pi * t["wst"]), (sems.d_wst1, pi * t["wst"]),
               (sems.d_fm, pi * t["fm"]),
               (sems.d_dp0, pi * t["dp"]), (sems.d_dp1, pi * t["dp"]),
               (sems.d_st0, pi * t["st"]), (sems.d_st1, pi * t["st"]),
               (sems.d_hsc, pi * t["hsc"]), (sems.d_bc, pi * t["bc"]),
               (sems.d_h, pi * t["h"])]
        if pi >= 2:
            out.append((sems.d_gf, t["gf"]))
        return out

    def wait_list(eng, lst):
        for s, v in lst:
            eng.wait_ge(s, v)

    with nc.Block() as block:
        @block.sync
        def _(eng):
            if "C" in phases:
                c_sync(eng)
            if "D" in phases:
                wait_list(eng, c_endstate())
                de_sync(eng, de_bases(0), 0, sh_d)
            if "Ep" in phases:
                wait_list(eng, de_endstate(1))
                de_sync(eng, de_bases(1), 8, moeps_d)
            if "A1" in phases:
                wait_list(eng, de_endstate(2))
                a1_sync(eng)
            if "B" in phases:
                wait_list(eng, a1_endstate())
                b_sync(eng)

        @block.tensor
        def _(eng):
            if "C" in phases:
                c_tensor(eng)
            if "D" in phases:
                wait_list(eng, c_endstate())
                de_tensor(eng, de_bases(0), False)
            if "Ep" in phases:
                wait_list(eng, de_endstate(1))
                de_tensor(eng, de_bases(1), True)
            if "A1" in phases:
                wait_list(eng, de_endstate(2))
                a1_tensor(eng)
            if "B" in phases:
                wait_list(eng, a1_endstate())
                b_tensor(eng)

        @block.vector
        def _(eng):
            if "C" in phases:
                c_vector(eng)
            if "D" in phases:
                wait_list(eng, c_endstate())
                de_vector(eng, de_bases(0), False)
            if "Ep" in phases:
                wait_list(eng, de_endstate(1))
                de_vector(eng, de_bases(1), True)
            if "A1" in phases:
                wait_list(eng, de_endstate(2))
                a1_vector(eng)
            if "B" in phases:
                wait_list(eng, a1_endstate())
                b_vector(eng)

        @block.scalar
        def _(eng):
            if "C" in phases:
                c_act(eng)
            if "D" in phases:
                wait_list(eng, c_endstate())
                de_act(eng, de_bases(0), False)
            if "Ep" in phases:
                wait_list(eng, de_endstate(1))
                de_act(eng, de_bases(1), True)
            if "A1" in phases:
                wait_list(eng, de_endstate(2))
                a1_act(eng)
            if "B" in phases:
                wait_list(eng, a1_endstate())
                b_act(eng)

        @block.gpsimd
        def _(eng):
            if "C" in phases:
                c_pool(eng)
                c_gpsimd_out(eng)
            if "D" in phases:
                wait_list(eng, c_endstate())
                de_pool(eng, de_bases(0), False)
            if "Ep" in phases:
                wait_list(eng, de_endstate(1))
                de_pool(eng, de_bases(1), True)
            if "A1" in phases:
                wait_list(eng, de_endstate(2))
                a1_pool(eng)
            if "B" in phases:
                wait_list(eng, a1_endstate())
                b_pool(eng)

    es.close()
    return nc


_consts = None


def _in_maps(inputs):
    global _consts
    if _consts is None:
        _consts = _host_constants()
    lay = _host_layouts(inputs)
    x = np.ascontiguousarray(np.asarray(inputs["L0_attn_x"], dtype=np.float32))
    base = dict(lay)
    base.update(_consts)
    maps = []
    for c in range(8):
        m = dict(base)
        m["x"] = np.ascontiguousarray(x[c])
        maps.append(m)
    return maps


def kernel(**inputs):
    nc = build_program()
    maps = _in_maps(inputs)
    res = run_bass_kernel_spmd(nc, maps, core_ids=list(range(8)))
    return np.stack([res.results[c]["out"] for c in range(8)], axis=0)


# version 03
# speedup vs baseline: 1.4519x; 1.1819x over previous
import numpy as np
import ml_dtypes
import sys

sys.path.insert(0, "/opt/trn_rl_repo")
import concourse.bass as bass
import concourse.mybir as mybir
from concourse.bass_utils import run_bass_kernel_spmd
from contextlib import ExitStack

F32 = mybir.dt.float32
BF16 = mybir.dt.bfloat16
AF = mybir.ActivationFunctionType
ALU = mybir.AluOpType
AX = mybir.AxisListType

T = 4096
D = 1024
H = 4
DK = 256
DV = 512
E = 16
HID = 512
S2 = 2048
CB = 128
NCH = T // CB
NT = 512
NTILES = T // NT
NTA = 256
NTILESA = T // NTA
EPS = 1e-5
GAMMA = [1.0 - 2.0 ** (-5.0 - h) for h in range(H)]
GC = [g ** CB for g in GAMMA]

PHASES = ["C", "D", "Ep", "A1", "B"]


def _host_constants():
    c = {}
    inv_freq = 1.0 / 10000.0 ** (np.arange(0, DK, 2, dtype=np.float64) / DK)
    ang = np.arange(T, dtype=np.float64)[None, :] * inv_freq[:, None]
    c["cos_t"] = np.cos(ang).astype(np.float32)
    c["sin_t"] = np.sin(ang).astype(np.float32)
    i = np.arange(CB, dtype=np.float64)
    dmt = np.zeros((H, CB, CB), np.float32)
    crs = np.zeros((H, CB, CB), np.float32)
    kdc = np.zeros((CB, H), np.float32)
    for h in range(H):
        g = GAMMA[h]
        diff = i[:, None] - i[None, :]
        m = np.where(diff >= 0, g ** np.maximum(diff, 0), 0.0)
        dmt[h] = m.T.astype(np.float32)
        crs[h] = np.tile((g ** (i + 1.0)).astype(np.float32)[None, :], (CB, 1))
        kdc[:, h] = (g ** (CB - 1.0 - i)).astype(np.float32)
    c["dmaskt"] = dmt
    c["crossb"] = crs
    c["kdec_col"] = kdc
    c["ident_f"] = np.eye(128, dtype=np.float32)
    c["ident_b"] = np.eye(128).astype(ml_dtypes.bfloat16)
    c["ones_b"] = np.ones((1, 128), ml_dtypes.bfloat16)
    return c


def _host_layouts(inputs):
    d = {}
    d["wq"] = np.ascontiguousarray(inputs["L0_attn_Wq"], np.float32)
    d["wk"] = np.ascontiguousarray(inputs["L0_attn_Wk"], np.float32)
    d["wv"] = np.ascontiguousarray(inputs["L0_attn_Wv"], np.float32)
    d["wg"] = np.ascontiguousarray(inputs["L0_attn_Wg"], np.float32)
    d["wo"] = np.ascontiguousarray(inputs["L0_attn_Wo"], np.float32)
    d["sfc1t"] = np.ascontiguousarray(np.asarray(inputs["L0_ffn_sfc1"]).T, np.float32)
    d["sfc2t"] = np.ascontiguousarray(np.asarray(inputs["L0_ffn_sfc2"]).T, np.float32)
    d["sfc3t"] = np.ascontiguousarray(np.asarray(inputs["L0_ffn_sfc3"]).T, np.float32)
    d["w1t"] = np.ascontiguousarray(np.asarray(inputs["L0_ffn_w1"]).transpose(0, 2, 1), np.float32)
    d["w2t"] = np.ascontiguousarray(np.asarray(inputs["L0_ffn_w2"]).transpose(0, 2, 1), np.float32)
    d["gatet"] = np.ascontiguousarray(np.asarray(inputs["L0_ffn_gate_w"]).T, np.float32)
    d["lnw_bc"] = np.ascontiguousarray(
        np.broadcast_to(np.asarray(inputs["L0_attn_ln_w"])[None, :], (128, D)), np.float32)
    d["lnb_col"] = np.ascontiguousarray(
        np.asarray(inputs["L0_attn_ln_b"]).reshape(D // 128, 128).T, np.float32)
    d["gnw_col"] = np.ascontiguousarray(
        np.asarray(inputs["L0_attn_g_norm_w"]).reshape(DV // 128, 128).T, np.float32)
    return d


def mm(eng, out, lhsT, rhs, start, stop):
    return eng.matmul(out=out, lhsT=lhsT, rhs=rhs, start=start, stop=stop)


def tp(eng, out, in_, ident, start, stop):
    return eng.matmul(out=out, lhsT=in_, rhs=ident, is_transpose=True,
                      start=start, stop=stop)


def build_program(debug_outputs=(), phases=PHASES):
    nc = bass.Bass("TRN2", target_bir_lowering=False, debug=False,
                   detect_race_conditions=False)

    def dram_in(name, shape, dt=F32):
        return nc.dram_tensor(name, list(shape), dt, kind="ExternalInput").ap()

    def scratch(name, shape, dt):
        kind = "ExternalOutput" if name in debug_outputs else "Internal"
        return nc.dram_tensor(name, list(shape), dt, kind=kind).ap()

    x_d = dram_in("x", (T, D))
    wq_d = dram_in("wq", (D, D))
    wk_d = dram_in("wk", (D, D))
    wv_d = dram_in("wv", (D, 2 * D))
    wg_d = dram_in("wg", (D, 2 * D))
    wo_d = dram_in("wo", (2 * D, D))
    sfc1t_d = dram_in("sfc1t", (D, S2))
    sfc2t_d = dram_in("sfc2t", (D, S2))
    sfc3t_d = dram_in("sfc3t", (S2, D))
    w1t_d = dram_in("w1t", (E, D, HID))
    w2t_d = dram_in("w2t", (E, HID, D))
    gatet_d = dram_in("gatet", (D, E))
    lnwbc_d = dram_in("lnw_bc", (128, D))
    lnbcol_d = dram_in("lnb_col", (128, 8))
    gnwcol_d = dram_in("gnw_col", (128, 4))
    cos_d = dram_in("cos_t", (128, T))
    sin_d = dram_in("sin_t", (128, T))
    dmaskt_d = dram_in("dmaskt", (H, CB, CB))
    crossb_d = dram_in("crossb", (H, CB, CB))
    kdeccol_d = dram_in("kdec_col", (CB, H))
    identf_d = dram_in("ident_f", (128, 128))
    identb_d = dram_in("ident_b", (128, 128), BF16)
    onesb_d = dram_in("ones_b", (1, 128), BF16)

    out_d = nc.dram_tensor("out", [T, D], F32, kind="ExternalOutput").ap()

    xnfm_d = scratch("xnfm", (8, 128, T), BF16)
    sh_d = scratch("sh", (T, D), F32)
    moeps_d = scratch("moeps", (T, D), F32)
    gf_d = scratch("gf", (T, D), BF16)
    qp_d = scratch("qp", (8, 128, T), BF16)
    kp_d = scratch("kp", (8, 128, T), BF16)
    vtm_d = scratch("vtm", (T, 2 * D), BF16)
    dispt_d = scratch("dispt", (16, T), BF16)
    sgtm_d = scratch("sgtm", (T, 2 * D), BF16)
    dbg = {}
    if "dispdbg" in debug_outputs:
        dbg["dispdbg"] = nc.dram_tensor("dispdbg", [16, T], BF16,
                                        kind="ExternalOutput").ap()
    if "statsdbg" in debug_outputs:
        dbg["statsdbg"] = nc.dram_tensor("statsdbg", [128, 4 * NCH], F32,
                                         kind="ExternalOutput").ap()

    es = ExitStack()

    identf = nc.alloc_sbuf_tensor("identf", [128, 128], F32)
    identb = nc.alloc_sbuf_tensor("identb", [128, 128], BF16)
    onesb = nc.alloc_sbuf_tensor("onesb", [1, 128], BF16)
    negm1 = nc.alloc_sbuf_tensor("negm1", [128, NCH], F32)
    rstd1 = nc.alloc_sbuf_tensor("rstd1", [128, NCH], F32)
    negm2 = nc.alloc_sbuf_tensor("negm2", [128, NCH], F32)
    rstd2 = nc.alloc_sbuf_tensor("rstd2", [128, NCH], F32)
    lnbcol = nc.alloc_sbuf_tensor("lnbcol", [128, 8], F32)
    lnbcol_b = nc.alloc_sbuf_tensor("lnbcol_b", [128, 8], BF16)

    def mkphase(fn):
        with ExitStack() as ph:
            aa = lambda n, s, dt: ph.enter_context(nc.sbuf_tensor(n, s, dt))
            pp = lambda n, s, dt: ph.enter_context(nc.psum_tensor(n, s, dt))
            return fn(aa, pp)

    C = mkphase(lambda a, p: dict(
        sfc1b=a("C_sfc1b", [128, 8 * S2], BF16),
        sfc2b=a("C_sfc2b", [128, 8 * S2], BF16),
        sfc3b=a("C_sfc3b", [128, 16 * D], BF16),
        gateb=a("C_gateb", [128, 8 * E], BF16),
        x=a("C_x", [128, 4 * D], F32),
        xn=a("C_xn", [128, 4 * D], F32),
        fm=a("C_fm", [128, 8 * NT], BF16),
        hs=a("C_hs", [128, 16 * NT], BF16),
        scr=a("C_scr", [128, 3 * D], F32),
        silu=a("C_silu", [128, 2 * NT], F32),
        cols=a("C_cols", [128, 16], F32),
        lg=a("C_lg", [128, 16], F32),
        lg2=a("C_lg2", [128, 16], F32),
        eq1=a("C_eq1", [128, 16], F32),
        eq2=a("C_eq2", [128, 16], F32),
        disp=a("C_disp", [128, 16], F32),
        dispT=a("C_dispT", [16, T], BF16),
        mcol=a("C_mcol", [128, 8], F32),
        P=[p(f"C_P{i}", [128, 512], F32) for i in range(6)],
        TPB=[p(f"C_TPB{i}", [128, 512], F32) for i in range(2)],
    ))

    DE = mkphase(lambda a, p: dict(
        w1b=a("D_w1b", [128, 8 * 8 * 512], BF16),
        w2b=a("D_w2b", [128, 8 * 4 * 1024], BF16),
        fm=a("D_fm", [128, 8 * NT], BF16),
        hsc=a("D_hsc", [128, 8 * 4 * 512], BF16),
        stream=a("D_stream", [128, 2 * D], F32),
        outh=a("D_outh", [128, 2 * D], F32),
        gfh=a("D_gfh", [128, 2 * D], BF16),
        silu=a("D_silu", [128, 2 * 512], F32),
        dprow=a("D_dprow", [1, 2 * 512], BF16),
        P=[p(f"D_P{i}", [128, 512], F32) for i in range(8)],
    ))

    B = mkphase(lambda a, p: dict(
        wob=a("B_wob", [128, 16 * D], BF16),
        wraw=a("B_wraw", [128, 16 * D], BF16),
        gnw=a("B_gnw", [128, 4], F32),
        dmask=a("B_dmask", [128, 4 * CB], F32),
        cross=a("B_cross", [128, 4 * CB], F32),
        kdec=a("B_kdec", [128, 4], F32),
        Sf=a("B_Sf", [128, 2 * H * 2 * DV], F32),
        Sb=a("B_Sb", [128, H * 2 * DV], BF16),
        q=a("B_q", [128, 2 * 8 * CB], BF16),
        k=a("B_k", [128, 2 * 8 * CB], BF16),
        v=a("B_v", [128, 2 * 2048], BF16),
        sgs=a("B_sgs", [128, 2 * 2048], BF16),
        xs=a("B_xs", [128, 2 * D], F32),
        gfs=a("B_gfs", [128, 2 * D], BF16),
        xn=a("B_xn", [128, D], F32),
        q2=a("B_q2", [128, 4 * 2 * CB], BF16),
        k2=a("B_k2", [128, 4 * DK], BF16),
        at=a("B_at", [128, 4 * CB], BF16),
        onf=a("B_onf", [128, DV], F32),
        og=a("B_og", [128, H * DV], BF16),
        ogfm=a("B_ogfm", [128, 16 * 128], BF16),
        sq=a("B_sq", [128, DV], F32),
        cols=a("B_cols", [128, 16], F32),
        so=a("B_so", [128, D], F32),
        t1=a("B_t1", [128, D], F32),
        outt=a("B_outt", [128, 2 * D], F32),
        P=[p(f"B_P{i}", [128, 512], F32) for i in range(6)],
        KTB=p("B_KTB", [128, 1024], BF16),
        OGB=p("B_OGB", [128, 1024], BF16),
    ))

    sem_names = [
        "s_in", "s_cst", "s_oxn", "s_osh0", "s_osh1",
        "s_wstgf",
        "c_st1", "c_c1", "c_sq1", "c_xn", "c_st2", "c_c2", "c_sq2",
        "c_tp", "c_fme", "c_ab", "c_silu", "c_hs", "c_s3", "c_s3e",
        "c_lg", "c_lge", "c_disp", "c_dtp", "c_dtpe",
        "d_wstf", "d_fm", "d_st0", "d_st1",
        "d_dp0", "d_dp1", "d_h", "d_bc", "d_silu", "d_hsc", "d_acc", "d_ev",
        "d_gf", "d_out0", "d_out1",
        "a_wstf", "a_cst", "a_x0", "a_x1", "a_cs0", "a_cs1", "a_xn",
        "a_nrm", "a_tp", "a_nfm", "a_qk", "a_qke", "a_rope", "a_vg", "a_vge",
        "a_oq", "a_ov", "a_pre", "a_pree",
        "b_cst", "b_wstf", "b_wsc",
        "b_iq0", "b_iq1", "b_ik0", "b_ik1", "b_iv0", "b_iv1",
        "b_isg0", "b_isg1", "b_ix0", "b_ix1", "b_igf0", "b_igf1",
        "b_pe", "b_mask", "b_q2", "b_k2", "b_sq", "b_rc1", "b_rc2", "b_rc3",
        "b_on", "b_og", "b_stt", "b_sbf", "b_ogt", "b_oge", "b_wo", "b_woev",
        "b_comb", "b_oo0", "b_oo1",
    ]

    class _S:
        pass
    sems = _S()
    for n in sem_names:
        setattr(sems, n, es.enter_context(nc.semaphore(n)))

    C_WCHUNKS = 8 + 8 + 16 + 1

    def c_sync(eng):
        eng.dma_start(out=identf[:, :], in_=identf_d).then_inc(sems.s_cst, 16)
        eng.dma_start(out=identb[:, :], in_=identb_d).then_inc(sems.s_cst, 16)
        eng.dma_start(out=onesb[:, :], in_=onesb_d).then_inc(sems.s_cst, 16)
        eng.dma_start(out=lnbcol[:, :], in_=lnbcol_d).then_inc(sems.s_cst, 16)
        for i in range(NTILES):
            if i >= 1:
                eng.wait_ge(sems.c_xn, i)
            eng.dma_start(
                out=C["x"][:, :].rearrange("p (s d) -> p s d", s=4),
                in_=x_d[i * NT:(i + 1) * NT, :].rearrange("(s p) d -> p s d", p=128),
            ).then_inc(sems.s_in, 16)

    def c_pool(eng):
        eng.dma_start(out=C["sfc1b"][:, :].rearrange("p (kc n) -> p kc n", kc=8),
                      in_=sfc1t_d.rearrange("(kc p) n -> p kc n", p=128)
                      ).then_inc(sems.s_wstgf, 16)
        eng.dma_start(out=C["sfc2b"][:, :].rearrange("p (kc n) -> p kc n", kc=8),
                      in_=sfc2t_d.rearrange("(kc p) n -> p kc n", p=128)
                      ).then_inc(sems.s_wstgf, 16)
        eng.dma_start(out=C["sfc3b"][:, :].rearrange("p (sc n) -> p sc n", sc=16),
                      in_=sfc3t_d.rearrange("(sc p) n -> p sc n", p=128)
                      ).then_inc(sems.s_wstgf, 16)
        eng.dma_start(out=C["gateb"][:, :].rearrange("p (kc e) -> p kc e", kc=8),
                      in_=gatet_d.rearrange("(kc p) e -> p kc e", p=128)
                      ).then_inc(sems.s_wstgf, 16)
        eng.dma_start(out=lnbcol_b[:, :], in_=lnbcol_d).then_inc(sems.s_wstgf, 16)

    def c_act(eng):
        for i in range(NTILES):
            eng.wait_ge(sems.s_in, 16 * (i + 1))
            for st in range(4):
                eng.activation(out=C["scr"][:, 0:D], in_=C["x"][:, st * D:(st + 1) * D],
                               func=AF.Square, accum_out=C["cols"][:, st:st + 1])
                a = eng.activation(out=C["scr"][:, 0:D], in_=C["x"][:, st * D:(st + 1) * D],
                                   func=AF.Identity, accum_out=C["cols"][:, 4 + st:4 + st + 1])
            a.then_inc(sems.c_st1, 1)
            eng.wait_ge(sems.c_c1, i + 1)
            eng.activation(out=C["mcol"][:, 0:4], in_=C["cols"][:, 8:12],
                           func=AF.Sqrt).then_inc(sems.c_sq1, 1)
            eng.wait_ge(sems.c_xn, i + 1)
            for st in range(4):
                eng.activation(out=C["scr"][:, 0:D], in_=C["xn"][:, st * D:(st + 1) * D],
                               func=AF.Square, accum_out=C["cols"][:, st:st + 1])
                a = eng.activation(out=C["scr"][:, 0:D], in_=C["xn"][:, st * D:(st + 1) * D],
                                   func=AF.Identity, accum_out=C["cols"][:, 4 + st:4 + st + 1])
            a.then_inc(sems.c_st2, 1)
            eng.wait_ge(sems.c_c2, i + 1)
            eng.activation(out=C["mcol"][:, 4:8], in_=C["cols"][:, 8:12],
                           func=AF.Sqrt).then_inc(sems.c_sq2, 1)
            for sc in range(16):
                Ksc = i * 16 + sc
                eng.wait_ge(sems.c_ab, Ksc + 1)
                if Ksc >= 2:
                    eng.wait_ge(sems.c_hs, Ksc - 1)
                eng.activation(out=C["silu"][:, (sc % 2) * NT:(sc % 2 + 1) * NT],
                               in_=C["P"][sc % 2][:, :],
                               func=AF.Silu).then_inc(sems.c_silu, 1)
            for mc in range(4):
                Km = i * 4 + mc
                eng.wait_ge(sems.c_disp, 2 * Km + 1)
                eng.activation(out=C["mcol"][:, 3:4], in_=C["mcol"][:, 2:3],
                               func=AF.Tanh, scale=0.5).then_inc(sems.c_dtp, 1)

    def c_vector(eng):
        for i in range(NTILES):
            eng.wait_ge(sems.c_st1, i + 1)
            nm = negm1[:, i * 4:(i + 1) * 4]
            eng.tensor_scalar(out=nm, in0=C["cols"][:, 4:8], scalar1=-1.0 / D,
                              scalar2=None, op0=ALU.mult)
            eng.drain()
            eng.tensor_mul(out=C["cols"][:, 12:16], in0=nm, in1=nm)
            eng.drain()
            eng.scalar_tensor_tensor(out=C["cols"][:, 8:12], in0=C["cols"][:, 0:4],
                                     scalar=1.0 / D, in1=C["cols"][:, 12:16],
                                     op0=ALU.mult, op1=ALU.subtract)
            eng.drain()
            eng.tensor_scalar(out=C["cols"][:, 8:12], in0=C["cols"][:, 8:12],
                              scalar1=EPS, scalar2=None,
                              op0=ALU.add).then_inc(sems.c_c1, 1)
            eng.wait_ge(sems.c_sq1, i + 1)
            eng.reciprocal(out=rstd1[:, i * 4:(i + 1) * 4], in_=C["mcol"][:, 0:4])
            eng.drain()
            for st in range(4):
                c = i * 4 + st
                a = eng.tensor_scalar(out=C["xn"][:, st * D:(st + 1) * D],
                                      in0=C["x"][:, st * D:(st + 1) * D],
                                      scalar1=negm1[:, c:c + 1], scalar2=rstd1[:, c:c + 1],
                                      op0=ALU.add, op1=ALU.mult)
            a.then_inc(sems.c_xn, 1)
            eng.wait_ge(sems.c_st2, i + 1)
            nm2 = negm2[:, i * 4:(i + 1) * 4]
            eng.tensor_scalar(out=nm2, in0=C["cols"][:, 4:8], scalar1=-1.0 / D,
                              scalar2=None, op0=ALU.mult)
            eng.drain()
            eng.tensor_mul(out=C["cols"][:, 12:16], in0=nm2, in1=nm2)
            eng.drain()
            eng.scalar_tensor_tensor(out=C["cols"][:, 8:12], in0=C["cols"][:, 0:4],
                                     scalar=1.0 / D, in1=C["cols"][:, 12:16],
                                     op0=ALU.mult, op1=ALU.subtract)
            eng.drain()
            eng.tensor_scalar(out=C["cols"][:, 8:12], in0=C["cols"][:, 8:12],
                              scalar1=EPS, scalar2=None,
                              op0=ALU.add).then_inc(sems.c_c2, 1)
            eng.wait_ge(sems.c_sq2, i + 1)
            eng.reciprocal(out=rstd2[:, i * 4:(i + 1) * 4], in_=C["mcol"][:, 4:8])
            for g in range(8):
                K = i * 8 + g
                st, gg = g // 2, g % 2
                eng.wait_ge(sems.c_tp, K + 1)
                if i > 0 and g == 0:
                    eng.wait_ge(sems.s_oxn, 16 * i)
                eng.tensor_copy(
                    out=C["fm"][:, :].rearrange("p (kc t) -> p kc t", kc=8)
                        [:, gg * 4:(gg + 1) * 4, st * 128:(st + 1) * 128],
                    in_=C["TPB"][K % 2][:, :].rearrange("p (s f) -> p s f", s=4)
                ).then_inc(sems.c_fme, 1)
            for sc in range(16):
                Ksc = i * 16 + sc
                eng.wait_ge(sems.c_silu, Ksc + 1)
                eng.tensor_mul(out=C["hs"][:, sc * NT:(sc + 1) * NT],
                               in0=C["silu"][:, (sc % 2) * NT:(sc % 2 + 1) * NT],
                               in1=C["P"][2 + sc % 2][:, :]).then_inc(sems.c_hs, 1)
            for g in range(8):
                K = i * 8 + g
                mc, ng = g // 2, g % 2
                eng.wait_ge(sems.c_s3, K + 1)
                if ng == 0:
                    occ = 2 * i + mc // 2
                    if occ > 0:
                        eng.wait_ge([sems.s_osh0, sems.s_osh1][mc % 2], 16 * occ)
                off = (1 + mc % 2) * D + ng * 512
                eng.tensor_copy(out=C["scr"][:, off:off + 512],
                                in_=C["P"][4 + ng][:, :]).then_inc(sems.c_s3e, 1)
            for mc in range(4):
                Km = i * 4 + mc
                eng.wait_ge(sems.c_lg, Km + 1)
                eng.tensor_copy(out=C["lg"][:, :],
                                in_=C["P"][4][:, 0:16]).then_inc(sems.c_lge, 1)
                eng.drain()
                eng.reduce_max(out=C["mcol"][:, 0:1], in_=C["lg"][:, :], axis=AX.X)
                eng.drain()
                eng.tensor_scalar(out=C["eq1"][:, :], in0=C["lg"][:, :],
                                  scalar1=C["mcol"][:, 0:1], scalar2=None, op0=ALU.is_ge)
                eng.drain()
                eng.scalar_tensor_tensor(out=C["lg2"][:, :], in0=C["eq1"][:, :],
                                         scalar=-1e30, in1=C["lg"][:, :],
                                         op0=ALU.mult, op1=ALU.add)
                eng.drain()
                eng.reduce_max(out=C["mcol"][:, 1:2], in_=C["lg2"][:, :], axis=AX.X)
                eng.drain()
                eng.tensor_scalar(out=C["eq2"][:, :], in0=C["lg2"][:, :],
                                  scalar1=C["mcol"][:, 1:2], scalar2=None, op0=ALU.is_ge)
                eng.tensor_sub(out=C["mcol"][:, 2:3], in0=C["mcol"][:, 0:1],
                               in1=C["mcol"][:, 1:2]).then_inc(sems.c_disp, 1)
                eng.wait_ge(sems.c_dtp, Km + 1)
                eng.tensor_scalar(out=C["mcol"][:, 4:5], in0=C["mcol"][:, 3:4],
                                  scalar1=0.5, scalar2=0.5, op0=ALU.mult, op1=ALU.add)
                eng.drain()
                eng.tensor_scalar(out=C["mcol"][:, 5:6], in0=C["mcol"][:, 4:5],
                                  scalar1=-1.0, scalar2=1.0, op0=ALU.mult, op1=ALU.add)
                eng.tensor_scalar(out=C["disp"][:, :], in0=C["eq1"][:, :],
                                  scalar1=C["mcol"][:, 4:5], scalar2=None, op0=ALU.mult)
                eng.drain()
                eng.scalar_tensor_tensor(out=C["disp"][:, :], in0=C["eq2"][:, :],
                                         scalar=C["mcol"][:, 5:6], in1=C["disp"][:, :],
                                         op0=ALU.mult,
                                         op1=ALU.add).then_inc(sems.c_disp, 1)
                eng.wait_ge(sems.c_dtpe, Km + 1)
                eng.tensor_copy(out=C["dispT"][0:16, Km * 128:(Km + 1) * 128],
                                in_=C["P"][5][0:16, 0:128]).then_inc(sems.c_lge, 1)

    def c_tensor(eng):
        for i in range(NTILES):
            eng.wait_ge(sems.c_xn, i + 1)
            for g in range(8):
                K = i * 8 + g
                st, gg = g // 2, g % 2
                if K >= 2:
                    eng.wait_ge(sems.c_fme, K - 1)
                for s in range(4):
                    kc = gg * 4 + s
                    a = tp(eng, C["TPB"][K % 2][:, s * 128:(s + 1) * 128],
                           C["xn"][:, st * D + kc * 128: st * D + (kc + 1) * 128],
                           identf[:, :], start=(s == 0), stop=(s == 3))
                a.then_inc(sems.c_tp, 1)
            eng.wait_ge(sems.c_fme, (i + 1) * 8)
            if i == 0:
                eng.wait_ge(sems.s_wstgf, 32)
            fmv = C["fm"][:, :].rearrange("p (kc t) -> p kc t", kc=8)
            for sc in range(16):
                Ksc = i * 16 + sc
                if Ksc >= 2:
                    eng.wait_ge(sems.c_silu, Ksc - 1)
                    eng.wait_ge(sems.c_hs, Ksc - 1)
                for kc in range(8):
                    mm(eng, C["P"][sc % 2][:, :],
                       C["sfc1b"][:, kc * 2048 + sc * 128: kc * 2048 + (sc + 1) * 128],
                       fmv[:, kc, :], start=(kc == 0), stop=(kc == 7))
                for kc in range(8):
                    a = mm(eng, C["P"][2 + sc % 2][:, :],
                           C["sfc2b"][:, kc * 2048 + sc * 128: kc * 2048 + (sc + 1) * 128],
                           fmv[:, kc, :], start=(kc == 0), stop=(kc == 7))
                a.then_inc(sems.c_ab, 1)
            eng.wait_ge(sems.c_hs, (i + 1) * 16)
            if i == 0:
                eng.wait_ge(sems.s_wstgf, 48)
            hsv = C["hs"][:, :].rearrange("p (sc t) -> p sc t", sc=16)
            for g in range(8):
                K = i * 8 + g
                mc, ng = g // 2, g % 2
                if K >= 2:
                    eng.wait_ge(sems.c_s3e, K - 1)
                for sc in range(16):
                    a = mm(eng, C["P"][4 + ng][:, :],
                           hsv[:, sc, mc * 128:(mc + 1) * 128],
                           C["sfc3b"][:, sc * D + ng * 512: sc * D + ng * 512 + 512],
                           start=(sc == 0), stop=(sc == 15))
                a.then_inc(sems.c_s3, 1)
            for mc in range(4):
                Km = i * 4 + mc
                if mc == 0:
                    eng.wait_ge(sems.c_s3e, i * 8 + 8)
                    if i == 0:
                        eng.wait_ge(sems.s_wstgf, 64)
                if Km >= 1:
                    eng.wait_ge(sems.c_lge, 2 * Km - 1)
                for kc in range(8):
                    a = mm(eng, C["P"][4][:, 0:16],
                           fmv[:, kc, mc * 128:(mc + 1) * 128],
                           C["gateb"][:, kc * 16:(kc + 1) * 16],
                           start=(kc == 0), stop=(kc == 7))
                a.then_inc(sems.c_lg, 1)
                eng.wait_ge(sems.c_disp, 2 * Km + 2)
                if Km >= 1:
                    eng.wait_ge(sems.c_lge, 2 * Km)
                tp(eng, C["P"][5][0:16, 0:128], C["disp"][:, :], identf[:, :],
                   start=True, stop=True).then_inc(sems.c_dtpe, 1)

    def c_gpsimd_out(eng):
        for i in range(NTILES):
            eng.wait_ge(sems.c_fme, (i + 1) * 8)
            eng.dma_start(
                out=xnfm_d[:, :, i * NT:(i + 1) * NT].rearrange("kc p t -> p kc t"),
                in_=C["fm"][:, :].rearrange("p (kc t) -> p kc t", kc=8),
            ).then_inc(sems.s_oxn, 16)
            for mc in range(4):
                eng.wait_ge(sems.c_s3e, i * 8 + 2 * (mc + 1))
                off = (1 + mc % 2) * D
                eng.dma_start(
                    out=sh_d[i * NT + mc * 128: i * NT + (mc + 1) * 128, :],
                    in_=C["scr"][:, off:off + D],
                ).then_inc([sems.s_osh0, sems.s_osh1][mc % 2], 16)
        nxn = 16 * NTILES
        eng.wait_ge(sems.c_lge, 2 * 4 * NTILES)
        eng.dma_start(out=dispt_d, in_=C["dispT"][:, :]).then_inc(sems.s_oxn, 16)
        nxn += 16
        if dbg:
            eng.wait_ge(sems.c_lge, 2 * 4 * NTILES)
        if "dispdbg" in dbg:
            eng.dma_start(out=dbg["dispdbg"], in_=C["dispT"][:, :]).then_inc(sems.s_oxn, 16)
            nxn += 16
        if "statsdbg" in dbg:
            eng.dma_start(out=dbg["statsdbg"][:, 0:NCH], in_=negm1[:, :]).then_inc(sems.s_oxn, 16)
            eng.dma_start(out=dbg["statsdbg"][:, NCH:2 * NCH], in_=rstd1[:, :]).then_inc(sems.s_oxn, 16)
            eng.dma_start(out=dbg["statsdbg"][:, 2 * NCH:3 * NCH], in_=negm2[:, :]).then_inc(sems.s_oxn, 16)
            eng.dma_start(out=dbg["statsdbg"][:, 3 * NCH:4 * NCH], in_=rstd2[:, :]).then_inc(sems.s_oxn, 16)
            nxn += 64
        eng.wait_ge(sems.s_oxn, nxn)
        eng.wait_ge(sems.s_osh0, 16 * 2 * NTILES)
        eng.wait_ge(sems.s_osh1, 16 * 2 * NTILES)

    DE_TOT = dict(wstf=256, fm=16 * NTILES, st=16 * 2 * NTILES,
                  dp=16 * 4 * NTILES, h=32 * NTILES, bc=8 * NTILES,
                  silu=32 * NTILES, hsc=32 * NTILES, acc=8 * NTILES,
                  ev=4 * NTILES, gf=4 * NTILES, out=16 * 2 * NTILES)

    def de_bases(pi):
        b = {k: pi * v for k, v in DE_TOT.items()}
        b["gf"] = 0
        return b

    def de_sync(eng, bs, ge0, stream_d):
        for i in range(NTILES):
            w = bs["h"] + 32 * i
            if w > 0:
                eng.wait_ge(sems.d_h, w)
            eng.dma_start(
                out=DE["fm"][:, :].rearrange("p (kc t) -> p kc t", kc=8),
                in_=xnfm_d[:, :, i * NT:(i + 1) * NT].rearrange("kc p t -> p kc t"),
            ).then_inc(sems.d_fm, 16)
            for e in range(8):
                w = bs["bc"] + 8 * i + e - 1
                if w > 0:
                    eng.wait_ge(sems.d_bc, w)
                eng.dma_start(
                    out=DE["dprow"][0:1, (e % 2) * 512:(e % 2) * 512 + 512],
                    in_=dispt_d[ge0 + e:ge0 + e + 1, i * NT:(i + 1) * NT],
                ).then_inc([sems.d_dp0, sems.d_dp1][e % 2], 16)
            for mc in range(4):
                w = bs["ev"] + 4 * i + mc - 1
                if w > 0:
                    eng.wait_ge(sems.d_ev, w)
                eng.dma_start(
                    out=DE["stream"][:, (mc % 2) * D:(mc % 2) * D + D],
                    in_=stream_d[i * NT + mc * 128: i * NT + (mc + 1) * 128, :],
                ).then_inc([sems.d_st0, sems.d_st1][mc % 2], 16)
        eng.wait_ge(sems.d_fm, bs["fm"] + 16 * NTILES)
        eng.wait_ge(sems.d_dp0, bs["dp"] + 16 * 4 * NTILES)
        eng.wait_ge(sems.d_dp1, bs["dp"] + 16 * 4 * NTILES)
        eng.wait_ge(sems.d_st0, bs["st"] + 16 * 2 * NTILES)
        eng.wait_ge(sems.d_st1, bs["st"] + 16 * 2 * NTILES)

    def de_pool(eng, bs, is_e):
        ge0 = 8 if is_e else 0
        for e in range(8):
            eng.dma_start(
                out=DE["w1b"][:, e * 4096:(e + 1) * 4096]
                    .rearrange("p (kc h) -> p kc h", kc=8),
                in_=w1t_d[ge0 + e].rearrange("(kc p) h -> p kc h", p=128),
            ).then_inc(sems.d_wstf, 16)
            eng.dma_start(
                out=DE["w2b"][:, e * 4096:(e + 1) * 4096]
                    .rearrange("p (kc d) -> p kc d", kc=4),
                in_=w2t_d[ge0 + e].rearrange("(kc p) d -> p kc d", p=128),
            ).then_inc(sems.d_wstf, 16)
        for i in range(NTILES):
            for mc in range(4):
                if is_e:
                    eng.wait_ge(sems.d_gf, bs["gf"] + 4 * i + mc + 1)
                    srcb = DE["gfh"][:, (mc % 2) * D:(mc % 2) * D + D]
                    dstd = gf_d[i * NT + mc * 128: i * NT + (mc + 1) * 128, :]
                else:
                    eng.wait_ge(sems.d_ev, bs["ev"] + 4 * i + mc + 1)
                    srcb = DE["outh"][:, (mc % 2) * D:(mc % 2) * D + D]
                    dstd = moeps_d[i * NT + mc * 128: i * NT + (mc + 1) * 128, :]
                eng.dma_start(out=dstd, in_=srcb).then_inc(
                    [sems.d_out0, sems.d_out1][mc % 2], 16)
        eng.wait_ge(sems.d_out0, bs["out"] + 16 * 2 * NTILES)
        eng.wait_ge(sems.d_out1, bs["out"] + 16 * 2 * NTILES)

    def de_tensor(eng, bs, is_e):
        fmv = DE["fm"][:, :].rearrange("p (kc t) -> p kc t", kc=8)
        hscv = DE["hsc"][:, :].rearrange("p (e hc t) -> p e hc t", e=8, hc=4)
        for i in range(NTILES):
            eng.wait_ge(sems.d_fm, bs["fm"] + 16 * (i + 1))
            for e in range(8):
                Ke = 8 * i + e
                if i == 0:
                    eng.wait_ge(sems.d_wstf, bs["wstf"] + 16 * (2 * e + 1))
                w = bs["silu"] + 4 * Ke
                if w > 0:
                    eng.wait_ge(sems.d_silu, w)
                for hc in range(4):
                    for kc in range(8):
                        a = mm(eng, DE["P"][hc][:, :],
                               DE["w1b"][:, e * 4096 + kc * 512 + hc * 128:
                                         e * 4096 + kc * 512 + (hc + 1) * 128],
                               fmv[:, kc, :], start=(kc == 0), stop=(kc == 7))
                    a.then_inc(sems.d_h, 1)
                eng.wait_ge([sems.d_dp0, sems.d_dp1][e % 2],
                            bs["dp"] + 16 * (4 * i + e // 2 + 1))
                w = bs["hsc"] + 4 * (Ke - 1)
                if w > 0:
                    eng.wait_ge(sems.d_hsc, w)
                if e < 2 and (bs["ev"] + 4 * i) > 0:
                    eng.wait_ge(sems.d_ev, bs["ev"] + 4 * i)
                mm(eng, DE["P"][4 + e % 2][:, :], onesb[0:1, :],
                   DE["dprow"][0:1, (e % 2) * 512:(e % 2) * 512 + 512],
                   start=True, stop=True).then_inc(sems.d_bc, 1)
            eng.wait_ge(sems.d_hsc, bs["hsc"] + 32 * (i + 1))
            if i == 0:
                eng.wait_ge(sems.d_wstf, bs["wstf"] + 16 * 16)
            for mc in range(4):
                w = bs["ev"] + 4 * i + mc - 1
                if w > 0:
                    eng.wait_ge(sems.d_ev, w)
                ob = 6 if mc % 2 == 0 else 4
                for ng in range(2):
                    for e in range(8):
                        for kc in range(4):
                            a = mm(eng, DE["P"][ob + ng][:, :],
                                   hscv[:, e, kc, mc * 128:(mc + 1) * 128],
                                   DE["w2b"][:, e * 4096 + kc * 1024 + ng * 512:
                                             e * 4096 + kc * 1024 + ng * 512 + 512],
                                   start=(e == 0 and kc == 0),
                                   stop=(e == 7 and kc == 3))
                    a.then_inc(sems.d_acc, 1)

    def de_act(eng, bs, is_e):
        for i in range(NTILES):
            for e in range(8):
                for hc in range(4):
                    ghc = 32 * i + 4 * e + hc
                    eng.wait_ge(sems.d_h, bs["h"] + ghc + 1)
                    w = bs["hsc"] + ghc - 1
                    if w > 0:
                        eng.wait_ge(sems.d_hsc, w)
                    eng.activation(out=DE["silu"][:, (ghc % 2) * 512:(ghc % 2) * 512 + 512],
                                   in_=DE["P"][hc][:, :],
                                   func=AF.Silu).then_inc(sems.d_silu, 1)
            if is_e:
                for mc in range(4):
                    eng.wait_ge(sems.d_ev, bs["ev"] + 4 * i + mc + 1)
                    w = bs["out"] + 16 * (2 * i + mc // 2)
                    if w > 0:
                        eng.wait_ge([sems.d_out0, sems.d_out1][mc % 2], w)
                    eng.activation(out=DE["gfh"][:, (mc % 2) * D:(mc % 2) * D + D],
                                   in_=DE["outh"][:, (mc % 2) * D:(mc % 2) * D + D],
                                   func=AF.Silu).then_inc(sems.d_gf, 1)

    def de_vector(eng, bs, is_e):
        hscv = DE["hsc"][:, :].rearrange("p (e hc t) -> p e hc t", e=8, hc=4)
        for i in range(NTILES):
            for e in range(8):
                for hc in range(4):
                    ghc = 32 * i + 4 * e + hc
                    eng.wait_ge(sems.d_silu, bs["silu"] + ghc + 1)
                    eng.wait_ge(sems.d_bc, bs["bc"] + 8 * i + e + 1)
                    eng.tensor_mul(out=hscv[:, e, hc, :],
                                   in0=DE["silu"][:, (ghc % 2) * 512:(ghc % 2) * 512 + 512],
                                   in1=DE["P"][4 + e % 2][:, :]).then_inc(sems.d_hsc, 1)
            for mc in range(4):
                eng.wait_ge(sems.d_acc, bs["acc"] + 8 * i + 2 * mc + 2)
                eng.wait_ge([sems.d_st0, sems.d_st1][mc % 2],
                            bs["st"] + 16 * (2 * i + mc // 2 + 1))
                if is_e:
                    w = bs["gf"] + 4 * i + mc - 1
                    if w > 0:
                        eng.wait_ge(sems.d_gf, w)
                else:
                    w = bs["out"] + 16 * (2 * i + mc // 2)
                    if w > 0:
                        eng.wait_ge([sems.d_out0, sems.d_out1][mc % 2], w)
                off = (mc % 2) * D
                ob = 6 if mc % 2 == 0 else 4
                eng.tensor_add(out=DE["outh"][:, off:off + 512],
                               in0=DE["P"][ob][:, :],
                               in1=DE["stream"][:, off:off + 512])
                eng.tensor_add(out=DE["outh"][:, off + 512:off + D],
                               in0=DE["P"][ob + 1][:, :],
                               in1=DE["stream"][:, off + 512:off + D]
                               ).then_inc(sems.d_ev, 1)

    A1 = mkphase(lambda a, p: dict(
        wqb=a("A_wqb", [128, 8 * D], BF16),
        wkb=a("A_wkb", [128, 8 * D], BF16),
        wvb=a("A_wvb", [128, 8 * 2048], BF16),
        wgb=a("A_wgb", [128, 8 * 2048], BF16),
        lnwbc=a("A_lnwbc", [128, D], F32),
        cs=a("A_cs", [128, 2 * 2 * NTA], F32),
        brow=a("A_brow", [1, 2048], F32),
        bqcol=a("A_bqcol", [128, 8], F32),
        bkcol=a("A_bkcol", [128, 8], F32),
        bvrow=a("A_bvrow", [1, 2048], BF16),
        bgrow=a("A_bgrow", [1, 2048], BF16),
        x=a("A_x", [128, 2 * 2 * D], F32),
        xn=a("A_xn", [128, 2 * D], F32),
        ntmp=a("A_ntmp", [128, D], F32),
        nrm=a("A_nrm", [128, 2 * D], BF16),
        nfm=a("A_nfm", [128, 8 * NTA], BF16),
        qf=a("A_qf", [128, 2 * NTA], F32),
        kf=a("A_kf", [128, 2 * NTA], F32),
        rt=a("A_rt", [128, 4 * NTA], F32),
        qp=a("A_qp", [128, 8 * NTA], BF16),
        kp=a("A_kp", [128, 8 * NTA], BF16),
        v=a("A_v", [128, 2 * 2048], BF16),
        sg=a("A_sg", [128, 2 * 2048], BF16),
        P=[p(f"A_P{i}", [128, 512], F32) for i in range(6)],
        TPB=[p(f"A_TPB{i}", [128, 1024], BF16) for i in range(2)],
    ))

    def a1_sync(eng):
        eng.dma_start(out=A1["lnwbc"][:, :], in_=lnwbc_d).then_inc(sems.a_cst, 16)
        for i in range(NTILESA):
            if i >= 2:
                eng.wait_ge(sems.a_xn, i - 1)
            eng.dma_start(
                out=A1["x"][:, (i % 2) * 2 * D:(i % 2) * 2 * D + 2 * D]
                    .rearrange("p (s d) -> p s d", s=2),
                in_=x_d[i * NTA:(i + 1) * NTA, :].rearrange("(s p) d -> p s d", p=128),
            ).then_inc([sems.a_x0, sems.a_x1][i % 2], 16)
            if i >= 2:
                eng.wait_ge(sems.a_rope, 8 * (i - 1))
            co = (i % 2) * 2 * NTA
            eng.dma_start(out=A1["cs"][:, co:co + NTA],
                          in_=cos_d[:, i * NTA:(i + 1) * NTA]).then_inc(
                              [sems.a_cs0, sems.a_cs1][i % 2], 16)
            eng.dma_start(out=A1["cs"][:, co + NTA:co + 2 * NTA],
                          in_=sin_d[:, i * NTA:(i + 1) * NTA]).then_inc(
                              [sems.a_cs0, sems.a_cs1][i % 2], 16)

    def a1_pool(eng):
        eng.dma_start(out=A1["wqb"][:, :].rearrange("p (kc n) -> p kc n", kc=8),
                      in_=wq_d.rearrange("(kc p) n -> p kc n", p=128)
                      ).then_inc(sems.a_wstf, 16)
        eng.dma_start(out=A1["wkb"][:, :].rearrange("p (kc n) -> p kc n", kc=8),
                      in_=wk_d.rearrange("(kc p) n -> p kc n", p=128)
                      ).then_inc(sems.a_wstf, 16)
        eng.dma_start(out=A1["wvb"][:, :].rearrange("p (kc n) -> p kc n", kc=8),
                      in_=wv_d.rearrange("(kc p) n -> p kc n", p=128)
                      ).then_inc(sems.a_wstf, 16)
        eng.dma_start(out=A1["wgb"][:, :].rearrange("p (kc n) -> p kc n", kc=8),
                      in_=wg_d.rearrange("(kc p) n -> p kc n", p=128)
                      ).then_inc(sems.a_wstf, 16)
        for i in range(NTILESA):
            eng.wait_ge(sems.a_rope, 8 * (i + 1))
            eng.dma_start(
                out=qp_d[:, :, i * NTA:(i + 1) * NTA].rearrange("kc p t -> p kc t"),
                in_=A1["qp"][:, :].rearrange("p (kc t) -> p kc t", kc=8),
            ).then_inc(sems.a_oq, 16)
            eng.dma_start(
                out=kp_d[:, :, i * NTA:(i + 1) * NTA].rearrange("kc p t -> p kc t"),
                in_=A1["kp"][:, :].rearrange("p (kc t) -> p kc t", kc=8),
            ).then_inc(sems.a_oq, 16)
            eng.wait_ge(sems.a_vge, 16 * (i + 1))
            eng.dma_start(
                out=vtm_d[i * NTA:(i + 1) * NTA, :].rearrange("(s p) n -> p s n", p=128),
                in_=A1["v"][:, :].rearrange("p (s n) -> p s n", s=2),
            ).then_inc(sems.a_ov, 16)
            eng.dma_start(
                out=sgtm_d[i * NTA:(i + 1) * NTA, :].rearrange("(s p) n -> p s n", p=128),
                in_=A1["sg"][:, :].rearrange("p (s n) -> p s n", s=2),
            ).then_inc(sems.a_ov, 16)
        eng.wait_ge(sems.a_oq, 16 * 2 * NTILESA)
        eng.wait_ge(sems.a_ov, 16 * 2 * NTILESA)

    def a1_tensor(eng):
        rows = ([("q", A1["wqb"], 1024, hx) for hx in range(2)]
                + [("k", A1["wkb"], 1024, hx) for hx in range(2)]
                + [("v", A1["wvb"], 2048, hx) for hx in range(4)]
                + [("g", A1["wgb"], 2048, hx) for hx in range(4)])
        for r, (kind, wb, wid, hx) in enumerate(rows):
            need = {"q": 16, "k": 32, "v": 48, "g": 64}[kind]
            eng.wait_ge(sems.a_wstf, need)
            if r >= 2:
                eng.wait_ge(sems.a_pree, r - 1)
            bank = A1["P"][5 if r % 2 == 0 else 4]
            for kc in range(8):
                a = mm(eng, bank[0:1, 0:512],
                       lnbcol_b[:, kc:kc + 1],
                       wb[:, kc * wid + hx * 512: kc * wid + (hx + 1) * 512],
                       start=(kc == 0), stop=(kc == 7))
            a.then_inc(sems.a_pre, 1)
        for j in range(16):
            eng.wait_ge(sems.a_pree, 12 + j)
            tp(eng, A1["P"][3][:, 0:1], A1["brow"][0:1, j * 128:(j + 1) * 128],
               identf[0:1, 0:1], start=True, stop=True).then_inc(sems.a_pre, 1)
        eng.wait_ge(sems.a_pree, 28)
        nfmv = A1["nfm"][:, :].rearrange("p (kc t) -> p kc t", kc=8)
        for i in range(NTILESA):
            for st in range(2):
                gt = 2 * i + st
                eng.wait_ge(sems.a_nrm, gt + 1)
                if gt >= 2:
                    eng.wait_ge(sems.a_nfm, gt - 1)
                for kc in range(8):
                    a = tp(eng, A1["TPB"][gt % 2][:, kc * 128:(kc + 1) * 128],
                           A1["nrm"][:, st * D + kc * 128: st * D + (kc + 1) * 128],
                           identb[:, :], start=(kc == 0), stop=(kc == 7))
                a.then_inc(sems.a_tp, 1)
            eng.wait_ge(sems.a_nfm, 2 * (i + 1))
            for hp in range(4):
                for qk in range(2):
                    wb = A1["wqb"] if qk == 0 else A1["wkb"]
                    w = 8 * i + (hp - 1) * 2 + qk + 1
                    if w > 0:
                        eng.wait_ge(sems.a_qke, w)
                    for half in range(2):
                        dq = 2 * hp + half
                        bank = A1["P"][qk * 2 + half]
                        for kc in range(8):
                            a = mm(eng, bank[:, 0:NTA],
                                   wb[:, kc * D + dq * 128: kc * D + (dq + 1) * 128],
                                   nfmv[:, kc, :], start=(kc == 0), stop=(kc == 7))
                    a.then_inc(sems.a_qk, 1)
            for st in range(2):
                for vg in range(2):
                    wb = A1["wvb"] if vg == 0 else A1["wgb"]
                    br = A1["bvrow"] if vg == 0 else A1["bgrow"]
                    for ng in range(4):
                        gv = 16 * i + st * 8 + vg * 4 + ng
                        if gv >= 1:
                            eng.wait_ge(sems.a_vge, gv)
                        bank = A1["P"][4 + vg]
                        for kc in range(8):
                            mm(eng, bank[:, :],
                               nfmv[:, kc, st * 128:(st + 1) * 128],
                               wb[:, kc * 2048 + ng * 512: kc * 2048 + (ng + 1) * 512],
                               start=(kc == 0), stop=False)
                        mm(eng, bank[:, :], onesb[0:1, 0:128],
                           br[0:1, ng * 512:(ng + 1) * 512],
                           start=False, stop=True).then_inc(sems.a_vg, 1)

    def a1_act(eng):
        rows = (["q0", "q1", "k0", "k1"] + [f"v{j}" for j in range(4)]
                + [f"g{j}" for j in range(4)])
        for r, tag in enumerate(rows):
            eng.wait_ge(sems.a_pre, r + 1)
            bank = A1["P"][5 if r % 2 == 0 else 4]
            kind, hx = tag[0], int(tag[1])
            if kind == "q":
                dst = A1["brow"][0:1, hx * 512:(hx + 1) * 512]
            elif kind == "k":
                dst = A1["brow"][0:1, 1024 + hx * 512: 1024 + (hx + 1) * 512]
            elif kind == "v":
                dst = A1["bvrow"][0:1, hx * 512:(hx + 1) * 512]
            else:
                dst = A1["bgrow"][0:1, hx * 512:(hx + 1) * 512]
            eng.activation(out=dst, in_=bank[0:1, 0:512],
                           func=AF.Identity).then_inc(sems.a_pree, 1)
        for j in range(16):
            eng.wait_ge(sems.a_pre, 12 + j + 1)
            dst = A1["bqcol"][:, j:j + 1] if j < 8 else A1["bkcol"][:, j - 8:j - 7]
            eng.activation(out=dst, in_=A1["P"][3][:, 0:1],
                           func=AF.Identity).then_inc(sems.a_pree, 1)
        for i in range(NTILESA):
            for st in range(2):
                gt = 2 * i + st
                eng.wait_ge(sems.a_tp, gt + 1)
                eng.activation(out=A1["nfm"][:, :]
                               .rearrange("p (kc t) -> p kc t", kc=8)
                               [:, :, st * 128:(st + 1) * 128],
                               in_=A1["TPB"][gt % 2][:, :]
                               .rearrange("p (kc t) -> p kc t", kc=8),
                               func=AF.Identity).then_inc(sems.a_nfm, 1)
            for st in range(2):
                for vg in range(2):
                    for ng in range(4):
                        gv = 16 * i + st * 8 + vg * 4 + ng
                        eng.wait_ge(sems.a_vg, gv + 1)
                        if i > 0 and st == 0 and vg == 0 and ng == 0:
                            eng.wait_ge(sems.a_ov, 16 * 2 * i)
                        dstbuf = A1["v"] if vg == 0 else A1["sg"]
                        dst = dstbuf[:, st * 2048 + ng * 512: st * 2048 + (ng + 1) * 512]
                        eng.activation(out=dst, in_=A1["P"][4 + vg][:, :],
                                       func=AF.Identity if vg == 0 else AF.Silu
                                       ).then_inc(sems.a_vge, 1)

    def a1_vector(eng):
        for i in range(NTILESA):
            eng.wait_ge([sems.a_x0, sems.a_x1][i % 2], 16 * (i // 2 + 1))
            eng.wait_ge(sems.a_cst, 16)
            xoff = (i % 2) * 2 * D
            for st in range(2):
                c = 2 * i + st
                a = eng.tensor_scalar(out=A1["xn"][:, st * D:(st + 1) * D],
                                      in0=A1["x"][:, xoff + st * D: xoff + (st + 1) * D],
                                      scalar1=negm1[:, c:c + 1], scalar2=rstd1[:, c:c + 1],
                                      op0=ALU.add, op1=ALU.mult)
            a.then_inc(sems.a_xn, 1)
            for st in range(2):
                c = 2 * i + st
                eng.tensor_scalar(out=A1["ntmp"][:, :],
                                  in0=A1["xn"][:, st * D:(st + 1) * D],
                                  scalar1=negm2[:, c:c + 1], scalar2=rstd2[:, c:c + 1],
                                  op0=ALU.add, op1=ALU.mult)
                eng.tensor_mul(out=A1["nrm"][:, st * D:(st + 1) * D],
                               in0=A1["ntmp"][:, :],
                               in1=A1["lnwbc"][:, :]).then_inc(sems.a_nrm, 1)
            eng.wait_ge([sems.a_cs0, sems.a_cs1][i % 2], 32 * (i // 2 + 1))
            co = (i % 2) * 2 * NTA
            coss = A1["cs"][:, co:co + NTA]
            sins = A1["cs"][:, co + NTA:co + 2 * NTA]
            for hp in range(4):
                for qk in range(2):
                    gq = 8 * i + hp * 2 + qk
                    eng.wait_ge(sems.a_qk, gq + 1)
                    if i > 0 and hp == 0 and qk == 0:
                        eng.wait_ge(sems.a_oq, 16 * 2 * i)
                    bcol = A1["bqcol"] if qk == 0 else A1["bkcol"]
                    fbuf = A1["qf"] if qk == 0 else A1["kf"]
                    obuf = A1["qp"] if qk == 0 else A1["kp"]
                    pA = A1["P"][qk * 2][:, 0:NTA]
                    pB = A1["P"][qk * 2 + 1][:, 0:NTA]
                    eng.tensor_scalar(out=fbuf[:, 0:NTA], in0=pA,
                                      scalar1=bcol[:, 2 * hp:2 * hp + 1], scalar2=None,
                                      op0=ALU.add)
                    a = eng.tensor_scalar(out=fbuf[:, NTA:2 * NTA], in0=pB,
                                          scalar1=bcol[:, 2 * hp + 1:2 * hp + 2],
                                          scalar2=None, op0=ALU.add)
                    a.then_inc(sems.a_qke, 1)
                    fA = fbuf[:, 0:NTA]
                    fB = fbuf[:, NTA:2 * NTA]
                    rt = A1["rt"]
                    if qk == 0:
                        sc_ = DK ** -0.5
                        eng.scalar_tensor_tensor(out=rt[:, 0:NTA], in0=fA, scalar=sc_,
                                                 in1=coss, op0=ALU.mult, op1=ALU.mult)
                        eng.scalar_tensor_tensor(out=rt[:, NTA:2 * NTA], in0=fB, scalar=sc_,
                                                 in1=sins, op0=ALU.mult, op1=ALU.mult)
                        eng.scalar_tensor_tensor(out=rt[:, 2 * NTA:3 * NTA], in0=fB, scalar=sc_,
                                                 in1=coss, op0=ALU.mult, op1=ALU.mult)
                        eng.scalar_tensor_tensor(out=rt[:, 3 * NTA:4 * NTA], in0=fA, scalar=sc_,
                                                 in1=sins, op0=ALU.mult, op1=ALU.mult)
                    else:
                        eng.tensor_mul(out=rt[:, 0:NTA], in0=fA, in1=coss)
                        eng.tensor_mul(out=rt[:, NTA:2 * NTA], in0=fB, in1=sins)
                        eng.tensor_mul(out=rt[:, 2 * NTA:3 * NTA], in0=fB, in1=coss)
                        eng.tensor_mul(out=rt[:, 3 * NTA:4 * NTA], in0=fA, in1=sins)
                    eng.tensor_sub(out=obuf[:, (2 * hp) * NTA:(2 * hp + 1) * NTA],
                                   in0=rt[:, 0:NTA], in1=rt[:, NTA:2 * NTA])
                    eng.tensor_add(out=obuf[:, (2 * hp + 1) * NTA:(2 * hp + 2) * NTA],
                                   in0=rt[:, 2 * NTA:3 * NTA],
                                   in1=rt[:, 3 * NTA:4 * NTA]).then_inc(sems.a_rope, 1)

    def a1_endstate():
        return [(sems.a_vg, 16 * NTILESA), (sems.a_rope, 8 * NTILESA),
                (sems.a_vge, 16 * NTILESA), (sems.a_wstf, 64),
                (sems.a_oq, 16 * 2 * NTILESA), (sems.a_ov, 16 * 2 * NTILESA),
                (sems.a_x0, 16 * NTILESA // 2), (sems.a_x1, 16 * NTILESA // 2),
                (sems.a_cs0, 16 * NTILESA), (sems.a_cs1, 16 * NTILESA),
                (sems.a_cst, 16), (sems.a_nfm, 2 * NTILESA),
                (sems.a_pree, 28), (sems.a_qke, 8 * NTILESA),
                (sems.a_xn, NTILESA), (sems.a_tp, 2 * NTILESA),
                (sems.a_qk, 8 * NTILESA), (sems.a_nrm, 2 * NTILESA)]

    def b_sync(eng):
        eng.dma_start(out=B["gnw"][:, :], in_=gnwcol_d).then_inc(sems.b_cst, 16)
        eng.dma_start(out=B["dmask"][:, :].rearrange("p (h i) -> p h i", h=H),
                      in_=dmaskt_d.rearrange("h p i -> p h i")).then_inc(sems.b_cst, 16)
        eng.dma_start(out=B["cross"][:, :].rearrange("p (h i) -> p h i", h=H),
                      in_=crossb_d.rearrange("h p i -> p h i")).then_inc(sems.b_cst, 16)
        eng.dma_start(out=B["kdec"][:, :], in_=kdeccol_d).then_inc(sems.b_cst, 16)
        for n in range(NCH):
            sl = n % 2
            if n >= 2:
                eng.wait_ge(sems.b_stt, 4 * (n - 1))
            eng.dma_start(
                out=B["q"][:, sl * 1024:(sl + 1) * 1024]
                    .rearrange("p (kc t) -> p kc t", kc=8),
                in_=qp_d[:, :, n * CB:(n + 1) * CB].rearrange("kc p t -> p kc t"),
            ).then_inc([sems.b_iq0, sems.b_iq1][sl], 16)
            eng.dma_start(
                out=B["k"][:, sl * 1024:(sl + 1) * 1024]
                    .rearrange("p (kc t) -> p kc t", kc=8),
                in_=kp_d[:, :, n * CB:(n + 1) * CB].rearrange("kc p t -> p kc t"),
            ).then_inc([sems.b_ik0, sems.b_ik1][sl], 16)
            eng.dma_start(
                out=B["v"][:, sl * 2048:(sl + 1) * 2048],
                in_=vtm_d[n * CB:(n + 1) * CB, :],
            ).then_inc([sems.b_iv0, sems.b_iv1][sl], 16)
            if n >= 2:
                eng.wait_ge(sems.b_og, 4 * (n - 1))
            eng.dma_start(
                out=B["sgs"][:, sl * 2048:(sl + 1) * 2048],
                in_=sgtm_d[n * CB:(n + 1) * CB, :],
            ).then_inc([sems.b_isg0, sems.b_isg1][sl], 16)
            if n >= 2:
                eng.wait_ge(sems.b_comb, n - 1)
            eng.dma_start(
                out=B["xs"][:, sl * D:(sl + 1) * D],
                in_=x_d[n * CB:(n + 1) * CB, :],
            ).then_inc([sems.b_ix0, sems.b_ix1][sl], 16)
            eng.dma_start(
                out=B["gfs"][:, sl * D:(sl + 1) * D],
                in_=gf_d[n * CB:(n + 1) * CB, :],
            ).then_inc([sems.b_igf0, sems.b_igf1][sl], 16)

    def b_pool(eng):
        eng.dma_start(out=B["wraw"][:, :].rearrange("p (kc n) -> p kc n", kc=16),
                      in_=wo_d.rearrange("(kc p) n -> p kc n", p=128)
                      ).then_inc(sems.b_wstf, 16)
        eng.memset(B["Sf"][:, :], 0.0)
        eng.memset(B["Sb"][:, :], 0.0).then_inc(sems.b_sbf, 4)
        for n in range(NCH):
            sl = n % 2
            eng.wait_ge(sems.b_woev, n + 1)
            eng.wait_ge([sems.b_ix0, sems.b_ix1][sl], 16 * (n // 2 + 1))
            if n >= 2:
                eng.wait_ge([sems.b_oo0, sems.b_oo1][sl], 16 * (n // 2))
            eng.tensor_add(out=B["outt"][:, sl * D:(sl + 1) * D],
                           in0=B["t1"][:, :],
                           in1=B["xs"][:, sl * D:(sl + 1) * D]).then_inc(sems.b_comb, 1)
            eng.wait_ge(sems.b_comb, n + 1)
            eng.dma_start(out=out_d[n * CB:(n + 1) * CB, :],
                          in_=B["outt"][:, sl * D:(sl + 1) * D]).then_inc(
                              [sems.b_oo0, sems.b_oo1][sl], 16)
        eng.wait_ge(sems.b_oo0, 16 * NCH // 2)
        eng.wait_ge(sems.b_oo1, 16 * NCH // 2)

    def b_tensor(eng):
        for n in range(NCH):
            sl = n % 2
            eng.wait_ge([sems.b_iq0, sems.b_iq1][sl], 16 * (n // 2 + 1))
            eng.wait_ge([sems.b_ik0, sems.b_ik1][sl], 16 * (n // 2 + 1))
            for h in range(H):
                gk = 4 * n + h
                if h == 0 and n >= 1:
                    eng.wait_ge(sems.b_stt, 4 * n)
                if h >= 2:
                    eng.wait_ge(sems.b_mask, 4 * n + h - 1)
                for c in range(2):
                    a = mm(eng, B["P"][2 + h % 2][:, 0:CB],
                           B["k"][:, sl * 1024 + (2 * h + c) * CB: sl * 1024 + (2 * h + c + 1) * CB],
                           B["q"][:, sl * 1024 + (2 * h + c) * CB: sl * 1024 + (2 * h + c + 1) * CB],
                           start=(c == 0), stop=(c == 1))
                a.then_inc(sems.b_pe, 1)
            for h in range(H):
                gk = 4 * n + h
                ktb = B["KTB"] if h % 2 == 0 else B["OGB"]
                if gk >= 2:
                    eng.wait_ge(sems.b_k2, gk - 1)
                if h == 1:
                    eng.wait_ge(sems.b_oge, 2 * n)
                for c in range(2):
                    a = tp(eng, ktb[:, c * 128:(c + 1) * 128],
                           B["k"][:, sl * 1024 + (2 * h + c) * CB: sl * 1024 + (2 * h + c + 1) * CB],
                           identb[:, :], start=(c == 0), stop=(c == 1))
                a.then_inc(sems.b_pe, 1)
            eng.wait_ge([sems.b_iv0, sems.b_iv1][sl], 16 * (n // 2 + 1))
            for h in range(H):
                gk = 4 * n + h
                eng.wait_ge(sems.b_mask, gk + 1)
                eng.wait_ge(sems.b_q2, gk + 1)
                eng.wait_ge(sems.b_sbf, 4 * n + h + 1)
                if gk >= 2:
                    eng.wait_ge(sems.b_on, gk - 1)
                mm(eng, B["P"][h % 2][:, :], B["at"][:, h * CB:(h + 1) * CB],
                   B["v"][:, sl * 2048 + h * DV: sl * 2048 + (h + 1) * DV],
                   start=True, stop=False)
                for c in range(2):
                    mm(eng, B["P"][h % 2][:, :],
                       B["q2"][:, (2 * h + c) * CB:(2 * h + c + 1) * CB],
                       B["Sb"][:, h * 1024 + c * 512: h * 1024 + (c + 1) * 512],
                       start=False, stop=(c == 1))
                if h == 0:
                    eng.wait_ge(sems.b_mask, 4 * n + 4)
                if gk >= 1:
                    eng.wait_ge(sems.b_stt, gk)
                eng.wait_ge(sems.b_k2, gk + 1)
                for c in range(2):
                    a = mm(eng, B["P"][2 + c][:, :],
                           B["k2"][:, h * DK + c * CB: h * DK + (c + 1) * CB],
                           B["v"][:, sl * 2048 + h * DV: sl * 2048 + (h + 1) * DV],
                           start=True, stop=True)
                a.then_inc(sems.b_pe, 1)
            for w in range(2):
                eng.wait_ge(sems.b_og, 4 * n + 2 * (w + 1))
                if w == 0:
                    eng.wait_ge(sems.b_k2, 4 * n + 4)
                if 2 * n + w >= 1:
                    eng.wait_ge(sems.b_oge, 2 * n + w)
                for j in range(8):
                    fc = w * 8 + j
                    a = tp(eng, B["OGB"][:, j * 128:(j + 1) * 128],
                           B["og"][:, fc * 128:(fc + 1) * 128],
                           identb[:, :], start=(j == 0), stop=(j == 7))
                a.then_inc(sems.b_ogt, 1)
            eng.wait_ge(sems.b_oge, 2 * n + 2)
            if n == 0:
                eng.wait_ge(sems.b_wsc, 16)
            else:
                eng.wait_ge(sems.b_woev, n)
            for ng in range(2):
                for fc in range(16):
                    a = mm(eng, B["P"][4 + ng][:, :],
                           B["ogfm"][:, fc * 128:(fc + 1) * 128],
                           B["wob"][:, fc * D + ng * 512: fc * D + ng * 512 + 512],
                           start=(fc == 0), stop=(fc == 15))
            a.then_inc(sems.b_wo, 1)

    def b_act(eng):
        eng.wait_ge(sems.b_cst, 64)
        for n in range(NCH):
            for h in range(H):
                gk = 4 * n + h
                eng.wait_ge(sems.b_pe, 12 * n + 4 + h + 1)
                ktb = B["KTB"] if h % 2 == 0 else B["OGB"]
                eng.activation(out=B["k2"][:, h * DK:(h + 1) * DK],
                               in_=ktb[:, 0:256],
                               func=AF.Identity,
                               scale=B["kdec"][:, h:h + 1]).then_inc(sems.b_k2, 1)
            for h in range(H):
                gk = 4 * n + h
                eng.wait_ge(sems.b_pe, 12 * n + 8 + h + 1)
                eng.activation(out=B["sq"][:, :], in_=B["P"][h % 2][:, :],
                               func=AF.Square,
                               accum_out=B["cols"][:, h:h + 1]).then_inc(sems.b_sq, 1)
                eng.wait_ge(sems.b_rc1, gk + 1)
                eng.activation(out=B["cols"][:, 8 + h:9 + h], in_=B["cols"][:, 4 + h:5 + h],
                               func=AF.Sqrt).then_inc(sems.b_rc2, 1)
                eng.wait_ge(sems.b_rc3, gk + 1)
                if gk >= 1:
                    eng.wait_ge(sems.b_og, gk)
                eng.activation(out=B["onf"][:, :], in_=B["P"][h % 2][:, :],
                               func=AF.Identity,
                               scale=B["cols"][:, 12 + h:13 + h]).then_inc(sems.b_on, 1)
            for h in range(H):
                gk = 4 * n + h
                eng.wait_ge(sems.b_stt, gk + 1)
                eng.activation(out=B["Sb"][:, h * 1024:(h + 1) * 1024],
                               in_=B["Sf"][:, (n % 2) * 4096 + h * 1024:
                                           (n % 2) * 4096 + (h + 1) * 1024],
                               func=AF.Identity).then_inc(sems.b_sbf, 1)
            for w in range(2):
                eng.wait_ge(sems.b_ogt, 2 * n + w + 1)
                if n >= 1 and w == 0:
                    eng.wait_ge(sems.b_wo, n)
                eng.activation(out=B["ogfm"][:, w * 1024:(w + 1) * 1024],
                               in_=B["OGB"][:, :],
                               func=AF.Identity).then_inc(sems.b_oge, 1)

    def b_vector(eng):
        eng.wait_ge(sems.b_cst, 64)
        eng.wait_ge(sems.b_wstf, 16)
        for kc in range(16):
            eng.tensor_scalar(out=B["wob"][:, kc * D:(kc + 1) * D],
                              in0=B["wraw"][:, kc * D:(kc + 1) * D],
                              scalar1=B["gnw"][:, kc % 4:kc % 4 + 1], scalar2=None,
                              op0=ALU.mult).then_inc(sems.b_wsc, 1)
        for n in range(NCH):
            sl = n % 2
            eng.wait_ge([sems.b_ix0, sems.b_ix1][sl], 16 * (n // 2 + 1))
            eng.tensor_scalar(out=B["xn"][:, :], in0=B["xs"][:, sl * D:(sl + 1) * D],
                              scalar1=negm1[:, n:n + 1], scalar2=rstd1[:, n:n + 1],
                              op0=ALU.add, op1=ALU.mult)
            for h in range(H):
                gk = 4 * n + h
                eng.wait_ge(sems.b_pe, 12 * n + h + 1)
                eng.tensor_mul(out=B["at"][:, h * CB:(h + 1) * CB],
                               in0=B["P"][2 + h % 2][:, 0:CB],
                               in1=B["dmask"][:, h * CB:(h + 1) * CB]
                               ).then_inc(sems.b_mask, 1)
            eng.wait_ge([sems.b_iq0, sems.b_iq1][sl], 16 * (n // 2 + 1))
            for h in range(H):
                for c in range(2):
                    a = eng.tensor_mul(
                        out=B["q2"][:, (2 * h + c) * CB:(2 * h + c + 1) * CB],
                        in0=B["q"][:, sl * 1024 + (2 * h + c) * CB: sl * 1024 + (2 * h + c + 1) * CB],
                        in1=B["cross"][:, h * CB:(h + 1) * CB])
                a.then_inc(sems.b_q2, 1)
            for h in range(H):
                gk = 4 * n + h
                eng.wait_ge(sems.b_sq, gk + 1)
                eng.tensor_scalar(out=B["cols"][:, 4 + h:5 + h], in0=B["cols"][:, h:h + 1],
                                  scalar1=1.0 / DV, scalar2=EPS,
                                  op0=ALU.mult, op1=ALU.add).then_inc(sems.b_rc1, 1)
                eng.wait_ge(sems.b_rc2, gk + 1)
                eng.reciprocal(out=B["cols"][:, 12 + h:13 + h],
                               in_=B["cols"][:, 8 + h:9 + h]).then_inc(sems.b_rc3, 1)
                eng.wait_ge(sems.b_pe, 12 * n + 8 + h + 1)
                if n == 0:
                    eng.wait_ge(sems.b_sbf, 4)
                for c in range(2):
                    a = eng.scalar_tensor_tensor(
                        out=B["Sf"][:, (n % 2) * 4096 + h * 1024 + c * 512:
                                    (n % 2) * 4096 + h * 1024 + (c + 1) * 512],
                        in0=B["Sf"][:, ((n + 1) % 2) * 4096 + h * 1024 + c * 512:
                                    ((n + 1) % 2) * 4096 + h * 1024 + (c + 1) * 512],
                        scalar=GC[h], in1=B["P"][2 + c][:, :],
                        op0=ALU.mult, op1=ALU.add)
                a.then_inc(sems.b_stt, 1)
                eng.wait_ge(sems.b_on, gk + 1)
                eng.wait_ge([sems.b_isg0, sems.b_isg1][sl], 16 * (n // 2 + 1))
                if n >= 1:
                    eng.wait_ge(sems.b_ogt, 2 * n - 1 if h < 2 else 2 * n)
                eng.tensor_mul(out=B["og"][:, h * DV:(h + 1) * DV],
                               in0=B["onf"][:, :],
                               in1=B["sgs"][:, sl * 2048 + h * DV: sl * 2048 + (h + 1) * DV]
                               ).then_inc(sems.b_og, 1)
            eng.wait_ge(sems.b_wo, n + 1)
            if n >= 1:
                eng.wait_ge(sems.b_comb, n)
            eng.tensor_add(out=B["so"][:, 0:512], in0=B["P"][4][:, :],
                           in1=B["xn"][:, 0:512])
            eng.tensor_add(out=B["so"][:, 512:D], in0=B["P"][5][:, :],
                           in1=B["xn"][:, 512:D])
            eng.wait_ge([sems.b_igf0, sems.b_igf1][sl], 16 * (n // 2 + 1))
            eng.tensor_mul(out=B["t1"][:, :], in0=B["so"][:, :],
                           in1=B["gfs"][:, sl * D:(sl + 1) * D]).then_inc(sems.b_woev, 1)

    def c_endstate():
        nxn = 16 * NTILES + 16
        if "dispdbg" in dbg:
            nxn += 16
        if "statsdbg" in dbg:
            nxn += 64
        return [(sems.c_dtpe, 4 * NTILES), (sems.c_lge, 8 * NTILES),
                (sems.c_dtp, 4 * NTILES), (sems.s_wstgf, 80),
                (sems.s_oxn, nxn),
                (sems.s_osh0, 16 * 2 * NTILES), (sems.s_osh1, 16 * 2 * NTILES),
                (sems.s_cst, 64), (sems.s_in, 16 * NTILES)]

    def de_endstate(pi):
        t = DE_TOT
        out = [(sems.d_acc, pi * t["acc"]), (sems.d_ev, pi * t["ev"]),
               (sems.d_silu, pi * t["silu"]), (sems.d_wstf, pi * t["wstf"]),
               (sems.d_out0, pi * t["out"]), (sems.d_out1, pi * t["out"]),
               (sems.d_fm, pi * t["fm"]),
               (sems.d_dp0, pi * t["dp"]), (sems.d_dp1, pi * t["dp"]),
               (sems.d_st0, pi * t["st"]), (sems.d_st1, pi * t["st"]),
               (sems.d_hsc, pi * t["hsc"]), (sems.d_bc, pi * t["bc"]),
               (sems.d_h, pi * t["h"])]
        if pi >= 2:
            out.append((sems.d_gf, t["gf"]))
        return out

    def wait_list(eng, lst):
        for s, v in lst:
            eng.wait_ge(s, v)

    with nc.Block() as block:
        @block.sync
        def _(eng):
            if "C" in phases:
                c_sync(eng)
            if "D" in phases:
                wait_list(eng, c_endstate())
                de_sync(eng, de_bases(0), 0, sh_d)
            if "Ep" in phases:
                wait_list(eng, de_endstate(1))
                de_sync(eng, de_bases(1), 8, moeps_d)
            if "A1" in phases:
                wait_list(eng, de_endstate(2))
                a1_sync(eng)
            if "B" in phases:
                wait_list(eng, a1_endstate())
                b_sync(eng)

        @block.tensor
        def _(eng):
            if "C" in phases:
                c_tensor(eng)
            if "D" in phases:
                wait_list(eng, c_endstate())
                de_tensor(eng, de_bases(0), False)
            if "Ep" in phases:
                wait_list(eng, de_endstate(1))
                de_tensor(eng, de_bases(1), True)
            if "A1" in phases:
                wait_list(eng, de_endstate(2))
                a1_tensor(eng)
            if "B" in phases:
                wait_list(eng, a1_endstate())
                b_tensor(eng)

        @block.vector
        def _(eng):
            if "C" in phases:
                c_vector(eng)
            if "D" in phases:
                wait_list(eng, c_endstate())
                de_vector(eng, de_bases(0), False)
            if "Ep" in phases:
                wait_list(eng, de_endstate(1))
                de_vector(eng, de_bases(1), True)
            if "A1" in phases:
                wait_list(eng, de_endstate(2))
                a1_vector(eng)
            if "B" in phases:
                wait_list(eng, a1_endstate())
                b_vector(eng)

        @block.scalar
        def _(eng):
            if "C" in phases:
                c_act(eng)
            if "D" in phases:
                wait_list(eng, c_endstate())
                de_act(eng, de_bases(0), False)
            if "Ep" in phases:
                wait_list(eng, de_endstate(1))
                de_act(eng, de_bases(1), True)
            if "A1" in phases:
                wait_list(eng, de_endstate(2))
                a1_act(eng)
            if "B" in phases:
                wait_list(eng, a1_endstate())
                b_act(eng)

        @block.gpsimd
        def _(eng):
            if "C" in phases:
                c_pool(eng)
                c_gpsimd_out(eng)
            if "D" in phases:
                wait_list(eng, c_endstate())
                de_pool(eng, de_bases(0), False)
            if "Ep" in phases:
                wait_list(eng, de_endstate(1))
                de_pool(eng, de_bases(1), True)
            if "A1" in phases:
                wait_list(eng, de_endstate(2))
                a1_pool(eng)
            if "B" in phases:
                wait_list(eng, a1_endstate())
                b_pool(eng)

    es.close()
    return nc


_consts = None


def _in_maps(inputs):
    global _consts
    if _consts is None:
        _consts = _host_constants()
    lay = _host_layouts(inputs)
    x = np.ascontiguousarray(np.asarray(inputs["L0_attn_x"], dtype=np.float32))
    base = dict(lay)
    base.update(_consts)
    maps = []
    for c in range(8):
        m = dict(base)
        m["x"] = np.ascontiguousarray(x[c])
        maps.append(m)
    return maps


def kernel(**inputs):
    nc = build_program()
    maps = _in_maps(inputs)
    res = run_bass_kernel_spmd(nc, maps, core_ids=list(range(8)))
    return np.stack([res.results[c]["out"] for c in range(8)], axis=0)


# version 05
# speedup vs baseline: 1.5223x; 1.0485x over previous
import numpy as np
import ml_dtypes
import sys

sys.path.insert(0, "/opt/trn_rl_repo")
import concourse.bass as bass
import concourse.mybir as mybir
from concourse.bass_utils import run_bass_kernel_spmd
from contextlib import ExitStack

F32 = mybir.dt.float32
BF16 = mybir.dt.bfloat16
AF = mybir.ActivationFunctionType
ALU = mybir.AluOpType
AX = mybir.AxisListType

T = 4096
D = 1024
H = 4
DK = 256
DV = 512
E = 16
HID = 512
S2 = 2048
CB = 128
NCH = T // CB
NT = 512
NTILES = T // NT
NTA = 256
NTILESA = T // NTA
EPS = 1e-5
GAMMA = [1.0 - 2.0 ** (-5.0 - h) for h in range(H)]
GC = [g ** CB for g in GAMMA]

PHASES = ["C", "D", "Ep", "A1", "B"]


def _host_constants():
    c = {}
    inv_freq = 1.0 / 10000.0 ** (np.arange(0, DK, 2, dtype=np.float64) / DK)
    ang = np.arange(T, dtype=np.float64)[None, :] * inv_freq[:, None]
    c["cos_t"] = np.cos(ang).astype(np.float32)
    c["sin_t"] = np.sin(ang).astype(np.float32)
    c["cosq_t"] = (np.cos(ang) * DK ** -0.5).astype(np.float32)
    c["sinq_t"] = (np.sin(ang) * DK ** -0.5).astype(np.float32)
    i = np.arange(CB, dtype=np.float64)
    dmt = np.zeros((H, CB, CB), np.float32)
    crs = np.zeros((H, CB, CB), np.float32)
    kdc = np.zeros((CB, H), np.float32)
    for h in range(H):
        g = GAMMA[h]
        diff = i[:, None] - i[None, :]
        m = np.where(diff >= 0, g ** np.maximum(diff, 0), 0.0)
        dmt[h] = m.T.astype(np.float32)
        crs[h] = np.tile((g ** (i + 1.0)).astype(np.float32)[None, :], (CB, 1))
        kdc[:, h] = (g ** (CB - 1.0 - i)).astype(np.float32)
    c["dmaskt"] = dmt
    c["crossb"] = crs
    c["kdec_col"] = kdc
    c["ident_f"] = np.eye(128, dtype=np.float32)
    c["ident_b"] = np.eye(128).astype(ml_dtypes.bfloat16)
    c["ones_b"] = np.ones((1, 128), ml_dtypes.bfloat16)
    return c


def _host_layouts(inputs):
    d = {}
    d["wq"] = np.ascontiguousarray(inputs["L0_attn_Wq"], np.float32)
    d["wk"] = np.ascontiguousarray(inputs["L0_attn_Wk"], np.float32)
    d["wv"] = np.ascontiguousarray(inputs["L0_attn_Wv"], np.float32)
    d["wg"] = np.ascontiguousarray(inputs["L0_attn_Wg"], np.float32)
    d["wo"] = np.ascontiguousarray(inputs["L0_attn_Wo"], np.float32)
    d["sfc1t"] = np.ascontiguousarray(np.asarray(inputs["L0_ffn_sfc1"]).T, np.float32)
    d["sfc2t"] = np.ascontiguousarray(np.asarray(inputs["L0_ffn_sfc2"]).T, np.float32)
    d["sfc3t"] = np.ascontiguousarray(np.asarray(inputs["L0_ffn_sfc3"]).T, np.float32)
    d["w1t"] = np.ascontiguousarray(np.asarray(inputs["L0_ffn_w1"]).transpose(0, 2, 1), np.float32)
    d["w2t"] = np.ascontiguousarray(np.asarray(inputs["L0_ffn_w2"]).transpose(0, 2, 1), np.float32)
    d["gatet"] = np.ascontiguousarray(np.asarray(inputs["L0_ffn_gate_w"]).T, np.float32)
    d["lnw_bc"] = np.ascontiguousarray(
        np.broadcast_to(np.asarray(inputs["L0_attn_ln_w"])[None, :], (128, D)), np.float32)
    d["lnb_col"] = np.ascontiguousarray(
        np.asarray(inputs["L0_attn_ln_b"]).reshape(D // 128, 128).T, np.float32)
    d["gnw_col"] = np.ascontiguousarray(
        np.asarray(inputs["L0_attn_g_norm_w"]).reshape(DV // 128, 128).T, np.float32)
    return d


def mm(eng, out, lhsT, rhs, start, stop):
    return eng.matmul(out=out, lhsT=lhsT, rhs=rhs, start=start, stop=stop)


def tp(eng, out, in_, ident, start, stop):
    return eng.matmul(out=out, lhsT=in_, rhs=ident, is_transpose=True,
                      start=start, stop=stop)


def build_program(debug_outputs=(), phases=PHASES):
    nc = bass.Bass("TRN2", target_bir_lowering=False, debug=False,
                   detect_race_conditions=False)

    def dram_in(name, shape, dt=F32):
        return nc.dram_tensor(name, list(shape), dt, kind="ExternalInput").ap()

    def scratch(name, shape, dt):
        kind = "ExternalOutput" if name in debug_outputs else "Internal"
        return nc.dram_tensor(name, list(shape), dt, kind=kind).ap()

    x_d = dram_in("x", (T, D))
    wq_d = dram_in("wq", (D, D))
    wk_d = dram_in("wk", (D, D))
    wv_d = dram_in("wv", (D, 2 * D))
    wg_d = dram_in("wg", (D, 2 * D))
    wo_d = dram_in("wo", (2 * D, D))
    sfc1t_d = dram_in("sfc1t", (D, S2))
    sfc2t_d = dram_in("sfc2t", (D, S2))
    sfc3t_d = dram_in("sfc3t", (S2, D))
    w1t_d = dram_in("w1t", (E, D, HID))
    w2t_d = dram_in("w2t", (E, HID, D))
    gatet_d = dram_in("gatet", (D, E))
    lnwbc_d = dram_in("lnw_bc", (128, D))
    lnbcol_d = dram_in("lnb_col", (128, 8))
    gnwcol_d = dram_in("gnw_col", (128, 4))
    cos_d = dram_in("cos_t", (128, T))
    sin_d = dram_in("sin_t", (128, T))
    cosq_d = dram_in("cosq_t", (128, T))
    sinq_d = dram_in("sinq_t", (128, T))
    dmaskt_d = dram_in("dmaskt", (H, CB, CB))
    crossb_d = dram_in("crossb", (H, CB, CB))
    kdeccol_d = dram_in("kdec_col", (CB, H))
    identf_d = dram_in("ident_f", (128, 128))
    identb_d = dram_in("ident_b", (128, 128), BF16)
    onesb_d = dram_in("ones_b", (1, 128), BF16)

    out_d = nc.dram_tensor("out", [T, D], F32, kind="ExternalOutput").ap()

    xnfm_d = scratch("xnfm", (8, 128, T), BF16)
    sh_d = scratch("sh", (T, D), F32)
    moeps_d = scratch("moeps", (T, D), F32)
    gf_d = scratch("gf", (T, D), BF16)
    qp_d = scratch("qp", (8, 128, T), BF16)
    kp_d = scratch("kp", (8, 128, T), BF16)
    vtm_d = scratch("vtm", (T, 2 * D), BF16)
    dispt_d = scratch("dispt", (16, T), BF16)
    sgtm_d = scratch("sgtm", (T, 2 * D), BF16)
    dbg = {}
    if "dispdbg" in debug_outputs:
        dbg["dispdbg"] = nc.dram_tensor("dispdbg", [16, T], BF16,
                                        kind="ExternalOutput").ap()
    if "statsdbg" in debug_outputs:
        dbg["statsdbg"] = nc.dram_tensor("statsdbg", [128, 4 * NCH], F32,
                                         kind="ExternalOutput").ap()

    es = ExitStack()

    identf = nc.alloc_sbuf_tensor("identf", [128, 128], F32)
    identb = nc.alloc_sbuf_tensor("identb", [128, 128], BF16)
    onesb = nc.alloc_sbuf_tensor("onesb", [1, 128], BF16)
    negm1 = nc.alloc_sbuf_tensor("negm1", [128, NCH], F32)
    rstd1 = nc.alloc_sbuf_tensor("rstd1", [128, NCH], F32)
    negm2 = nc.alloc_sbuf_tensor("negm2", [128, NCH], F32)
    rstd2 = nc.alloc_sbuf_tensor("rstd2", [128, NCH], F32)
    lnbcol = nc.alloc_sbuf_tensor("lnbcol", [128, 8], F32)
    lnbcol_b = nc.alloc_sbuf_tensor("lnbcol_b", [128, 8], BF16)

    def mkphase(fn):
        with ExitStack() as ph:
            aa = lambda n, s, dt: ph.enter_context(nc.sbuf_tensor(n, s, dt))
            pp = lambda n, s, dt: ph.enter_context(nc.psum_tensor(n, s, dt))
            return fn(aa, pp)

    C = mkphase(lambda a, p: dict(
        sfc1b=a("C_sfc1b", [128, 8 * S2], BF16),
        sfc2b=a("C_sfc2b", [128, 8 * S2], BF16),
        sfc3b=a("C_sfc3b", [128, 16 * D], BF16),
        gateb=a("C_gateb", [128, 8 * E], BF16),
        x=a("C_x", [128, 4 * D], F32),
        xn=a("C_xn", [128, 4 * D], F32),
        fm=a("C_fm", [128, 8 * NT], BF16),
        hs=a("C_hs", [128, 16 * NT], BF16),
        scr=a("C_scr", [128, 3 * D], F32),
        silu=a("C_silu", [128, 2 * NT], F32),
        cols=a("C_cols", [128, 16], F32),
        lg=a("C_lg", [128, 16], F32),
        lg2=a("C_lg2", [128, 16], F32),
        eq1=a("C_eq1", [128, 16], F32),
        eq2=a("C_eq2", [128, 16], F32),
        disp=a("C_disp", [128, 16], F32),
        dispT=a("C_dispT", [16, T], BF16),
        mcol=a("C_mcol", [128, 8], F32),
        P=[p(f"C_P{i}", [128, 512], F32) for i in range(6)],
        TPB=[p(f"C_TPB{i}", [128, 512], F32) for i in range(2)],
    ))

    DE = mkphase(lambda a, p: dict(
        w1b=a("D_w1b", [128, 8 * 8 * 512], BF16),
        w2b=a("D_w2b", [128, 8 * 4 * 1024], BF16),
        fm=a("D_fm", [128, 8 * NT], BF16),
        hsc=a("D_hsc", [128, 8 * 4 * 512], BF16),
        stream=a("D_stream", [128, 2 * D], F32),
        outh=a("D_outh", [128, 2 * D], F32),
        gfh=a("D_gfh", [128, 2 * D], BF16),
        silu=a("D_silu", [128, 2 * 512], F32),
        dprow=a("D_dprow", [1, 2 * 512], BF16),
        P=[p(f"D_P{i}", [128, 512], F32) for i in range(8)],
    ))

    B = mkphase(lambda a, p: dict(
        wob=a("B_wob", [128, 16 * D], BF16),
        wraw=a("B_wraw", [128, 16 * D], BF16),
        gnw=a("B_gnw", [128, 4], F32),
        dmask=a("B_dmask", [128, 4 * CB], F32),
        cross=a("B_cross", [128, 4 * CB], F32),
        kdec=a("B_kdec", [128, 4], F32),
        Sf=a("B_Sf", [128, 2 * H * 2 * DV], F32),
        Sb=a("B_Sb", [128, H * 2 * DV], BF16),
        q=a("B_q", [128, 2 * 8 * CB], BF16),
        k=a("B_k", [128, 2 * 8 * CB], BF16),
        v=a("B_v", [128, 2 * 2048], BF16),
        sgs=a("B_sgs", [128, 2 * 2048], BF16),
        xs=a("B_xs", [128, 2 * D], F32),
        gfs=a("B_gfs", [128, 2 * D], BF16),
        xn=a("B_xn", [128, 2 * D], F32),
        q2=a("B_q2", [128, 4 * 2 * CB], BF16),
        k2=a("B_k2", [128, 4 * DK], BF16),
        at=a("B_at", [128, 4 * CB], BF16),
        sgn=a("B_sgn", [128, DV], BF16),
        epsc=a("B_epsc", [128, 1], F32),
        og=a("B_og", [128, H * DV], BF16),
        ogfm=a("B_ogfm", [128, 16 * 128], BF16),
        sq=a("B_sq", [128, DV], F32),
        cols=a("B_cols", [128, 16], F32),
        so=a("B_so", [128, D], F32),
        t1=a("B_t1", [128, D], F32),
        outt=a("B_outt", [128, 2 * D], F32),
        P=[p(f"B_P{i}", [128, 512], F32) for i in range(6)],
        KTB=p("B_KTB", [128, 1024], BF16),
        OGB=p("B_OGB", [128, 1024], BF16),
    ))

    sem_names = [
        "s_in", "s_cst", "s_oxn", "s_osh0", "s_osh1",
        "s_wstgf",
        "c_st1", "c_c1", "c_sq1", "c_xn", "c_st2", "c_c2", "c_sq2",
        "c_tp", "c_fme", "c_ab", "c_silu", "c_hs", "c_s3", "c_s3e",
        "c_lg", "c_lge", "c_disp", "c_dtp", "c_dtpe",
        "d_wstf", "d_fm", "d_st0", "d_st1",
        "d_dp0", "d_dp1", "d_h", "d_bc", "d_silu", "d_hsc", "d_acc", "d_ev",
        "d_gf", "d_out0", "d_out1",
        "a_wstf", "a_cst", "a_x0", "a_x1", "a_cs0", "a_cs1", "a_xn",
        "a_nrm", "a_tp", "a_nfm", "a_qk", "a_qke", "a_rope", "a_vg", "a_vge",
        "a_oq", "a_ov", "a_pre", "a_pree",
        "b_cst", "b_wstf", "b_wsc",
        "b_iq0", "b_iq1", "b_ik0", "b_ik1", "b_iv0", "b_iv1",
        "b_isg0", "b_isg1", "b_ix0", "b_ix1", "b_igf0", "b_igf1",
        "b_pe", "b_mask", "b_q2", "b_k2", "b_sq", "b_rc1", "b_rc2", "b_rc3",
        "b_on", "b_og", "b_stt", "b_sbf", "b_ogt", "b_oge", "b_wo", "b_woev",
        "b_comb", "b_oo0", "b_oo1",
    ]

    class _S:
        pass
    sems = _S()
    for n in sem_names:
        setattr(sems, n, es.enter_context(nc.semaphore(n)))

    C_WCHUNKS = 8 + 8 + 16 + 1

    def c_sync(eng):
        eng.dma_start(out=identf[:, :], in_=identf_d).then_inc(sems.s_cst, 16)
        eng.dma_start(out=identb[:, :], in_=identb_d).then_inc(sems.s_cst, 16)
        eng.dma_start(out=onesb[:, :], in_=onesb_d).then_inc(sems.s_cst, 16)
        eng.dma_start(out=lnbcol[:, :], in_=lnbcol_d).then_inc(sems.s_cst, 16)
        for i in range(NTILES):
            if i >= 1:
                eng.wait_ge(sems.c_xn, i)
            eng.dma_start(
                out=C["x"][:, :].rearrange("p (s d) -> p s d", s=4),
                in_=x_d[i * NT:(i + 1) * NT, :].rearrange("(s p) d -> p s d", p=128),
            ).then_inc(sems.s_in, 16)

    def c_pool(eng):
        eng.dma_start(out=C["sfc1b"][:, :].rearrange("p (kc n) -> p kc n", kc=8),
                      in_=sfc1t_d.rearrange("(kc p) n -> p kc n", p=128)
                      ).then_inc(sems.s_wstgf, 16)
        eng.dma_start(out=C["sfc2b"][:, :].rearrange("p (kc n) -> p kc n", kc=8),
                      in_=sfc2t_d.rearrange("(kc p) n -> p kc n", p=128)
                      ).then_inc(sems.s_wstgf, 16)
        eng.dma_start(out=C["sfc3b"][:, :].rearrange("p (sc n) -> p sc n", sc=16),
                      in_=sfc3t_d.rearrange("(sc p) n -> p sc n", p=128)
                      ).then_inc(sems.s_wstgf, 16)
        eng.dma_start(out=C["gateb"][:, :].rearrange("p (kc e) -> p kc e", kc=8),
                      in_=gatet_d.rearrange("(kc p) e -> p kc e", p=128)
                      ).then_inc(sems.s_wstgf, 16)
        eng.dma_start(out=lnbcol_b[:, :], in_=lnbcol_d).then_inc(sems.s_wstgf, 16)

    def c_act(eng):
        def ln_part(i):
            eng.wait_ge(sems.s_in, 16 * (i + 1))
            for st in range(4):
                eng.activation(out=C["scr"][:, 0:D], in_=C["x"][:, st * D:(st + 1) * D],
                               func=AF.Square, accum_out=C["cols"][:, st:st + 1])
                a = eng.activation(out=C["scr"][:, 0:D], in_=C["x"][:, st * D:(st + 1) * D],
                                   func=AF.Identity, accum_out=C["cols"][:, 4 + st:4 + st + 1])
            a.then_inc(sems.c_st1, 1)
            eng.wait_ge(sems.c_c1, i + 1)
            eng.activation(out=C["mcol"][:, 0:4], in_=C["cols"][:, 8:12],
                           func=AF.Sqrt).then_inc(sems.c_sq1, 1)
            eng.wait_ge(sems.c_xn, i + 1)
            for st in range(4):
                eng.activation(out=C["scr"][:, 0:D], in_=C["xn"][:, st * D:(st + 1) * D],
                               func=AF.Square, accum_out=C["cols"][:, st:st + 1])
                a = eng.activation(out=C["scr"][:, 0:D], in_=C["xn"][:, st * D:(st + 1) * D],
                                   func=AF.Identity, accum_out=C["cols"][:, 4 + st:4 + st + 1])
            a.then_inc(sems.c_st2, 1)
            eng.wait_ge(sems.c_c2, i + 1)
            eng.activation(out=C["mcol"][:, 4:8], in_=C["cols"][:, 8:12],
                           func=AF.Sqrt).then_inc(sems.c_sq2, 1)

        def silu_part(i):
            for sc in range(16):
                Ksc = i * 16 + sc
                eng.wait_ge(sems.c_ab, Ksc + 1)
                if Ksc >= 2:
                    eng.wait_ge(sems.c_hs, Ksc - 1)
                eng.activation(out=C["silu"][:, (sc % 2) * NT:(sc % 2 + 1) * NT],
                               in_=C["P"][sc % 2][:, :],
                               func=AF.Silu).then_inc(sems.c_silu, 1)
            for mc in range(4):
                Km = i * 4 + mc
                eng.wait_ge(sems.c_disp, 2 * Km + 1)
                eng.activation(out=C["mcol"][:, 3:4], in_=C["mcol"][:, 2:3],
                               func=AF.Tanh, scale=0.5).then_inc(sems.c_dtp, 1)

        for it in range(NTILES + 1):
            if it < NTILES:
                ln_part(it)
            if it >= 1:
                silu_part(it - 1)

    def c_vector(eng):
        def ln_cols(i):
            eng.wait_ge(sems.c_st1, i + 1)
            nm = negm1[:, i * 4:(i + 1) * 4]
            eng.tensor_scalar(out=nm, in0=C["cols"][:, 4:8], scalar1=-1.0 / D,
                              scalar2=None, op0=ALU.mult)
            eng.drain()
            eng.tensor_mul(out=C["cols"][:, 12:16], in0=nm, in1=nm)
            eng.drain()
            eng.scalar_tensor_tensor(out=C["cols"][:, 8:12], in0=C["cols"][:, 0:4],
                                     scalar=1.0 / D, in1=C["cols"][:, 12:16],
                                     op0=ALU.mult, op1=ALU.subtract)
            eng.drain()
            eng.tensor_scalar(out=C["cols"][:, 8:12], in0=C["cols"][:, 8:12],
                              scalar1=EPS, scalar2=None,
                              op0=ALU.add).then_inc(sems.c_c1, 1)
            eng.wait_ge(sems.c_sq1, i + 1)
            eng.reciprocal(out=rstd1[:, i * 4:(i + 1) * 4], in_=C["mcol"][:, 0:4])
            eng.drain()
            for st in range(4):
                c = i * 4 + st
                a = eng.tensor_scalar(out=C["xn"][:, st * D:(st + 1) * D],
                                      in0=C["x"][:, st * D:(st + 1) * D],
                                      scalar1=negm1[:, c:c + 1], scalar2=rstd1[:, c:c + 1],
                                      op0=ALU.add, op1=ALU.mult)
            a.then_inc(sems.c_xn, 1)
            eng.wait_ge(sems.c_st2, i + 1)
            nm2 = negm2[:, i * 4:(i + 1) * 4]
            eng.tensor_scalar(out=nm2, in0=C["cols"][:, 4:8], scalar1=-1.0 / D,
                              scalar2=None, op0=ALU.mult)
            eng.drain()
            eng.tensor_mul(out=C["cols"][:, 12:16], in0=nm2, in1=nm2)
            eng.drain()
            eng.scalar_tensor_tensor(out=C["cols"][:, 8:12], in0=C["cols"][:, 0:4],
                                     scalar=1.0 / D, in1=C["cols"][:, 12:16],
                                     op0=ALU.mult, op1=ALU.subtract)
            eng.drain()
            eng.tensor_scalar(out=C["cols"][:, 8:12], in0=C["cols"][:, 8:12],
                              scalar1=EPS, scalar2=None,
                              op0=ALU.add).then_inc(sems.c_c2, 1)
            eng.wait_ge(sems.c_sq2, i + 1)
            eng.reciprocal(out=rstd2[:, i * 4:(i + 1) * 4], in_=C["mcol"][:, 4:8])

        def fm_evacs(i):
            for g in range(8):
                K = i * 8 + g
                st, gg = g // 2, g % 2
                eng.wait_ge(sems.c_tp, K + 1)
                if i > 0 and g == 0:
                    eng.wait_ge(sems.s_oxn, 16 * i)
                eng.tensor_copy(
                    out=C["fm"][:, :].rearrange("p (kc t) -> p kc t", kc=8)
                        [:, gg * 4:(gg + 1) * 4, st * 128:(st + 1) * 128],
                    in_=C["TPB"][K % 2][:, :].rearrange("p (s f) -> p s f", s=4)
                ).then_inc(sems.c_fme, 1)

        def work_part(i):
            for sc in range(16):
                Ksc = i * 16 + sc
                eng.wait_ge(sems.c_silu, Ksc + 1)
                eng.tensor_mul(out=C["hs"][:, sc * NT:(sc + 1) * NT],
                               in0=C["silu"][:, (sc % 2) * NT:(sc % 2 + 1) * NT],
                               in1=C["P"][2 + sc % 2][:, :]).then_inc(sems.c_hs, 1)
            for g in range(8):
                K = i * 8 + g
                mc, ng = g // 2, g % 2
                eng.wait_ge(sems.c_s3, K + 1)
                if ng == 0:
                    occ = 2 * i + mc // 2
                    if occ > 0:
                        eng.wait_ge([sems.s_osh0, sems.s_osh1][mc % 2], 16 * occ)
                off = (1 + mc % 2) * D + ng * 512
                eng.tensor_copy(out=C["scr"][:, off:off + 512],
                                in_=C["P"][4 + ng][:, :]).then_inc(sems.c_s3e, 1)
            for mc in range(4):
                Km = i * 4 + mc
                eng.wait_ge(sems.c_lg, Km + 1)
                eng.tensor_copy(out=C["lg"][:, :],
                                in_=C["P"][4][:, 0:16]).then_inc(sems.c_lge, 1)
                eng.drain()
                eng.reduce_max(out=C["mcol"][:, 0:1], in_=C["lg"][:, :], axis=AX.X)
                eng.drain()
                eng.tensor_scalar(out=C["eq1"][:, :], in0=C["lg"][:, :],
                                  scalar1=C["mcol"][:, 0:1], scalar2=None, op0=ALU.is_ge)
                eng.drain()
                eng.scalar_tensor_tensor(out=C["lg2"][:, :], in0=C["eq1"][:, :],
                                         scalar=-1e30, in1=C["lg"][:, :],
                                         op0=ALU.mult, op1=ALU.add)
                eng.drain()
                eng.reduce_max(out=C["mcol"][:, 1:2], in_=C["lg2"][:, :], axis=AX.X)
                eng.drain()
                eng.tensor_scalar(out=C["eq2"][:, :], in0=C["lg2"][:, :],
                                  scalar1=C["mcol"][:, 1:2], scalar2=None, op0=ALU.is_ge)
                eng.tensor_sub(out=C["mcol"][:, 2:3], in0=C["mcol"][:, 0:1],
                               in1=C["mcol"][:, 1:2]).then_inc(sems.c_disp, 1)
                eng.wait_ge(sems.c_dtp, Km + 1)
                eng.tensor_scalar(out=C["mcol"][:, 4:5], in0=C["mcol"][:, 3:4],
                                  scalar1=0.5, scalar2=0.5, op0=ALU.mult, op1=ALU.add)
                eng.drain()
                eng.tensor_scalar(out=C["mcol"][:, 5:6], in0=C["mcol"][:, 4:5],
                                  scalar1=-1.0, scalar2=1.0, op0=ALU.mult, op1=ALU.add)
                eng.tensor_scalar(out=C["disp"][:, :], in0=C["eq1"][:, :],
                                  scalar1=C["mcol"][:, 4:5], scalar2=None, op0=ALU.mult)
                eng.drain()
                eng.scalar_tensor_tensor(out=C["disp"][:, :], in0=C["eq2"][:, :],
                                         scalar=C["mcol"][:, 5:6], in1=C["disp"][:, :],
                                         op0=ALU.mult,
                                         op1=ALU.add).then_inc(sems.c_disp, 1)
                eng.wait_ge(sems.c_dtpe, Km + 1)
                eng.tensor_copy(out=C["dispT"][0:16, Km * 128:(Km + 1) * 128],
                                in_=C["P"][5][0:16, 0:128]).then_inc(sems.c_lge, 1)

        for it in range(NTILES + 1):
            if it < NTILES:
                ln_cols(it)
            if it >= 1:
                work_part(it - 1)
            if it < NTILES:
                fm_evacs(it)

    def c_tensor(eng):
        fmv = C["fm"][:, :].rearrange("p (kc t) -> p kc t", kc=8)
        hsv = C["hs"][:, :].rearrange("p (sc t) -> p sc t", sc=16)

        def transposes(i):
            eng.wait_ge(sems.c_xn, i + 1)
            for g in range(8):
                K = i * 8 + g
                st, gg = g // 2, g % 2
                if K >= 2:
                    eng.wait_ge(sems.c_fme, K - 1)
                for s in range(4):
                    kc = gg * 4 + s
                    a = tp(eng, C["TPB"][K % 2][:, s * 128:(s + 1) * 128],
                           C["xn"][:, st * D + kc * 128: st * D + (kc + 1) * 128],
                           identf[:, :], start=(s == 0), stop=(s == 3))
                a.then_inc(sems.c_tp, 1)

        def sfc(i):
            eng.wait_ge(sems.c_fme, (i + 1) * 8)
            if i == 0:
                eng.wait_ge(sems.s_wstgf, 32)
            for sc in range(16):
                Ksc = i * 16 + sc
                if Ksc >= 2:
                    eng.wait_ge(sems.c_silu, Ksc - 1)
                    eng.wait_ge(sems.c_hs, Ksc - 1)
                for kc in range(8):
                    mm(eng, C["P"][sc % 2][:, :],
                       C["sfc1b"][:, kc * 2048 + sc * 128: kc * 2048 + (sc + 1) * 128],
                       fmv[:, kc, :], start=(kc == 0), stop=(kc == 7))
                for kc in range(8):
                    a = mm(eng, C["P"][2 + sc % 2][:, :],
                           C["sfc2b"][:, kc * 2048 + sc * 128: kc * 2048 + (sc + 1) * 128],
                           fmv[:, kc, :], start=(kc == 0), stop=(kc == 7))
                a.then_inc(sems.c_ab, 1)
            eng.wait_ge(sems.c_hs, (i + 1) * 16)
            if i == 0:
                eng.wait_ge(sems.s_wstgf, 48)
            for g in range(8):
                K = i * 8 + g
                mc, ng = g // 2, g % 2
                if K >= 2:
                    eng.wait_ge(sems.c_s3e, K - 1)
                for sc in range(16):
                    a = mm(eng, C["P"][4 + ng][:, :],
                           hsv[:, sc, mc * 128:(mc + 1) * 128],
                           C["sfc3b"][:, sc * D + ng * 512: sc * D + ng * 512 + 512],
                           start=(sc == 0), stop=(sc == 15))
                a.then_inc(sems.c_s3, 1)
            for mc in range(4):
                Km = i * 4 + mc
                if mc == 0:
                    eng.wait_ge(sems.c_s3e, i * 8 + 8)
                    if i == 0:
                        eng.wait_ge(sems.s_wstgf, 64)
                if Km >= 1:
                    eng.wait_ge(sems.c_lge, 2 * Km - 1)
                for kc in range(8):
                    a = mm(eng, C["P"][4][:, 0:16],
                           fmv[:, kc, mc * 128:(mc + 1) * 128],
                           C["gateb"][:, kc * 16:(kc + 1) * 16],
                           start=(kc == 0), stop=(kc == 7))
                a.then_inc(sems.c_lg, 1)
                eng.wait_ge(sems.c_disp, 2 * Km + 2)
                if Km >= 1:
                    eng.wait_ge(sems.c_lge, 2 * Km)
                tp(eng, C["P"][5][0:16, 0:128], C["disp"][:, :], identf[:, :],
                   start=True, stop=True).then_inc(sems.c_dtpe, 1)

        for i in range(NTILES):
            transposes(i)
            sfc(i)

    def c_gpsimd_out(eng):
        for i in range(NTILES):
            eng.wait_ge(sems.c_fme, (i + 1) * 8)
            eng.dma_start(
                out=xnfm_d[:, :, i * NT:(i + 1) * NT].rearrange("kc p t -> p kc t"),
                in_=C["fm"][:, :].rearrange("p (kc t) -> p kc t", kc=8),
            ).then_inc(sems.s_oxn, 16)
            for mc in range(4):
                eng.wait_ge(sems.c_s3e, i * 8 + 2 * (mc + 1))
                off = (1 + mc % 2) * D
                eng.dma_start(
                    out=sh_d[i * NT + mc * 128: i * NT + (mc + 1) * 128, :],
                    in_=C["scr"][:, off:off + D],
                ).then_inc([sems.s_osh0, sems.s_osh1][mc % 2], 16)
        nxn = 16 * NTILES
        eng.wait_ge(sems.c_lge, 2 * 4 * NTILES)
        eng.dma_start(out=dispt_d, in_=C["dispT"][:, :]).then_inc(sems.s_oxn, 16)
        nxn += 16
        if dbg:
            eng.wait_ge(sems.c_lge, 2 * 4 * NTILES)
        if "dispdbg" in dbg:
            eng.dma_start(out=dbg["dispdbg"], in_=C["dispT"][:, :]).then_inc(sems.s_oxn, 16)
            nxn += 16
        if "statsdbg" in dbg:
            eng.dma_start(out=dbg["statsdbg"][:, 0:NCH], in_=negm1[:, :]).then_inc(sems.s_oxn, 16)
            eng.dma_start(out=dbg["statsdbg"][:, NCH:2 * NCH], in_=rstd1[:, :]).then_inc(sems.s_oxn, 16)
            eng.dma_start(out=dbg["statsdbg"][:, 2 * NCH:3 * NCH], in_=negm2[:, :]).then_inc(sems.s_oxn, 16)
            eng.dma_start(out=dbg["statsdbg"][:, 3 * NCH:4 * NCH], in_=rstd2[:, :]).then_inc(sems.s_oxn, 16)
            nxn += 64
        eng.wait_ge(sems.s_oxn, nxn)
        eng.wait_ge(sems.s_osh0, 16 * 2 * NTILES)
        eng.wait_ge(sems.s_osh1, 16 * 2 * NTILES)

    DE_TOT = dict(wstf=256, fm=16 * NTILES, st=16 * 2 * NTILES,
                  dp=16 * 4 * NTILES, h=32 * NTILES, bc=8 * NTILES,
                  silu=32 * NTILES, hsc=32 * NTILES, acc=8 * NTILES,
                  ev=4 * NTILES, gf=4 * NTILES, out=16 * 2 * NTILES)

    def de_bases(pi):
        b = {k: pi * v for k, v in DE_TOT.items()}
        b["gf"] = 0
        return b

    def de_sync(eng, bs, ge0, stream_d):
        for i in range(NTILES):
            w = bs["h"] + 32 * i
            if w > 0:
                eng.wait_ge(sems.d_h, w)
            eng.dma_start(
                out=DE["fm"][:, :].rearrange("p (kc t) -> p kc t", kc=8),
                in_=xnfm_d[:, :, i * NT:(i + 1) * NT].rearrange("kc p t -> p kc t"),
            ).then_inc(sems.d_fm, 16)
            for e in range(8):
                w = bs["bc"] + 8 * i + e - 1
                if w > 0:
                    eng.wait_ge(sems.d_bc, w)
                eng.dma_start(
                    out=DE["dprow"][0:1, (e % 2) * 512:(e % 2) * 512 + 512],
                    in_=dispt_d[ge0 + e:ge0 + e + 1, i * NT:(i + 1) * NT],
                ).then_inc([sems.d_dp0, sems.d_dp1][e % 2], 16)
            for mc in range(4):
                w = bs["ev"] + 4 * i + mc - 1
                if w > 0:
                    eng.wait_ge(sems.d_ev, w)
                eng.dma_start(
                    out=DE["stream"][:, (mc % 2) * D:(mc % 2) * D + D],
                    in_=stream_d[i * NT + mc * 128: i * NT + (mc + 1) * 128, :],
                ).then_inc([sems.d_st0, sems.d_st1][mc % 2], 16)
        eng.wait_ge(sems.d_fm, bs["fm"] + 16 * NTILES)
        eng.wait_ge(sems.d_dp0, bs["dp"] + 16 * 4 * NTILES)
        eng.wait_ge(sems.d_dp1, bs["dp"] + 16 * 4 * NTILES)
        eng.wait_ge(sems.d_st0, bs["st"] + 16 * 2 * NTILES)
        eng.wait_ge(sems.d_st1, bs["st"] + 16 * 2 * NTILES)

    def de_pool(eng, bs, is_e):
        ge0 = 8 if is_e else 0
        for e in range(8):
            eng.dma_start(
                out=DE["w1b"][:, e * 4096:(e + 1) * 4096]
                    .rearrange("p (kc h) -> p kc h", kc=8),
                in_=w1t_d[ge0 + e].rearrange("(kc p) h -> p kc h", p=128),
            ).then_inc(sems.d_wstf, 16)
            eng.dma_start(
                out=DE["w2b"][:, e * 4096:(e + 1) * 4096]
                    .rearrange("p (kc d) -> p kc d", kc=4),
                in_=w2t_d[ge0 + e].rearrange("(kc p) d -> p kc d", p=128),
            ).then_inc(sems.d_wstf, 16)
        for i in range(NTILES):
            for mc in range(4):
                if is_e:
                    eng.wait_ge(sems.d_gf, bs["gf"] + 4 * i + mc + 1)
                    srcb = DE["gfh"][:, (mc % 2) * D:(mc % 2) * D + D]
                    dstd = gf_d[i * NT + mc * 128: i * NT + (mc + 1) * 128, :]
                else:
                    eng.wait_ge(sems.d_ev, bs["ev"] + 4 * i + mc + 1)
                    srcb = DE["outh"][:, (mc % 2) * D:(mc % 2) * D + D]
                    dstd = moeps_d[i * NT + mc * 128: i * NT + (mc + 1) * 128, :]
                eng.dma_start(out=dstd, in_=srcb).then_inc(
                    [sems.d_out0, sems.d_out1][mc % 2], 16)
        eng.wait_ge(sems.d_out0, bs["out"] + 16 * 2 * NTILES)
        eng.wait_ge(sems.d_out1, bs["out"] + 16 * 2 * NTILES)

    def de_tensor(eng, bs, is_e):
        fmv = DE["fm"][:, :].rearrange("p (kc t) -> p kc t", kc=8)
        hscv = DE["hsc"][:, :].rearrange("p (e hc t) -> p e hc t", e=8, hc=4)
        for i in range(NTILES):
            eng.wait_ge(sems.d_fm, bs["fm"] + 16 * (i + 1))
            for e in range(8):
                Ke = 8 * i + e
                if i == 0:
                    eng.wait_ge(sems.d_wstf, bs["wstf"] + 16 * (2 * e + 1))
                w = bs["silu"] + 4 * Ke
                if w > 0:
                    eng.wait_ge(sems.d_silu, w)
                for hc in range(4):
                    for kc in range(8):
                        a = mm(eng, DE["P"][hc][:, :],
                               DE["w1b"][:, e * 4096 + kc * 512 + hc * 128:
                                         e * 4096 + kc * 512 + (hc + 1) * 128],
                               fmv[:, kc, :], start=(kc == 0), stop=(kc == 7))
                    a.then_inc(sems.d_h, 1)
                eng.wait_ge([sems.d_dp0, sems.d_dp1][e % 2],
                            bs["dp"] + 16 * (4 * i + e // 2 + 1))
                w = bs["hsc"] + 4 * (Ke - 1)
                if w > 0:
                    eng.wait_ge(sems.d_hsc, w)
                if e < 2 and (bs["ev"] + 4 * i) > 0:
                    eng.wait_ge(sems.d_ev, bs["ev"] + 4 * i)
                mm(eng, DE["P"][4 + e % 2][:, :], onesb[0:1, :],
                   DE["dprow"][0:1, (e % 2) * 512:(e % 2) * 512 + 512],
                   start=True, stop=True).then_inc(sems.d_bc, 1)
            eng.wait_ge(sems.d_hsc, bs["hsc"] + 32 * (i + 1))
            if i == 0:
                eng.wait_ge(sems.d_wstf, bs["wstf"] + 16 * 16)
            for mc in range(4):
                w = bs["ev"] + 4 * i + mc - 1
                if w > 0:
                    eng.wait_ge(sems.d_ev, w)
                ob = 6 if mc % 2 == 0 else 4
                for ng in range(2):
                    for e in range(8):
                        for kc in range(4):
                            a = mm(eng, DE["P"][ob + ng][:, :],
                                   hscv[:, e, kc, mc * 128:(mc + 1) * 128],
                                   DE["w2b"][:, e * 4096 + kc * 1024 + ng * 512:
                                             e * 4096 + kc * 1024 + ng * 512 + 512],
                                   start=(e == 0 and kc == 0),
                                   stop=(e == 7 and kc == 3))
                    a.then_inc(sems.d_acc, 1)

    def de_act(eng, bs, is_e):
        for i in range(NTILES):
            for e in range(8):
                for hc in range(4):
                    ghc = 32 * i + 4 * e + hc
                    eng.wait_ge(sems.d_h, bs["h"] + ghc + 1)
                    w = bs["hsc"] + ghc - 1
                    if w > 0:
                        eng.wait_ge(sems.d_hsc, w)
                    eng.activation(out=DE["silu"][:, (ghc % 2) * 512:(ghc % 2) * 512 + 512],
                                   in_=DE["P"][hc][:, :],
                                   func=AF.Silu).then_inc(sems.d_silu, 1)
            if is_e:
                for mc in range(4):
                    eng.wait_ge(sems.d_ev, bs["ev"] + 4 * i + mc + 1)
                    w = bs["out"] + 16 * (2 * i + mc // 2)
                    if w > 0:
                        eng.wait_ge([sems.d_out0, sems.d_out1][mc % 2], w)
                    eng.activation(out=DE["gfh"][:, (mc % 2) * D:(mc % 2) * D + D],
                                   in_=DE["outh"][:, (mc % 2) * D:(mc % 2) * D + D],
                                   func=AF.Silu).then_inc(sems.d_gf, 1)

    def de_vector(eng, bs, is_e):
        hscv = DE["hsc"][:, :].rearrange("p (e hc t) -> p e hc t", e=8, hc=4)
        for i in range(NTILES):
            for e in range(8):
                for hc in range(4):
                    ghc = 32 * i + 4 * e + hc
                    eng.wait_ge(sems.d_silu, bs["silu"] + ghc + 1)
                    eng.wait_ge(sems.d_bc, bs["bc"] + 8 * i + e + 1)
                    eng.tensor_mul(out=hscv[:, e, hc, :],
                                   in0=DE["silu"][:, (ghc % 2) * 512:(ghc % 2) * 512 + 512],
                                   in1=DE["P"][4 + e % 2][:, :]).then_inc(sems.d_hsc, 1)
            for mc in range(4):
                eng.wait_ge(sems.d_acc, bs["acc"] + 8 * i + 2 * mc + 2)
                eng.wait_ge([sems.d_st0, sems.d_st1][mc % 2],
                            bs["st"] + 16 * (2 * i + mc // 2 + 1))
                if is_e:
                    w = bs["gf"] + 4 * i + mc - 1
                    if w > 0:
                        eng.wait_ge(sems.d_gf, w)
                else:
                    w = bs["out"] + 16 * (2 * i + mc // 2)
                    if w > 0:
                        eng.wait_ge([sems.d_out0, sems.d_out1][mc % 2], w)
                off = (mc % 2) * D
                ob = 6 if mc % 2 == 0 else 4
                eng.tensor_add(out=DE["outh"][:, off:off + 512],
                               in0=DE["P"][ob][:, :],
                               in1=DE["stream"][:, off:off + 512])
                eng.tensor_add(out=DE["outh"][:, off + 512:off + D],
                               in0=DE["P"][ob + 1][:, :],
                               in1=DE["stream"][:, off + 512:off + D]
                               ).then_inc(sems.d_ev, 1)

    A1 = mkphase(lambda a, p: dict(
        wqb=a("A_wqb", [128, 8 * D], BF16),
        wkb=a("A_wkb", [128, 8 * D], BF16),
        wvb=a("A_wvb", [128, 8 * 2048], BF16),
        wgb=a("A_wgb", [128, 8 * 2048], BF16),
        lnwbc=a("A_lnwbc", [128, D], F32),
        cs=a("A_cs", [128, 2 * 4 * NTA], BF16),
        brow=a("A_brow", [1, 2048], F32),
        bqcol=a("A_bqcol", [128, 8], F32),
        bkcol=a("A_bkcol", [128, 8], F32),
        bvrow=a("A_bvrow", [1, 2048], BF16),
        bgrow=a("A_bgrow", [1, 2048], BF16),
        x=a("A_x", [128, 2 * 2 * D], F32),
        xn=a("A_xn", [128, 2 * D], F32),
        ntmp=a("A_ntmp", [128, D], F32),
        nrm=a("A_nrm", [128, 2 * D], BF16),
        nfm=a("A_nfm", [128, 8 * NTA], BF16),
        qf=a("A_qf", [128, 2 * NTA], BF16),
        kf=a("A_kf", [128, 2 * NTA], BF16),
        rt=a("A_rt", [128, 4 * NTA], BF16),
        qp=a("A_qp", [128, 8 * NTA], BF16),
        kp=a("A_kp", [128, 8 * NTA], BF16),
        v=a("A_v", [128, 2 * 2048], BF16),
        sg=a("A_sg", [128, 2 * 2048], BF16),
        P=[p(f"A_P{i}", [128, 512], F32) for i in range(6)],
        TPB=[p(f"A_TPB{i}", [128, 1024], BF16) for i in range(2)],
    ))

    def a1_sync(eng):
        eng.dma_start(out=A1["lnwbc"][:, :], in_=lnwbc_d).then_inc(sems.a_cst, 16)
        for i in range(NTILESA):
            if i >= 2:
                eng.wait_ge(sems.a_xn, i - 1)
            eng.dma_start(
                out=A1["x"][:, (i % 2) * 2 * D:(i % 2) * 2 * D + 2 * D]
                    .rearrange("p (s d) -> p s d", s=2),
                in_=x_d[i * NTA:(i + 1) * NTA, :].rearrange("(s p) d -> p s d", p=128),
            ).then_inc([sems.a_x0, sems.a_x1][i % 2], 16)

    def a1_pool(eng):
        eng.dma_start(out=A1["wqb"][:, :].rearrange("p (kc n) -> p kc n", kc=8),
                      in_=wq_d.rearrange("(kc p) n -> p kc n", p=128)
                      ).then_inc(sems.a_wstf, 16)
        eng.dma_start(out=A1["wkb"][:, :].rearrange("p (kc n) -> p kc n", kc=8),
                      in_=wk_d.rearrange("(kc p) n -> p kc n", p=128)
                      ).then_inc(sems.a_wstf, 16)
        eng.dma_start(out=A1["wvb"][:, :].rearrange("p (kc n) -> p kc n", kc=8),
                      in_=wv_d.rearrange("(kc p) n -> p kc n", p=128)
                      ).then_inc(sems.a_wstf, 16)
        eng.dma_start(out=A1["wgb"][:, :].rearrange("p (kc n) -> p kc n", kc=8),
                      in_=wg_d.rearrange("(kc p) n -> p kc n", p=128)
                      ).then_inc(sems.a_wstf, 16)
        for i in range(NTILESA):
            if i >= 2:
                eng.wait_ge(sems.a_rope, 8 * (i - 1))
            co = (i % 2) * 4 * NTA
            for j, srcd in enumerate([cos_d, sin_d, cosq_d, sinq_d]):
                eng.dma_start(out=A1["cs"][:, co + j * NTA:co + (j + 1) * NTA],
                              in_=srcd[:, i * NTA:(i + 1) * NTA]).then_inc(
                                  [sems.a_cs0, sems.a_cs1][i % 2], 16)
            eng.wait_ge(sems.a_rope, 8 * (i + 1))
            eng.dma_start(
                out=qp_d[:, :, i * NTA:(i + 1) * NTA].rearrange("kc p t -> p kc t"),
                in_=A1["qp"][:, :].rearrange("p (kc t) -> p kc t", kc=8),
            ).then_inc(sems.a_oq, 16)
            eng.dma_start(
                out=kp_d[:, :, i * NTA:(i + 1) * NTA].rearrange("kc p t -> p kc t"),
                in_=A1["kp"][:, :].rearrange("p (kc t) -> p kc t", kc=8),
            ).then_inc(sems.a_oq, 16)
            eng.wait_ge(sems.a_vge, 16 * (i + 1))
            eng.dma_start(
                out=vtm_d[i * NTA:(i + 1) * NTA, :].rearrange("(s p) n -> p s n", p=128),
                in_=A1["v"][:, :].rearrange("p (s n) -> p s n", s=2),
            ).then_inc(sems.a_ov, 16)
            eng.dma_start(
                out=sgtm_d[i * NTA:(i + 1) * NTA, :].rearrange("(s p) n -> p s n", p=128),
                in_=A1["sg"][:, :].rearrange("p (s n) -> p s n", s=2),
            ).then_inc(sems.a_ov, 16)
        eng.wait_ge(sems.a_oq, 16 * 2 * NTILESA)
        eng.wait_ge(sems.a_ov, 16 * 2 * NTILESA)

    def a1_tensor(eng):
        rows = ([("q", A1["wqb"], 1024, hx) for hx in range(2)]
                + [("k", A1["wkb"], 1024, hx) for hx in range(2)]
                + [("v", A1["wvb"], 2048, hx) for hx in range(4)]
                + [("g", A1["wgb"], 2048, hx) for hx in range(4)])
        for r, (kind, wb, wid, hx) in enumerate(rows):
            need = {"q": 16, "k": 32, "v": 48, "g": 64}[kind]
            eng.wait_ge(sems.a_wstf, need)
            if r >= 2:
                eng.wait_ge(sems.a_pree, r - 1)
            bank = A1["P"][5 if r % 2 == 0 else 4]
            for kc in range(8):
                a = mm(eng, bank[0:1, 0:512],
                       lnbcol_b[:, kc:kc + 1],
                       wb[:, kc * wid + hx * 512: kc * wid + (hx + 1) * 512],
                       start=(kc == 0), stop=(kc == 7))
            a.then_inc(sems.a_pre, 1)
        for j in range(16):
            eng.wait_ge(sems.a_pree, 12 + j)
            tp(eng, A1["P"][3][:, 0:1], A1["brow"][0:1, j * 128:(j + 1) * 128],
               identf[0:1, 0:1], start=True, stop=True).then_inc(sems.a_pre, 1)
        eng.wait_ge(sems.a_pree, 28)
        nfmv = A1["nfm"][:, :].rearrange("p (kc t) -> p kc t", kc=8)
        for i in range(NTILESA):
            for st in range(2):
                gt = 2 * i + st
                eng.wait_ge(sems.a_nrm, gt + 1)
                if gt >= 2:
                    eng.wait_ge(sems.a_nfm, gt - 1)
                for kc in range(8):
                    a = tp(eng, A1["TPB"][gt % 2][:, kc * 128:(kc + 1) * 128],
                           A1["nrm"][:, st * D + kc * 128: st * D + (kc + 1) * 128],
                           identb[:, :], start=(kc == 0), stop=(kc == 7))
                a.then_inc(sems.a_tp, 1)
            eng.wait_ge(sems.a_nfm, 2 * (i + 1))
            for hp in range(4):
                for qk in range(2):
                    wb = A1["wqb"] if qk == 0 else A1["wkb"]
                    if hp == 0 and qk == 0 and i >= 1:
                        eng.wait_ge(sems.a_vge, 16 * i)
                    w = 8 * i + (hp - 1) * 2 + qk + 1
                    if w > 0:
                        eng.wait_ge(sems.a_qke, w)
                    for half in range(2):
                        dq = 2 * hp + half
                        bank = A1["P"][qk * 2 + half]
                        for kc in range(8):
                            a = mm(eng, bank[:, 0:NTA],
                                   wb[:, kc * D + dq * 128: kc * D + (dq + 1) * 128],
                                   nfmv[:, kc, :], start=(kc == 0), stop=(kc == 7))
                    a.then_inc(sems.a_qk, 1)
            for st in range(2):
                for vg in range(2):
                    wb = A1["wvb"] if vg == 0 else A1["wgb"]
                    br = A1["bvrow"] if vg == 0 else A1["bgrow"]
                    for ng in range(4):
                        gv = 16 * i + st * 8 + vg * 4 + ng
                        if st == 0 and ng == 1:
                            eng.wait_ge(sems.a_qke, 8 * (i + 1))
                        if gv >= 2:
                            eng.wait_ge(sems.a_vge, gv - 1)
                        bank = A1["P"][(4 + vg) if ng % 2 == 0 else vg]
                        for kc in range(8):
                            mm(eng, bank[:, :],
                               nfmv[:, kc, st * 128:(st + 1) * 128],
                               wb[:, kc * 2048 + ng * 512: kc * 2048 + (ng + 1) * 512],
                               start=(kc == 0), stop=False)
                        mm(eng, bank[:, :], onesb[0:1, 0:128],
                           br[0:1, ng * 512:(ng + 1) * 512],
                           start=False, stop=True).then_inc(sems.a_vg, 1)

    def a1_act(eng):
        rows = (["q0", "q1", "k0", "k1"] + [f"v{j}" for j in range(4)]
                + [f"g{j}" for j in range(4)])
        for r, tag in enumerate(rows):
            eng.wait_ge(sems.a_pre, r + 1)
            bank = A1["P"][5 if r % 2 == 0 else 4]
            kind, hx = tag[0], int(tag[1])
            if kind == "q":
                dst = A1["brow"][0:1, hx * 512:(hx + 1) * 512]
            elif kind == "k":
                dst = A1["brow"][0:1, 1024 + hx * 512: 1024 + (hx + 1) * 512]
            elif kind == "v":
                dst = A1["bvrow"][0:1, hx * 512:(hx + 1) * 512]
            else:
                dst = A1["bgrow"][0:1, hx * 512:(hx + 1) * 512]
            eng.activation(out=dst, in_=bank[0:1, 0:512],
                           func=AF.Identity).then_inc(sems.a_pree, 1)
        for j in range(16):
            eng.wait_ge(sems.a_pre, 12 + j + 1)
            dst = A1["bqcol"][:, j:j + 1] if j < 8 else A1["bkcol"][:, j - 8:j - 7]
            eng.activation(out=dst, in_=A1["P"][3][:, 0:1],
                           func=AF.Identity).then_inc(sems.a_pree, 1)
        for i in range(NTILESA):
            for st in range(2):
                gt = 2 * i + st
                eng.wait_ge(sems.a_tp, gt + 1)
                eng.activation(out=A1["nfm"][:, :]
                               .rearrange("p (kc t) -> p kc t", kc=8)
                               [:, :, st * 128:(st + 1) * 128],
                               in_=A1["TPB"][gt % 2][:, :]
                               .rearrange("p (kc t) -> p kc t", kc=8),
                               func=AF.Identity).then_inc(sems.a_nfm, 1)
            for st in range(2):
                for vg in range(2):
                    for ng in range(4):
                        gv = 16 * i + st * 8 + vg * 4 + ng
                        eng.wait_ge(sems.a_vg, gv + 1)
                        if i > 0 and st == 0 and vg == 0 and ng == 0:
                            eng.wait_ge(sems.a_ov, 16 * 2 * i)
                        dstbuf = A1["v"] if vg == 0 else A1["sg"]
                        dst = dstbuf[:, st * 2048 + ng * 512: st * 2048 + (ng + 1) * 512]
                        bank = A1["P"][(4 + vg) if ng % 2 == 0 else vg]
                        eng.activation(out=dst, in_=bank[:, :],
                                       func=AF.Identity if vg == 0 else AF.Silu
                                       ).then_inc(sems.a_vge, 1)

    def a1_vector(eng):
        for i in range(NTILESA):
            eng.wait_ge([sems.a_x0, sems.a_x1][i % 2], 16 * (i // 2 + 1))
            eng.wait_ge(sems.a_cst, 16)
            xoff = (i % 2) * 2 * D
            for st in range(2):
                c = 2 * i + st
                a = eng.tensor_scalar(out=A1["xn"][:, st * D:(st + 1) * D],
                                      in0=A1["x"][:, xoff + st * D: xoff + (st + 1) * D],
                                      scalar1=negm1[:, c:c + 1], scalar2=rstd1[:, c:c + 1],
                                      op0=ALU.add, op1=ALU.mult)
            a.then_inc(sems.a_xn, 1)
            for st in range(2):
                c = 2 * i + st
                eng.tensor_scalar(out=A1["ntmp"][:, :],
                                  in0=A1["xn"][:, st * D:(st + 1) * D],
                                  scalar1=negm2[:, c:c + 1], scalar2=rstd2[:, c:c + 1],
                                  op0=ALU.add, op1=ALU.mult)
                eng.tensor_mul(out=A1["nrm"][:, st * D:(st + 1) * D],
                               in0=A1["ntmp"][:, :],
                               in1=A1["lnwbc"][:, :]).then_inc(sems.a_nrm, 1)
            eng.wait_ge([sems.a_cs0, sems.a_cs1][i % 2], 64 * (i // 2 + 1))
            co = (i % 2) * 4 * NTA
            for hp in range(4):
                for qk in range(2):
                    gq = 8 * i + hp * 2 + qk
                    eng.wait_ge(sems.a_qk, gq + 1)
                    if i > 0 and hp == 0 and qk == 0:
                        eng.wait_ge(sems.a_oq, 16 * 2 * i)
                    bcol = A1["bqcol"] if qk == 0 else A1["bkcol"]
                    fbuf = A1["qf"] if qk == 0 else A1["kf"]
                    obuf = A1["qp"] if qk == 0 else A1["kp"]
                    pA = A1["P"][qk * 2][:, 0:NTA]
                    pB = A1["P"][qk * 2 + 1][:, 0:NTA]
                    eng.tensor_scalar(out=fbuf[:, 0:NTA], in0=pA,
                                      scalar1=bcol[:, 2 * hp:2 * hp + 1], scalar2=None,
                                      op0=ALU.add)
                    a = eng.tensor_scalar(out=fbuf[:, NTA:2 * NTA], in0=pB,
                                          scalar1=bcol[:, 2 * hp + 1:2 * hp + 2],
                                          scalar2=None, op0=ALU.add)
                    a.then_inc(sems.a_qke, 1)
                    fA = fbuf[:, 0:NTA]
                    fB = fbuf[:, NTA:2 * NTA]
                    rt = A1["rt"]
                    jo = 2 * NTA if qk == 0 else 0
                    coss = A1["cs"][:, co + jo:co + jo + NTA]
                    sins = A1["cs"][:, co + jo + NTA:co + jo + 2 * NTA]
                    eng.tensor_mul(out=rt[:, 0:NTA], in0=fA, in1=coss)
                    eng.tensor_mul(out=rt[:, NTA:2 * NTA], in0=fB, in1=sins)
                    eng.tensor_mul(out=rt[:, 2 * NTA:3 * NTA], in0=fB, in1=coss)
                    eng.tensor_mul(out=rt[:, 3 * NTA:4 * NTA], in0=fA, in1=sins)
                    eng.tensor_sub(out=obuf[:, (2 * hp) * NTA:(2 * hp + 1) * NTA],
                                   in0=rt[:, 0:NTA], in1=rt[:, NTA:2 * NTA])
                    eng.tensor_add(out=obuf[:, (2 * hp + 1) * NTA:(2 * hp + 2) * NTA],
                                   in0=rt[:, 2 * NTA:3 * NTA],
                                   in1=rt[:, 3 * NTA:4 * NTA]).then_inc(sems.a_rope, 1)

    def a1_endstate():
        return [(sems.a_vg, 16 * NTILESA), (sems.a_rope, 8 * NTILESA),
                (sems.a_vge, 16 * NTILESA), (sems.a_wstf, 64),
                (sems.a_oq, 16 * 2 * NTILESA), (sems.a_ov, 16 * 2 * NTILESA),
                (sems.a_x0, 16 * NTILESA // 2), (sems.a_x1, 16 * NTILESA // 2),
                (sems.a_cs0, 16 * NTILESA), (sems.a_cs1, 16 * NTILESA),
                (sems.a_cst, 16), (sems.a_nfm, 2 * NTILESA),
                (sems.a_pree, 28), (sems.a_qke, 8 * NTILESA),
                (sems.a_xn, NTILESA), (sems.a_tp, 2 * NTILESA),
                (sems.a_qk, 8 * NTILESA), (sems.a_nrm, 2 * NTILESA)]

    def b_sync(eng):
        eng.dma_start(out=B["gnw"][:, :], in_=gnwcol_d).then_inc(sems.b_cst, 16)
        eng.dma_start(out=B["dmask"][:, :].rearrange("p (h i) -> p h i", h=H),
                      in_=dmaskt_d.rearrange("h p i -> p h i")).then_inc(sems.b_cst, 16)
        eng.dma_start(out=B["cross"][:, :].rearrange("p (h i) -> p h i", h=H),
                      in_=crossb_d.rearrange("h p i -> p h i")).then_inc(sems.b_cst, 16)
        eng.dma_start(out=B["kdec"][:, :], in_=kdeccol_d).then_inc(sems.b_cst, 16)
        for n in range(NCH):
            sl = n % 2
            if n >= 2:
                eng.wait_ge(sems.b_stt, 4 * (n - 1))
            eng.dma_start(
                out=B["q"][:, sl * 1024:(sl + 1) * 1024]
                    .rearrange("p (kc t) -> p kc t", kc=8),
                in_=qp_d[:, :, n * CB:(n + 1) * CB].rearrange("kc p t -> p kc t"),
            ).then_inc([sems.b_iq0, sems.b_iq1][sl], 16)
            eng.dma_start(
                out=B["k"][:, sl * 1024:(sl + 1) * 1024]
                    .rearrange("p (kc t) -> p kc t", kc=8),
                in_=kp_d[:, :, n * CB:(n + 1) * CB].rearrange("kc p t -> p kc t"),
            ).then_inc([sems.b_ik0, sems.b_ik1][sl], 16)
            eng.dma_start(
                out=B["v"][:, sl * 2048:(sl + 1) * 2048],
                in_=vtm_d[n * CB:(n + 1) * CB, :],
            ).then_inc([sems.b_iv0, sems.b_iv1][sl], 16)
            if n >= 2:
                eng.wait_ge(sems.b_og, 4 * (n - 1))
            eng.dma_start(
                out=B["sgs"][:, sl * 2048:(sl + 1) * 2048],
                in_=sgtm_d[n * CB:(n + 1) * CB, :],
            ).then_inc([sems.b_isg0, sems.b_isg1][sl], 16)
            if n >= 2:
                eng.wait_ge(sems.b_comb, n - 1)
            eng.dma_start(
                out=B["xs"][:, sl * D:(sl + 1) * D],
                in_=x_d[n * CB:(n + 1) * CB, :],
            ).then_inc([sems.b_ix0, sems.b_ix1][sl], 16)
            eng.dma_start(
                out=B["gfs"][:, sl * D:(sl + 1) * D],
                in_=gf_d[n * CB:(n + 1) * CB, :],
            ).then_inc([sems.b_igf0, sems.b_igf1][sl], 16)

    def b_pool(eng):
        eng.dma_start(out=B["wraw"][:, :].rearrange("p (kc n) -> p kc n", kc=16),
                      in_=wo_d.rearrange("(kc p) n -> p kc n", p=128)
                      ).then_inc(sems.b_wstf, 16)
        eng.memset(B["epsc"][:, :], EPS).then_inc(sems.b_rc1, 1)
        eng.memset(B["Sf"][:, :], 0.0)
        eng.memset(B["Sb"][:, :], 0.0).then_inc(sems.b_sbf, 4)
        for n in range(NCH):
            sl = n % 2
            eng.wait_ge(sems.b_woev, n + 1)
            eng.wait_ge([sems.b_ix0, sems.b_ix1][sl], 16 * (n // 2 + 1))
            if n >= 2:
                eng.wait_ge([sems.b_oo0, sems.b_oo1][sl], 16 * (n // 2))
            eng.tensor_add(out=B["outt"][:, sl * D:(sl + 1) * D],
                           in0=B["t1"][:, :],
                           in1=B["xs"][:, sl * D:(sl + 1) * D]).then_inc(sems.b_comb, 1)
            eng.wait_ge(sems.b_comb, n + 1)
            eng.dma_start(out=out_d[n * CB:(n + 1) * CB, :],
                          in_=B["outt"][:, sl * D:(sl + 1) * D]).then_inc(
                              [sems.b_oo0, sems.b_oo1][sl], 16)
        eng.wait_ge(sems.b_oo0, 16 * NCH // 2)
        eng.wait_ge(sems.b_oo1, 16 * NCH // 2)

    def b_tensor(eng):
        def ogtp_wo(m):
            for w in range(2):
                eng.wait_ge(sems.b_og, 4 * m + 2 * (w + 1))
                if w == 0:
                    eng.wait_ge(sems.b_k2, 4 * m + 4)
                if 2 * m + w >= 1:
                    eng.wait_ge(sems.b_oge, 2 * m + w)
                for j in range(8):
                    fc = w * 8 + j
                    a = tp(eng, B["OGB"][:, j * 128:(j + 1) * 128],
                           B["og"][:, fc * 128:(fc + 1) * 128],
                           identb[:, :], start=(j == 0), stop=(j == 7))
                a.then_inc(sems.b_ogt, 1)
            eng.wait_ge(sems.b_oge, 2 * m + 2)
            if m == 0:
                eng.wait_ge(sems.b_wsc, 16)
            else:
                eng.wait_ge(sems.b_woev, m)
            for ng in range(2):
                for fc in range(16):
                    a = mm(eng, B["P"][4 + ng][:, :],
                           B["ogfm"][:, fc * 128:(fc + 1) * 128],
                           B["wob"][:, fc * D + ng * 512: fc * D + ng * 512 + 512],
                           start=(fc == 0), stop=(fc == 15))
            a.then_inc(sems.b_wo, 1)

        for n in range(NCH):
            sl = n % 2
            eng.wait_ge([sems.b_iq0, sems.b_iq1][sl], 16 * (n // 2 + 1))
            eng.wait_ge([sems.b_ik0, sems.b_ik1][sl], 16 * (n // 2 + 1))
            for h in range(H):
                gk = 4 * n + h
                if h == 0 and n >= 1:
                    eng.wait_ge(sems.b_stt, 4 * n)
                if h >= 2:
                    eng.wait_ge(sems.b_mask, 4 * n + h - 1)
                for c in range(2):
                    a = mm(eng, B["P"][2 + h % 2][:, 0:CB],
                           B["k"][:, sl * 1024 + (2 * h + c) * CB: sl * 1024 + (2 * h + c + 1) * CB],
                           B["q"][:, sl * 1024 + (2 * h + c) * CB: sl * 1024 + (2 * h + c + 1) * CB],
                           start=(c == 0), stop=(c == 1))
                a.then_inc(sems.b_pe, 1)
            if n >= 1:
                ogtp_wo(n - 1)
            for h in range(H):
                gk = 4 * n + h
                if gk >= 1:
                    eng.wait_ge(sems.b_k2, gk)
                for c in range(2):
                    a = tp(eng, B["KTB"][:, c * 128:(c + 1) * 128],
                           B["k"][:, sl * 1024 + (2 * h + c) * CB: sl * 1024 + (2 * h + c + 1) * CB],
                           identb[:, :], start=(c == 0), stop=(c == 1))
                a.then_inc(sems.b_pe, 1)
            eng.wait_ge([sems.b_iv0, sems.b_iv1][sl], 16 * (n // 2 + 1))
            for h in range(H):
                gk = 4 * n + h
                eng.wait_ge(sems.b_mask, gk + 1)
                eng.wait_ge(sems.b_q2, gk + 1)
                eng.wait_ge(sems.b_sbf, 4 * n + h + 1)
                if gk >= 2:
                    eng.wait_ge(sems.b_og, gk - 1)
                mm(eng, B["P"][h % 2][:, :], B["at"][:, h * CB:(h + 1) * CB],
                   B["v"][:, sl * 2048 + h * DV: sl * 2048 + (h + 1) * DV],
                   start=True, stop=False)
                for c in range(2):
                    mm(eng, B["P"][h % 2][:, :],
                       B["q2"][:, (2 * h + c) * CB:(2 * h + c + 1) * CB],
                       B["Sb"][:, h * 1024 + c * 512: h * 1024 + (c + 1) * 512],
                       start=False, stop=(c == 1))
                if h == 0:
                    eng.wait_ge(sems.b_mask, 4 * n + 4)
                if gk >= 1:
                    eng.wait_ge(sems.b_stt, gk)
                eng.wait_ge(sems.b_k2, gk + 1)
                for c in range(2):
                    a = mm(eng, B["P"][2 + c][:, :],
                           B["k2"][:, h * DK + c * CB: h * DK + (c + 1) * CB],
                           B["v"][:, sl * 2048 + h * DV: sl * 2048 + (h + 1) * DV],
                           start=True, stop=True)
                a.then_inc(sems.b_pe, 1)
        ogtp_wo(NCH - 1)

    def b_act(eng):
        eng.wait_ge(sems.b_cst, 64)
        eng.wait_ge(sems.b_rc1, 1)
        for n in range(NCH):
            for h in range(H):
                gk = 4 * n + h
                eng.wait_ge(sems.b_pe, 12 * n + 4 + h + 1)
                eng.activation(out=B["k2"][:, h * DK:(h + 1) * DK],
                               in_=B["KTB"][:, 0:256],
                               func=AF.Identity,
                               scale=B["kdec"][:, h:h + 1]).then_inc(sems.b_k2, 1)
            for h in range(H):
                gk = 4 * n + h
                eng.wait_ge(sems.b_pe, 12 * n + 8 + h + 1)
                if gk >= 4:
                    eng.wait_ge(sems.b_rc3, gk - 3)
                eng.activation(out=B["sq"][:, :], in_=B["P"][h % 2][:, :],
                               func=AF.Square,
                               accum_out=B["cols"][:, h:h + 1]).then_inc(sems.b_sq, 1)
                eng.drain()
                eng.activation(out=B["cols"][:, 8 + h:9 + h], in_=B["cols"][:, h:h + 1],
                               func=AF.Sqrt, scale=1.0 / DV,
                               bias=B["epsc"][:, 0:1]).then_inc(sems.b_rc2, 1)
            for h in range(H):
                gk = 4 * n + h
                eng.wait_ge(sems.b_stt, gk + 1)
                eng.activation(out=B["Sb"][:, h * 1024:(h + 1) * 1024],
                               in_=B["Sf"][:, (n % 2) * 4096 + h * 1024:
                                           (n % 2) * 4096 + (h + 1) * 1024],
                               func=AF.Identity).then_inc(sems.b_sbf, 1)
            for w in range(2):
                eng.wait_ge(sems.b_ogt, 2 * n + w + 1)
                if n >= 1 and w == 0:
                    eng.wait_ge(sems.b_wo, n)
                eng.activation(out=B["ogfm"][:, w * 1024:(w + 1) * 1024],
                               in_=B["OGB"][:, :],
                               func=AF.Identity).then_inc(sems.b_oge, 1)

    def b_vector(eng):
        def so_t1(m):
            ms = m % 2
            eng.wait_ge(sems.b_wo, m + 1)
            if m >= 1:
                eng.wait_ge(sems.b_comb, m)
            eng.tensor_add(out=B["so"][:, 0:512], in0=B["P"][4][:, :],
                           in1=B["xn"][:, ms * D:ms * D + 512])
            eng.tensor_add(out=B["so"][:, 512:D], in0=B["P"][5][:, :],
                           in1=B["xn"][:, ms * D + 512:(ms + 1) * D])
            eng.wait_ge([sems.b_igf0, sems.b_igf1][ms], 16 * (m // 2 + 1))
            eng.tensor_mul(out=B["t1"][:, :], in0=B["so"][:, :],
                           in1=B["gfs"][:, ms * D:(ms + 1) * D]).then_inc(sems.b_woev, 1)

        eng.wait_ge(sems.b_cst, 64)
        eng.wait_ge(sems.b_wstf, 16)
        for kc in range(16):
            eng.tensor_scalar(out=B["wob"][:, kc * D:(kc + 1) * D],
                              in0=B["wraw"][:, kc * D:(kc + 1) * D],
                              scalar1=B["gnw"][:, kc % 4:kc % 4 + 1], scalar2=None,
                              op0=ALU.mult).then_inc(sems.b_wsc, 1)
        for n in range(NCH):
            sl = n % 2
            eng.wait_ge([sems.b_ix0, sems.b_ix1][sl], 16 * (n // 2 + 1))
            eng.tensor_scalar(out=B["xn"][:, sl * D:(sl + 1) * D],
                              in0=B["xs"][:, sl * D:(sl + 1) * D],
                              scalar1=negm1[:, n:n + 1], scalar2=rstd1[:, n:n + 1],
                              op0=ALU.add, op1=ALU.mult)
            for h in range(H):
                gk = 4 * n + h
                eng.wait_ge(sems.b_pe, 12 * n + h + 1)
                eng.tensor_mul(out=B["at"][:, h * CB:(h + 1) * CB],
                               in0=B["P"][2 + h % 2][:, 0:CB],
                               in1=B["dmask"][:, h * CB:(h + 1) * CB]
                               ).then_inc(sems.b_mask, 1)
            eng.wait_ge([sems.b_iq0, sems.b_iq1][sl], 16 * (n // 2 + 1))
            for h in range(H):
                for c in range(2):
                    a = eng.tensor_mul(
                        out=B["q2"][:, (2 * h + c) * CB:(2 * h + c + 1) * CB],
                        in0=B["q"][:, sl * 1024 + (2 * h + c) * CB: sl * 1024 + (2 * h + c + 1) * CB],
                        in1=B["cross"][:, h * CB:(h + 1) * CB])
                a.then_inc(sems.b_q2, 1)
            if n >= 1:
                so_t1(n - 1)
            for h in range(H):
                gk = 4 * n + h
                eng.wait_ge(sems.b_rc2, gk + 1)
                eng.reciprocal(out=B["cols"][:, 12 + h:13 + h],
                               in_=B["cols"][:, 8 + h:9 + h]).then_inc(sems.b_rc3, 1)
                eng.drain()
                eng.wait_ge([sems.b_isg0, sems.b_isg1][sl], 16 * (n // 2 + 1))
                eng.tensor_scalar(out=B["sgn"][:, :],
                                  in0=B["sgs"][:, sl * 2048 + h * DV: sl * 2048 + (h + 1) * DV],
                                  scalar1=B["cols"][:, 12 + h:13 + h], scalar2=None,
                                  op0=ALU.mult)
                eng.wait_ge(sems.b_pe, 12 * n + 8 + h + 1)
                if n == 0:
                    eng.wait_ge(sems.b_sbf, 4)
                for c in range(2):
                    a = eng.scalar_tensor_tensor(
                        out=B["Sf"][:, (n % 2) * 4096 + h * 1024 + c * 512:
                                    (n % 2) * 4096 + h * 1024 + (c + 1) * 512],
                        in0=B["Sf"][:, ((n + 1) % 2) * 4096 + h * 1024 + c * 512:
                                    ((n + 1) % 2) * 4096 + h * 1024 + (c + 1) * 512],
                        scalar=GC[h], in1=B["P"][2 + c][:, :],
                        op0=ALU.mult, op1=ALU.add)
                a.then_inc(sems.b_stt, 1)
                if n >= 1:
                    eng.wait_ge(sems.b_ogt, 2 * n - 1 if h < 2 else 2 * n)
                eng.tensor_mul(out=B["og"][:, h * DV:(h + 1) * DV],
                               in0=B["P"][h % 2][:, :],
                               in1=B["sgn"][:, :]).then_inc(sems.b_og, 1)
        so_t1(NCH - 1)

    def c_endstate():
        nxn = 16 * NTILES + 16
        if "dispdbg" in dbg:
            nxn += 16
        if "statsdbg" in dbg:
            nxn += 64
        return [(sems.c_dtpe, 4 * NTILES), (sems.c_lge, 8 * NTILES),
                (sems.c_dtp, 4 * NTILES), (sems.s_wstgf, 80),
                (sems.s_oxn, nxn),
                (sems.s_osh0, 16 * 2 * NTILES), (sems.s_osh1, 16 * 2 * NTILES),
                (sems.s_cst, 64), (sems.s_in, 16 * NTILES)]

    def de_endstate(pi):
        t = DE_TOT
        out = [(sems.d_acc, pi * t["acc"]), (sems.d_ev, pi * t["ev"]),
               (sems.d_silu, pi * t["silu"]), (sems.d_wstf, pi * t["wstf"]),
               (sems.d_out0, pi * t["out"]), (sems.d_out1, pi * t["out"]),
               (sems.d_fm, pi * t["fm"]),
               (sems.d_dp0, pi * t["dp"]), (sems.d_dp1, pi * t["dp"]),
               (sems.d_st0, pi * t["st"]), (sems.d_st1, pi * t["st"]),
               (sems.d_hsc, pi * t["hsc"]), (sems.d_bc, pi * t["bc"]),
               (sems.d_h, pi * t["h"])]
        if pi >= 2:
            out.append((sems.d_gf, t["gf"]))
        return out

    def wait_list(eng, lst):
        for s, v in lst:
            eng.wait_ge(s, v)

    with nc.Block() as block:
        @block.sync
        def _(eng):
            if "C" in phases:
                c_sync(eng)
            if "D" in phases:
                wait_list(eng, c_endstate())
                de_sync(eng, de_bases(0), 0, sh_d)
            if "Ep" in phases:
                wait_list(eng, de_endstate(1))
                de_sync(eng, de_bases(1), 8, moeps_d)
            if "A1" in phases:
                wait_list(eng, de_endstate(2))
                a1_sync(eng)
            if "B" in phases:
                wait_list(eng, a1_endstate())
                b_sync(eng)

        @block.tensor
        def _(eng):
            if "C" in phases:
                c_tensor(eng)
            if "D" in phases:
                wait_list(eng, c_endstate())
                de_tensor(eng, de_bases(0), False)
            if "Ep" in phases:
                wait_list(eng, de_endstate(1))
                de_tensor(eng, de_bases(1), True)
            if "A1" in phases:
                wait_list(eng, de_endstate(2))
                a1_tensor(eng)
            if "B" in phases:
                wait_list(eng, a1_endstate())
                b_tensor(eng)

        @block.vector
        def _(eng):
            if "C" in phases:
                c_vector(eng)
            if "D" in phases:
                wait_list(eng, c_endstate())
                de_vector(eng, de_bases(0), False)
            if "Ep" in phases:
                wait_list(eng, de_endstate(1))
                de_vector(eng, de_bases(1), True)
            if "A1" in phases:
                wait_list(eng, de_endstate(2))
                a1_vector(eng)
            if "B" in phases:
                wait_list(eng, a1_endstate())
                b_vector(eng)

        @block.scalar
        def _(eng):
            if "C" in phases:
                c_act(eng)
            if "D" in phases:
                wait_list(eng, c_endstate())
                de_act(eng, de_bases(0), False)
            if "Ep" in phases:
                wait_list(eng, de_endstate(1))
                de_act(eng, de_bases(1), True)
            if "A1" in phases:
                wait_list(eng, de_endstate(2))
                a1_act(eng)
            if "B" in phases:
                wait_list(eng, a1_endstate())
                b_act(eng)

        @block.gpsimd
        def _(eng):
            if "C" in phases:
                c_pool(eng)
                c_gpsimd_out(eng)
            if "D" in phases:
                wait_list(eng, c_endstate())
                de_pool(eng, de_bases(0), False)
            if "Ep" in phases:
                wait_list(eng, de_endstate(1))
                de_pool(eng, de_bases(1), True)
            if "A1" in phases:
                wait_list(eng, de_endstate(2))
                a1_pool(eng)
            if "B" in phases:
                wait_list(eng, a1_endstate())
                b_pool(eng)

    es.close()
    return nc


_consts = None


def _in_maps(inputs):
    global _consts
    if _consts is None:
        _consts = _host_constants()
    lay = _host_layouts(inputs)
    x = np.ascontiguousarray(np.asarray(inputs["L0_attn_x"], dtype=np.float32))
    base = dict(lay)
    base.update(_consts)
    maps = []
    for c in range(8):
        m = dict(base)
        m["x"] = np.ascontiguousarray(x[c])
        maps.append(m)
    return maps


def kernel(**inputs):
    nc = build_program()
    maps = _in_maps(inputs)
    res = run_bass_kernel_spmd(nc, maps, core_ids=list(range(8)))
    return np.stack([res.results[c]["out"] for c in range(8)], axis=0)


# version 06
# speedup vs baseline: 1.6021x; 1.0524x over previous
import numpy as np
import ml_dtypes
import sys

sys.path.insert(0, "/opt/trn_rl_repo")
import concourse.bass as bass
import concourse.mybir as mybir
from concourse.bass_utils import run_bass_kernel_spmd
from contextlib import ExitStack

F32 = mybir.dt.float32
BF16 = mybir.dt.bfloat16
AF = mybir.ActivationFunctionType
ALU = mybir.AluOpType
AX = mybir.AxisListType

T = 4096
D = 1024
H = 4
DK = 256
DV = 512
E = 16
HID = 512
S2 = 2048
CB = 128
NCH = T // CB
NT = 512
NTILES = T // NT
NTA = 256
NTILESA = T // NTA
EPS = 1e-5
GAMMA = [1.0 - 2.0 ** (-5.0 - h) for h in range(H)]
GC = [g ** CB for g in GAMMA]

PHASES = ["C", "D", "Ep", "A1", "B"]


def _host_constants():
    c = {}
    inv_freq = 1.0 / 10000.0 ** (np.arange(0, DK, 2, dtype=np.float64) / DK)
    ang = np.arange(T, dtype=np.float64)[None, :] * inv_freq[:, None]
    c["cos_t"] = np.cos(ang).astype(np.float32)
    c["sin_t"] = np.sin(ang).astype(np.float32)
    c["cosq_t"] = (np.cos(ang) * DK ** -0.5).astype(np.float32)
    c["sinq_t"] = (np.sin(ang) * DK ** -0.5).astype(np.float32)
    i = np.arange(CB, dtype=np.float64)
    dmt = np.zeros((H, CB, CB), np.float32)
    crs = np.zeros((H, CB, CB), np.float32)
    kdc = np.zeros((CB, H), np.float32)
    for h in range(H):
        g = GAMMA[h]
        diff = i[:, None] - i[None, :]
        m = np.where(diff >= 0, g ** np.maximum(diff, 0), 0.0)
        dmt[h] = m.T.astype(np.float32)
        crs[h] = np.tile((g ** (i + 1.0)).astype(np.float32)[None, :], (CB, 1))
        kdc[:, h] = (g ** (CB - 1.0 - i)).astype(np.float32)
    c["dmaskt"] = dmt
    c["crossb"] = crs
    c["kdec_col"] = kdc
    c["ident_f"] = np.eye(128, dtype=np.float32)
    c["ident_b"] = np.eye(128).astype(ml_dtypes.bfloat16)
    c["ones_b"] = np.ones((1, 128), ml_dtypes.bfloat16)
    return c


def _host_layouts(inputs):
    d = {}
    d["wq"] = np.ascontiguousarray(inputs["L0_attn_Wq"], np.float32)
    d["wk"] = np.ascontiguousarray(inputs["L0_attn_Wk"], np.float32)
    d["wv"] = np.ascontiguousarray(inputs["L0_attn_Wv"], np.float32)
    d["wg"] = np.ascontiguousarray(inputs["L0_attn_Wg"], np.float32)
    d["wo"] = np.ascontiguousarray(inputs["L0_attn_Wo"], np.float32)
    d["sfc1t"] = np.ascontiguousarray(np.asarray(inputs["L0_ffn_sfc1"]).T, np.float32)
    d["sfc2t"] = np.ascontiguousarray(np.asarray(inputs["L0_ffn_sfc2"]).T, np.float32)
    d["sfc3t"] = np.ascontiguousarray(np.asarray(inputs["L0_ffn_sfc3"]).T, np.float32)
    d["w1t"] = np.ascontiguousarray(np.asarray(inputs["L0_ffn_w1"]).transpose(0, 2, 1), np.float32)
    d["w2t"] = np.ascontiguousarray(np.asarray(inputs["L0_ffn_w2"]).transpose(0, 2, 1), np.float32)
    d["gatet"] = np.ascontiguousarray(np.asarray(inputs["L0_ffn_gate_w"]).T, np.float32)
    d["lnw_bc"] = np.ascontiguousarray(
        np.broadcast_to(np.asarray(inputs["L0_attn_ln_w"])[None, :], (128, D)), np.float32)
    d["lnb_col"] = np.ascontiguousarray(
        np.asarray(inputs["L0_attn_ln_b"]).reshape(D // 128, 128).T, np.float32)
    d["gnw_col"] = np.ascontiguousarray(
        np.asarray(inputs["L0_attn_g_norm_w"]).reshape(DV // 128, 128).T, np.float32)
    return d


def mm(eng, out, lhsT, rhs, start, stop):
    return eng.matmul(out=out, lhsT=lhsT, rhs=rhs, start=start, stop=stop)


def tp(eng, out, in_, ident, start, stop):
    return eng.matmul(out=out, lhsT=in_, rhs=ident, is_transpose=True,
                      start=start, stop=stop)


def build_program(debug_outputs=(), phases=PHASES):
    nc = bass.Bass("TRN2", target_bir_lowering=False, debug=False,
                   detect_race_conditions=False)

    def dram_in(name, shape, dt=F32):
        return nc.dram_tensor(name, list(shape), dt, kind="ExternalInput").ap()

    def scratch(name, shape, dt):
        kind = "ExternalOutput" if name in debug_outputs else "Internal"
        return nc.dram_tensor(name, list(shape), dt, kind=kind).ap()

    x_d = dram_in("x", (T, D))
    wq_d = dram_in("wq", (D, D))
    wk_d = dram_in("wk", (D, D))
    wv_d = dram_in("wv", (D, 2 * D))
    wg_d = dram_in("wg", (D, 2 * D))
    wo_d = dram_in("wo", (2 * D, D))
    sfc1t_d = dram_in("sfc1t", (D, S2))
    sfc2t_d = dram_in("sfc2t", (D, S2))
    sfc3t_d = dram_in("sfc3t", (S2, D))
    w1t_d = dram_in("w1t", (E, D, HID))
    w2t_d = dram_in("w2t", (E, HID, D))
    gatet_d = dram_in("gatet", (D, E))
    lnwbc_d = dram_in("lnw_bc", (128, D))
    lnbcol_d = dram_in("lnb_col", (128, 8))
    gnwcol_d = dram_in("gnw_col", (128, 4))
    cos_d = dram_in("cos_t", (128, T))
    sin_d = dram_in("sin_t", (128, T))
    cosq_d = dram_in("cosq_t", (128, T))
    sinq_d = dram_in("sinq_t", (128, T))
    dmaskt_d = dram_in("dmaskt", (H, CB, CB))
    crossb_d = dram_in("crossb", (H, CB, CB))
    kdeccol_d = dram_in("kdec_col", (CB, H))
    identf_d = dram_in("ident_f", (128, 128))
    identb_d = dram_in("ident_b", (128, 128), BF16)
    onesb_d = dram_in("ones_b", (1, 128), BF16)

    out_d = nc.dram_tensor("out", [T, D], F32, kind="ExternalOutput").ap()

    xnfm_d = scratch("xnfm", (8, 128, T), BF16)
    sh_d = scratch("sh", (T, D), F32)
    moeps_d = scratch("moeps", (T, D), F32)
    gf_d = scratch("gf", (T, D), BF16)
    qp_d = scratch("qp", (8, 128, T), BF16)
    kp_d = scratch("kp", (8, 128, T), BF16)
    vtm_d = scratch("vtm", (T, 2 * D), BF16)
    dispt_d = scratch("dispt", (16, T), BF16)
    sgtm_d = scratch("sgtm", (T, 2 * D), BF16)
    dbg = {}
    if "dispdbg" in debug_outputs:
        dbg["dispdbg"] = nc.dram_tensor("dispdbg", [16, T], BF16,
                                        kind="ExternalOutput").ap()
    if "statsdbg" in debug_outputs:
        dbg["statsdbg"] = nc.dram_tensor("statsdbg", [128, 4 * NCH], F32,
                                         kind="ExternalOutput").ap()

    es = ExitStack()

    identf = nc.alloc_sbuf_tensor("identf", [128, 128], F32)
    identb = nc.alloc_sbuf_tensor("identb", [128, 128], BF16)
    onesb = nc.alloc_sbuf_tensor("onesb", [1, 128], BF16)
    negm1 = nc.alloc_sbuf_tensor("negm1", [128, NCH], F32)
    rstd1 = nc.alloc_sbuf_tensor("rstd1", [128, NCH], F32)
    negm2 = nc.alloc_sbuf_tensor("negm2", [128, NCH], F32)
    rstd2 = nc.alloc_sbuf_tensor("rstd2", [128, NCH], F32)
    lnbcol = nc.alloc_sbuf_tensor("lnbcol", [128, 8], F32)
    lnbcol_b = nc.alloc_sbuf_tensor("lnbcol_b", [128, 8], BF16)

    def mkphase(fn):
        with ExitStack() as ph:
            aa = lambda n, s, dt: ph.enter_context(nc.sbuf_tensor(n, s, dt))
            pp = lambda n, s, dt: ph.enter_context(nc.psum_tensor(n, s, dt))
            return fn(aa, pp)

    C = mkphase(lambda a, p: dict(
        sfc1b=a("C_sfc1b", [128, 8 * S2], BF16),
        sfc2b=a("C_sfc2b", [128, 8 * S2], BF16),
        sfc3b=a("C_sfc3b", [128, 16 * D], BF16),
        gateb=a("C_gateb", [128, 8 * E], BF16),
        x=a("C_x", [128, 4 * D], F32),
        xn=a("C_xn", [128, 4 * D], F32),
        fm=a("C_fm", [128, 8 * NT], BF16),
        hs=a("C_hs", [128, 16 * NT], BF16),
        scr=a("C_scr", [128, 3 * D], F32),
        silu=a("C_silu", [128, 2 * NT], F32),
        cols=a("C_cols", [128, 16], F32),
        lg=a("C_lg", [128, 16], F32),
        lg2=a("C_lg2", [128, 16], F32),
        eq1=a("C_eq1", [128, 16], F32),
        eq2=a("C_eq2", [128, 16], F32),
        disp=a("C_disp", [128, 16], F32),
        dispT=a("C_dispT", [16, T], BF16),
        mcol=a("C_mcol", [128, 8], F32),
        P=[p(f"C_P{i}", [128, 512], F32) for i in range(6)],
        TPB=[p(f"C_TPB{i}", [128, 512], F32) for i in range(2)],
    ))

    DE = mkphase(lambda a, p: dict(
        w1b=a("D_w1b", [128, 8 * 8 * 512], BF16),
        w2b=a("D_w2b", [128, 8 * 4 * 1024], BF16),
        fm=a("D_fm", [128, 8 * NT], BF16),
        hsc=a("D_hsc", [128, 8 * 4 * 512], BF16),
        stream=a("D_stream", [128, 2 * D], F32),
        outh=a("D_outh", [128, 2 * D], F32),
        gfh=a("D_gfh", [128, 2 * D], BF16),
        silu=a("D_silu", [128, 2 * 512], F32),
        dprow=a("D_dprow", [1, 2 * 512], BF16),
        P=[p(f"D_P{i}", [128, 512], F32) for i in range(8)],
    ))

    B = mkphase(lambda a, p: dict(
        wob=a("B_wob", [128, 16 * D], BF16),
        wraw=a("B_wraw", [128, 16 * D], BF16),
        gnw=a("B_gnw", [128, 4], F32),
        dmask=a("B_dmask", [128, 4 * CB], F32),
        cross=a("B_cross", [128, 4 * CB], F32),
        kdec=a("B_kdec", [128, 4], F32),
        Sf=a("B_Sf", [128, 2 * H * 2 * DV], F32),
        Sb=a("B_Sb", [128, H * 2 * DV], BF16),
        q=a("B_q", [128, 2 * 8 * CB], BF16),
        k=a("B_k", [128, 2 * 8 * CB], BF16),
        v=a("B_v", [128, 2 * 2048], BF16),
        sgs=a("B_sgs", [128, 2 * 2048], BF16),
        xs=a("B_xs", [128, 2 * D], F32),
        gfs=a("B_gfs", [128, 2 * D], BF16),
        xn=a("B_xn", [128, 2 * D], F32),
        q2=a("B_q2", [128, 4 * 2 * CB], BF16),
        k2=a("B_k2", [128, 4 * DK], BF16),
        at=a("B_at", [128, 4 * CB], BF16),
        sgn=a("B_sgn", [128, DV], BF16),
        epsc=a("B_epsc", [128, 1], F32),
        og=a("B_og", [128, H * DV], BF16),
        ogfm=a("B_ogfm", [128, 16 * 128], BF16),
        sq=a("B_sq", [128, DV], F32),
        cols=a("B_cols", [128, 16], F32),
        so=a("B_so", [128, D], F32),
        t1=a("B_t1", [128, D], F32),
        outt=a("B_outt", [128, 2 * D], F32),
        P=[p(f"B_P{i}", [128, 512], F32) for i in range(6)],
        KTB=p("B_KTB", [128, 1024], BF16),
        OGB=p("B_OGB", [128, 1024], BF16),
    ))

    sem_names = [
        "s_in", "s_cst", "s_oxn", "s_osh0", "s_osh1",
        "s_wstgf",
        "c_st1", "c_c1", "c_sq1", "c_xn", "c_st2", "c_c2", "c_sq2",
        "c_tp", "c_fme", "c_ab", "c_silu", "c_hs", "c_s3", "c_s3e",
        "c_lg", "c_lge", "c_disp", "c_dtp", "c_dtpe",
        "d_wstf", "d_fm", "d_st0", "d_st1",
        "d_dp0", "d_dp1", "d_h", "d_bc", "d_silu", "d_hsc", "d_acc", "d_ev",
        "d_gf", "d_out0", "d_out1",
        "a_wstf", "a_cst", "a_x0", "a_x1", "a_cs0", "a_cs1", "a_xn",
        "a_nrm", "a_tp", "a_nfm", "a_qk", "a_qke", "a_rope", "a_vg", "a_vge",
        "a_oq", "a_ov", "a_pre", "a_pree",
        "b_cst", "b_wstf", "b_wsc",
        "b_iq0", "b_iq1", "b_ik0", "b_ik1", "b_iv0", "b_iv1",
        "b_isg0", "b_isg1", "b_ix0", "b_ix1", "b_igf0", "b_igf1",
        "b_pe", "b_mask", "b_q2", "b_k2", "b_sq", "b_rc1", "b_rc2", "b_rc3",
        "b_on", "b_og", "b_stt", "b_sbf", "b_ogt", "b_oge", "b_wo", "b_woev",
        "b_comb", "b_oo0", "b_oo1",
    ]

    class _S:
        pass
    sems = _S()
    for n in sem_names:
        setattr(sems, n, es.enter_context(nc.semaphore(n)))

    C_WCHUNKS = 8 + 8 + 16 + 1

    def c_sync(eng):
        eng.dma_start(out=identf[:, :], in_=identf_d).then_inc(sems.s_cst, 16)
        eng.dma_start(out=identb[:, :], in_=identb_d).then_inc(sems.s_cst, 16)
        eng.dma_start(out=onesb[:, :], in_=onesb_d).then_inc(sems.s_cst, 16)
        eng.dma_start(out=lnbcol[:, :], in_=lnbcol_d).then_inc(sems.s_cst, 16)
        for i in range(NTILES):
            if i >= 1:
                eng.wait_ge(sems.c_xn, i)
            eng.dma_start(
                out=C["x"][:, :].rearrange("p (s d) -> p s d", s=4),
                in_=x_d[i * NT:(i + 1) * NT, :].rearrange("(s p) d -> p s d", p=128),
            ).then_inc(sems.s_in, 16)

    def c_pool(eng):
        eng.dma_start(out=C["sfc1b"][:, :].rearrange("p (kc n) -> p kc n", kc=8),
                      in_=sfc1t_d.rearrange("(kc p) n -> p kc n", p=128)
                      ).then_inc(sems.s_wstgf, 16)
        eng.dma_start(out=C["sfc2b"][:, :].rearrange("p (kc n) -> p kc n", kc=8),
                      in_=sfc2t_d.rearrange("(kc p) n -> p kc n", p=128)
                      ).then_inc(sems.s_wstgf, 16)
        eng.dma_start(out=C["sfc3b"][:, :].rearrange("p (sc n) -> p sc n", sc=16),
                      in_=sfc3t_d.rearrange("(sc p) n -> p sc n", p=128)
                      ).then_inc(sems.s_wstgf, 16)
        eng.dma_start(out=C["gateb"][:, :].rearrange("p (kc e) -> p kc e", kc=8),
                      in_=gatet_d.rearrange("(kc p) e -> p kc e", p=128)
                      ).then_inc(sems.s_wstgf, 16)
        eng.dma_start(out=lnbcol_b[:, :], in_=lnbcol_d).then_inc(sems.s_wstgf, 16)

    def c_act(eng):
        def ln_part(i):
            eng.wait_ge(sems.s_in, 16 * (i + 1))
            for st in range(4):
                eng.activation(out=C["scr"][:, 0:D], in_=C["x"][:, st * D:(st + 1) * D],
                               func=AF.Square, accum_out=C["cols"][:, st:st + 1])
                a = eng.activation(out=C["scr"][:, 0:D], in_=C["x"][:, st * D:(st + 1) * D],
                                   func=AF.Identity, accum_out=C["cols"][:, 4 + st:4 + st + 1])
            a.then_inc(sems.c_st1, 1)
            eng.wait_ge(sems.c_c1, i + 1)
            eng.activation(out=C["mcol"][:, 0:4], in_=C["cols"][:, 8:12],
                           func=AF.Sqrt).then_inc(sems.c_sq1, 1)
            eng.wait_ge(sems.c_xn, i + 1)
            for st in range(4):
                eng.activation(out=C["scr"][:, 0:D], in_=C["xn"][:, st * D:(st + 1) * D],
                               func=AF.Square, accum_out=C["cols"][:, st:st + 1])
                a = eng.activation(out=C["scr"][:, 0:D], in_=C["xn"][:, st * D:(st + 1) * D],
                                   func=AF.Identity, accum_out=C["cols"][:, 4 + st:4 + st + 1])
            a.then_inc(sems.c_st2, 1)
            eng.wait_ge(sems.c_c2, i + 1)
            eng.activation(out=C["mcol"][:, 4:8], in_=C["cols"][:, 8:12],
                           func=AF.Sqrt).then_inc(sems.c_sq2, 1)

        def silu_part(i):
            for sc in range(16):
                Ksc = i * 16 + sc
                eng.wait_ge(sems.c_ab, Ksc + 1)
                if Ksc >= 2:
                    eng.wait_ge(sems.c_hs, Ksc - 1)
                eng.activation(out=C["silu"][:, (sc % 2) * NT:(sc % 2 + 1) * NT],
                               in_=C["P"][sc % 2][:, :],
                               func=AF.Silu).then_inc(sems.c_silu, 1)
            for mc in range(4):
                Km = i * 4 + mc
                eng.wait_ge(sems.c_disp, 2 * Km + 1)
                eng.activation(out=C["mcol"][:, 3:4], in_=C["mcol"][:, 2:3],
                               func=AF.Tanh, scale=0.5).then_inc(sems.c_dtp, 1)

        for it in range(NTILES + 1):
            if it < NTILES:
                ln_part(it)
            if it >= 1:
                silu_part(it - 1)

    def c_vector(eng):
        def ln_cols(i):
            eng.wait_ge(sems.c_st1, i + 1)
            nm = negm1[:, i * 4:(i + 1) * 4]
            eng.tensor_scalar(out=nm, in0=C["cols"][:, 4:8], scalar1=-1.0 / D,
                              scalar2=None, op0=ALU.mult)
            eng.drain()
            eng.tensor_mul(out=C["cols"][:, 12:16], in0=nm, in1=nm)
            eng.drain()
            eng.scalar_tensor_tensor(out=C["cols"][:, 8:12], in0=C["cols"][:, 0:4],
                                     scalar=1.0 / D, in1=C["cols"][:, 12:16],
                                     op0=ALU.mult, op1=ALU.subtract)
            eng.drain()
            eng.tensor_scalar(out=C["cols"][:, 8:12], in0=C["cols"][:, 8:12],
                              scalar1=EPS, scalar2=None,
                              op0=ALU.add).then_inc(sems.c_c1, 1)
            eng.wait_ge(sems.c_sq1, i + 1)
            eng.reciprocal(out=rstd1[:, i * 4:(i + 1) * 4], in_=C["mcol"][:, 0:4])
            eng.drain()
            for st in range(4):
                c = i * 4 + st
                a = eng.tensor_scalar(out=C["xn"][:, st * D:(st + 1) * D],
                                      in0=C["x"][:, st * D:(st + 1) * D],
                                      scalar1=negm1[:, c:c + 1], scalar2=rstd1[:, c:c + 1],
                                      op0=ALU.add, op1=ALU.mult)
            a.then_inc(sems.c_xn, 1)
            eng.wait_ge(sems.c_st2, i + 1)
            nm2 = negm2[:, i * 4:(i + 1) * 4]
            eng.tensor_scalar(out=nm2, in0=C["cols"][:, 4:8], scalar1=-1.0 / D,
                              scalar2=None, op0=ALU.mult)
            eng.drain()
            eng.tensor_mul(out=C["cols"][:, 12:16], in0=nm2, in1=nm2)
            eng.drain()
            eng.scalar_tensor_tensor(out=C["cols"][:, 8:12], in0=C["cols"][:, 0:4],
                                     scalar=1.0 / D, in1=C["cols"][:, 12:16],
                                     op0=ALU.mult, op1=ALU.subtract)
            eng.drain()
            eng.tensor_scalar(out=C["cols"][:, 8:12], in0=C["cols"][:, 8:12],
                              scalar1=EPS, scalar2=None,
                              op0=ALU.add).then_inc(sems.c_c2, 1)
            eng.wait_ge(sems.c_sq2, i + 1)
            eng.reciprocal(out=rstd2[:, i * 4:(i + 1) * 4], in_=C["mcol"][:, 4:8])

        def fm_evacs(i):
            for g in range(8):
                K = i * 8 + g
                st, gg = g // 2, g % 2
                eng.wait_ge(sems.c_tp, K + 1)
                if i > 0 and g == 0:
                    eng.wait_ge(sems.s_oxn, 16 * i)
                eng.tensor_copy(
                    out=C["fm"][:, :].rearrange("p (kc t) -> p kc t", kc=8)
                        [:, gg * 4:(gg + 1) * 4, st * 128:(st + 1) * 128],
                    in_=C["TPB"][K % 2][:, :].rearrange("p (s f) -> p s f", s=4)
                ).then_inc(sems.c_fme, 1)

        def work_part(i):
            for sc in range(16):
                Ksc = i * 16 + sc
                eng.wait_ge(sems.c_silu, Ksc + 1)
                eng.tensor_mul(out=C["hs"][:, sc * NT:(sc + 1) * NT],
                               in0=C["silu"][:, (sc % 2) * NT:(sc % 2 + 1) * NT],
                               in1=C["P"][2 + sc % 2][:, :]).then_inc(sems.c_hs, 1)
            for g in range(8):
                K = i * 8 + g
                mc, ng = g // 2, g % 2
                eng.wait_ge(sems.c_s3, K + 1)
                if ng == 0:
                    occ = 2 * i + mc // 2
                    if occ > 0:
                        eng.wait_ge([sems.s_osh0, sems.s_osh1][mc % 2], 16 * occ)
                off = (1 + mc % 2) * D + ng * 512
                eng.tensor_copy(out=C["scr"][:, off:off + 512],
                                in_=C["P"][4 + ng][:, :]).then_inc(sems.c_s3e, 1)
            for mc in range(4):
                Km = i * 4 + mc
                eng.wait_ge(sems.c_lg, Km + 1)
                eng.tensor_copy(out=C["lg"][:, :],
                                in_=C["P"][4][:, 0:16]).then_inc(sems.c_lge, 1)
                eng.drain()
                eng.reduce_max(out=C["mcol"][:, 0:1], in_=C["lg"][:, :], axis=AX.X)
                eng.drain()
                eng.tensor_scalar(out=C["eq1"][:, :], in0=C["lg"][:, :],
                                  scalar1=C["mcol"][:, 0:1], scalar2=None, op0=ALU.is_ge)
                eng.drain()
                eng.scalar_tensor_tensor(out=C["lg2"][:, :], in0=C["eq1"][:, :],
                                         scalar=-1e30, in1=C["lg"][:, :],
                                         op0=ALU.mult, op1=ALU.add)
                eng.drain()
                eng.reduce_max(out=C["mcol"][:, 1:2], in_=C["lg2"][:, :], axis=AX.X)
                eng.drain()
                eng.tensor_scalar(out=C["eq2"][:, :], in0=C["lg2"][:, :],
                                  scalar1=C["mcol"][:, 1:2], scalar2=None, op0=ALU.is_ge)
                eng.tensor_sub(out=C["mcol"][:, 2:3], in0=C["mcol"][:, 0:1],
                               in1=C["mcol"][:, 1:2]).then_inc(sems.c_disp, 1)
                eng.wait_ge(sems.c_dtp, Km + 1)
                eng.tensor_scalar(out=C["mcol"][:, 4:5], in0=C["mcol"][:, 3:4],
                                  scalar1=0.5, scalar2=0.5, op0=ALU.mult, op1=ALU.add)
                eng.drain()
                eng.tensor_scalar(out=C["mcol"][:, 5:6], in0=C["mcol"][:, 4:5],
                                  scalar1=-1.0, scalar2=1.0, op0=ALU.mult, op1=ALU.add)
                eng.tensor_scalar(out=C["disp"][:, :], in0=C["eq1"][:, :],
                                  scalar1=C["mcol"][:, 4:5], scalar2=None, op0=ALU.mult)
                eng.drain()
                eng.scalar_tensor_tensor(out=C["disp"][:, :], in0=C["eq2"][:, :],
                                         scalar=C["mcol"][:, 5:6], in1=C["disp"][:, :],
                                         op0=ALU.mult,
                                         op1=ALU.add).then_inc(sems.c_disp, 1)
                eng.wait_ge(sems.c_dtpe, Km + 1)
                eng.tensor_copy(out=C["dispT"][0:16, Km * 128:(Km + 1) * 128],
                                in_=C["P"][5][0:16, 0:128]).then_inc(sems.c_lge, 1)

        for it in range(NTILES + 1):
            if it < NTILES:
                ln_cols(it)
            if it >= 1:
                work_part(it - 1)
            if it < NTILES:
                fm_evacs(it)

    def c_tensor(eng):
        fmv = C["fm"][:, :].rearrange("p (kc t) -> p kc t", kc=8)
        hsv = C["hs"][:, :].rearrange("p (sc t) -> p sc t", sc=16)

        def transposes(i):
            eng.wait_ge(sems.c_xn, i + 1)
            for g in range(8):
                K = i * 8 + g
                st, gg = g // 2, g % 2
                if K >= 2:
                    eng.wait_ge(sems.c_fme, K - 1)
                for s in range(4):
                    kc = gg * 4 + s
                    a = tp(eng, C["TPB"][K % 2][:, s * 128:(s + 1) * 128],
                           C["xn"][:, st * D + kc * 128: st * D + (kc + 1) * 128],
                           identf[:, :], start=(s == 0), stop=(s == 3))
                a.then_inc(sems.c_tp, 1)

        def sfc(i):
            eng.wait_ge(sems.c_fme, (i + 1) * 8)
            if i == 0:
                eng.wait_ge(sems.s_wstgf, 32)
            for sc in range(16):
                Ksc = i * 16 + sc
                if Ksc >= 2:
                    eng.wait_ge(sems.c_silu, Ksc - 1)
                    eng.wait_ge(sems.c_hs, Ksc - 1)
                for kc in range(8):
                    mm(eng, C["P"][sc % 2][:, :],
                       C["sfc1b"][:, kc * 2048 + sc * 128: kc * 2048 + (sc + 1) * 128],
                       fmv[:, kc, :], start=(kc == 0), stop=(kc == 7))
                for kc in range(8):
                    a = mm(eng, C["P"][2 + sc % 2][:, :],
                           C["sfc2b"][:, kc * 2048 + sc * 128: kc * 2048 + (sc + 1) * 128],
                           fmv[:, kc, :], start=(kc == 0), stop=(kc == 7))
                a.then_inc(sems.c_ab, 1)
            eng.wait_ge(sems.c_hs, (i + 1) * 16)
            if i == 0:
                eng.wait_ge(sems.s_wstgf, 48)
            for g in range(8):
                K = i * 8 + g
                mc, ng = g // 2, g % 2
                if K >= 2:
                    eng.wait_ge(sems.c_s3e, K - 1)
                for sc in range(16):
                    a = mm(eng, C["P"][4 + ng][:, :],
                           hsv[:, sc, mc * 128:(mc + 1) * 128],
                           C["sfc3b"][:, sc * D + ng * 512: sc * D + ng * 512 + 512],
                           start=(sc == 0), stop=(sc == 15))
                a.then_inc(sems.c_s3, 1)
            for mc in range(4):
                Km = i * 4 + mc
                if mc == 0:
                    eng.wait_ge(sems.c_s3e, i * 8 + 8)
                    if i == 0:
                        eng.wait_ge(sems.s_wstgf, 64)
                if Km >= 1:
                    eng.wait_ge(sems.c_lge, 2 * Km - 1)
                for kc in range(8):
                    a = mm(eng, C["P"][4][:, 0:16],
                           fmv[:, kc, mc * 128:(mc + 1) * 128],
                           C["gateb"][:, kc * 16:(kc + 1) * 16],
                           start=(kc == 0), stop=(kc == 7))
                a.then_inc(sems.c_lg, 1)
                eng.wait_ge(sems.c_disp, 2 * Km + 2)
                if Km >= 1:
                    eng.wait_ge(sems.c_lge, 2 * Km)
                tp(eng, C["P"][5][0:16, 0:128], C["disp"][:, :], identf[:, :],
                   start=True, stop=True).then_inc(sems.c_dtpe, 1)

        for i in range(NTILES):
            transposes(i)
            sfc(i)

    def c_gpsimd_out(eng):
        for i in range(NTILES):
            eng.wait_ge(sems.c_fme, (i + 1) * 8)
            eng.dma_start(
                out=xnfm_d[:, :, i * NT:(i + 1) * NT].rearrange("kc p t -> p kc t"),
                in_=C["fm"][:, :].rearrange("p (kc t) -> p kc t", kc=8),
            ).then_inc(sems.s_oxn, 16)
            for mc in range(4):
                eng.wait_ge(sems.c_s3e, i * 8 + 2 * (mc + 1))
                off = (1 + mc % 2) * D
                eng.dma_start(
                    out=sh_d[i * NT + mc * 128: i * NT + (mc + 1) * 128, :],
                    in_=C["scr"][:, off:off + D],
                ).then_inc([sems.s_osh0, sems.s_osh1][mc % 2], 16)
        nxn = 16 * NTILES
        eng.wait_ge(sems.c_lge, 2 * 4 * NTILES)
        eng.dma_start(out=dispt_d, in_=C["dispT"][:, :]).then_inc(sems.s_oxn, 16)
        nxn += 16
        if dbg:
            eng.wait_ge(sems.c_lge, 2 * 4 * NTILES)
        if "dispdbg" in dbg:
            eng.dma_start(out=dbg["dispdbg"], in_=C["dispT"][:, :]).then_inc(sems.s_oxn, 16)
            nxn += 16
        if "statsdbg" in dbg:
            eng.dma_start(out=dbg["statsdbg"][:, 0:NCH], in_=negm1[:, :]).then_inc(sems.s_oxn, 16)
            eng.dma_start(out=dbg["statsdbg"][:, NCH:2 * NCH], in_=rstd1[:, :]).then_inc(sems.s_oxn, 16)
            eng.dma_start(out=dbg["statsdbg"][:, 2 * NCH:3 * NCH], in_=negm2[:, :]).then_inc(sems.s_oxn, 16)
            eng.dma_start(out=dbg["statsdbg"][:, 3 * NCH:4 * NCH], in_=rstd2[:, :]).then_inc(sems.s_oxn, 16)
            nxn += 64
        eng.wait_ge(sems.s_oxn, nxn)
        eng.wait_ge(sems.s_osh0, 16 * 2 * NTILES)
        eng.wait_ge(sems.s_osh1, 16 * 2 * NTILES)

    DE_TOT = dict(wstf=256, fm=16 * NTILES, st=16 * 2 * NTILES,
                  dp=16 * 4 * NTILES, h=32 * NTILES, bc=8 * NTILES,
                  silu=32 * NTILES, hsc=32 * NTILES, acc=8 * NTILES,
                  ev=4 * NTILES, gf=4 * NTILES, out=16 * 2 * NTILES)

    def de_bases(pi):
        b = {k: pi * v for k, v in DE_TOT.items()}
        b["gf"] = 0
        return b

    def de_sync(eng, bs, ge0, stream_d):
        for i in range(NTILES):
            w = bs["h"] + 32 * i
            if w > 0:
                eng.wait_ge(sems.d_h, w)
            eng.dma_start(
                out=DE["fm"][:, :].rearrange("p (kc t) -> p kc t", kc=8),
                in_=xnfm_d[:, :, i * NT:(i + 1) * NT].rearrange("kc p t -> p kc t"),
            ).then_inc(sems.d_fm, 16)
            for e in range(8):
                w = bs["bc"] + 8 * i + e - 1
                if w > 0:
                    eng.wait_ge(sems.d_bc, w)
                eng.dma_start(
                    out=DE["dprow"][0:1, (e % 2) * 512:(e % 2) * 512 + 512],
                    in_=dispt_d[ge0 + e:ge0 + e + 1, i * NT:(i + 1) * NT],
                ).then_inc([sems.d_dp0, sems.d_dp1][e % 2], 16)
            for mc in range(4):
                w = bs["ev"] + 4 * i + mc - 1
                if w > 0:
                    eng.wait_ge(sems.d_ev, w)
                eng.dma_start(
                    out=DE["stream"][:, (mc % 2) * D:(mc % 2) * D + D],
                    in_=stream_d[i * NT + mc * 128: i * NT + (mc + 1) * 128, :],
                ).then_inc([sems.d_st0, sems.d_st1][mc % 2], 16)
        eng.wait_ge(sems.d_fm, bs["fm"] + 16 * NTILES)
        eng.wait_ge(sems.d_dp0, bs["dp"] + 16 * 4 * NTILES)
        eng.wait_ge(sems.d_dp1, bs["dp"] + 16 * 4 * NTILES)
        eng.wait_ge(sems.d_st0, bs["st"] + 16 * 2 * NTILES)
        eng.wait_ge(sems.d_st1, bs["st"] + 16 * 2 * NTILES)

    def de_pool(eng, bs, is_e):
        ge0 = 8 if is_e else 0
        for e in range(8):
            eng.dma_start(
                out=DE["w1b"][:, e * 4096:(e + 1) * 4096]
                    .rearrange("p (kc h) -> p kc h", kc=8),
                in_=w1t_d[ge0 + e].rearrange("(kc p) h -> p kc h", p=128),
            ).then_inc(sems.d_wstf, 16)
            eng.dma_start(
                out=DE["w2b"][:, e * 4096:(e + 1) * 4096]
                    .rearrange("p (kc d) -> p kc d", kc=4),
                in_=w2t_d[ge0 + e].rearrange("(kc p) d -> p kc d", p=128),
            ).then_inc(sems.d_wstf, 16)
        for i in range(NTILES):
            for mc in range(4):
                if is_e:
                    eng.wait_ge(sems.d_gf, bs["gf"] + 4 * i + mc + 1)
                    srcb = DE["gfh"][:, (mc % 2) * D:(mc % 2) * D + D]
                    dstd = gf_d[i * NT + mc * 128: i * NT + (mc + 1) * 128, :]
                else:
                    eng.wait_ge(sems.d_ev, bs["ev"] + 4 * i + mc + 1)
                    srcb = DE["outh"][:, (mc % 2) * D:(mc % 2) * D + D]
                    dstd = moeps_d[i * NT + mc * 128: i * NT + (mc + 1) * 128, :]
                eng.dma_start(out=dstd, in_=srcb).then_inc(
                    [sems.d_out0, sems.d_out1][mc % 2], 16)
        eng.wait_ge(sems.d_out0, bs["out"] + 16 * 2 * NTILES)
        eng.wait_ge(sems.d_out1, bs["out"] + 16 * 2 * NTILES)

    def de_tensor(eng, bs, is_e):
        fmv = DE["fm"][:, :].rearrange("p (kc t) -> p kc t", kc=8)
        hscv = DE["hsc"][:, :].rearrange("p (e hc t) -> p e hc t", e=8, hc=4)
        for i in range(NTILES):
            eng.wait_ge(sems.d_fm, bs["fm"] + 16 * (i + 1))
            for e in range(8):
                Ke = 8 * i + e
                if i == 0:
                    eng.wait_ge(sems.d_wstf, bs["wstf"] + 16 * (2 * e + 1))
                w = bs["silu"] + 4 * Ke
                if w > 0:
                    eng.wait_ge(sems.d_silu, w)
                for hc in range(4):
                    for kc in range(8):
                        a = mm(eng, DE["P"][hc][:, :],
                               DE["w1b"][:, e * 4096 + kc * 512 + hc * 128:
                                         e * 4096 + kc * 512 + (hc + 1) * 128],
                               fmv[:, kc, :], start=(kc == 0), stop=(kc == 7))
                    a.then_inc(sems.d_h, 1)
                eng.wait_ge([sems.d_dp0, sems.d_dp1][e % 2],
                            bs["dp"] + 16 * (4 * i + e // 2 + 1))
                w = bs["hsc"] + 4 * (Ke - 1)
                if w > 0:
                    eng.wait_ge(sems.d_hsc, w)
                if e < 2 and (bs["ev"] + 4 * i) > 0:
                    eng.wait_ge(sems.d_ev, bs["ev"] + 4 * i)
                mm(eng, DE["P"][4 + e % 2][:, :], onesb[0:1, :],
                   DE["dprow"][0:1, (e % 2) * 512:(e % 2) * 512 + 512],
                   start=True, stop=True).then_inc(sems.d_bc, 1)
            eng.wait_ge(sems.d_hsc, bs["hsc"] + 32 * (i + 1))
            if i == 0:
                eng.wait_ge(sems.d_wstf, bs["wstf"] + 16 * 16)
            for mc in range(4):
                w = bs["ev"] + 4 * i + mc - 1
                if w > 0:
                    eng.wait_ge(sems.d_ev, w)
                ob = 6 if mc % 2 == 0 else 4
                for ng in range(2):
                    for e in range(8):
                        for kc in range(4):
                            a = mm(eng, DE["P"][ob + ng][:, :],
                                   hscv[:, e, kc, mc * 128:(mc + 1) * 128],
                                   DE["w2b"][:, e * 4096 + kc * 1024 + ng * 512:
                                             e * 4096 + kc * 1024 + ng * 512 + 512],
                                   start=(e == 0 and kc == 0),
                                   stop=(e == 7 and kc == 3))
                    a.then_inc(sems.d_acc, 1)

    def de_act(eng, bs, is_e):
        for i in range(NTILES):
            for e in range(8):
                for hc in range(4):
                    ghc = 32 * i + 4 * e + hc
                    eng.wait_ge(sems.d_h, bs["h"] + ghc + 1)
                    w = bs["hsc"] + ghc - 1
                    if w > 0:
                        eng.wait_ge(sems.d_hsc, w)
                    eng.activation(out=DE["silu"][:, (ghc % 2) * 512:(ghc % 2) * 512 + 512],
                                   in_=DE["P"][hc][:, :],
                                   func=AF.Silu).then_inc(sems.d_silu, 1)
            if is_e:
                for mc in range(4):
                    eng.wait_ge(sems.d_ev, bs["ev"] + 4 * i + mc + 1)
                    w = bs["out"] + 16 * (2 * i + mc // 2)
                    if w > 0:
                        eng.wait_ge([sems.d_out0, sems.d_out1][mc % 2], w)
                    eng.activation(out=DE["gfh"][:, (mc % 2) * D:(mc % 2) * D + D],
                                   in_=DE["outh"][:, (mc % 2) * D:(mc % 2) * D + D],
                                   func=AF.Silu).then_inc(sems.d_gf, 1)

    def de_vector(eng, bs, is_e):
        hscv = DE["hsc"][:, :].rearrange("p (e hc t) -> p e hc t", e=8, hc=4)
        for i in range(NTILES):
            for e in range(8):
                for hc in range(4):
                    ghc = 32 * i + 4 * e + hc
                    eng.wait_ge(sems.d_silu, bs["silu"] + ghc + 1)
                    eng.wait_ge(sems.d_bc, bs["bc"] + 8 * i + e + 1)
                    eng.tensor_mul(out=hscv[:, e, hc, :],
                                   in0=DE["silu"][:, (ghc % 2) * 512:(ghc % 2) * 512 + 512],
                                   in1=DE["P"][4 + e % 2][:, :]).then_inc(sems.d_hsc, 1)
            for mc in range(4):
                eng.wait_ge(sems.d_acc, bs["acc"] + 8 * i + 2 * mc + 2)
                eng.wait_ge([sems.d_st0, sems.d_st1][mc % 2],
                            bs["st"] + 16 * (2 * i + mc // 2 + 1))
                if is_e:
                    w = bs["gf"] + 4 * i + mc - 1
                    if w > 0:
                        eng.wait_ge(sems.d_gf, w)
                else:
                    w = bs["out"] + 16 * (2 * i + mc // 2)
                    if w > 0:
                        eng.wait_ge([sems.d_out0, sems.d_out1][mc % 2], w)
                off = (mc % 2) * D
                ob = 6 if mc % 2 == 0 else 4
                eng.tensor_add(out=DE["outh"][:, off:off + 512],
                               in0=DE["P"][ob][:, :],
                               in1=DE["stream"][:, off:off + 512])
                eng.tensor_add(out=DE["outh"][:, off + 512:off + D],
                               in0=DE["P"][ob + 1][:, :],
                               in1=DE["stream"][:, off + 512:off + D]
                               ).then_inc(sems.d_ev, 1)

    A1 = mkphase(lambda a, p: dict(
        wqb=a("A_wqb", [128, 8 * D], BF16),
        wkb=a("A_wkb", [128, 8 * D], BF16),
        wvb=a("A_wvb", [128, 8 * 2048], BF16),
        wgb=a("A_wgb", [128, 8 * 2048], BF16),
        lnwbc=a("A_lnwbc", [128, D], F32),
        cs=a("A_cs", [128, 2 * 4 * NTA], BF16),
        brow=a("A_brow", [1, 2048], F32),
        bqcol=a("A_bqcol", [128, 8], F32),
        bkcol=a("A_bkcol", [128, 8], F32),
        bvrow=a("A_bvrow", [1, 2048], BF16),
        bgrow=a("A_bgrow", [1, 2048], BF16),
        x=a("A_x", [128, 2 * 2 * D], F32),
        xn=a("A_xn", [128, 2 * D], F32),
        ntmp=a("A_ntmp", [128, D], F32),
        nrm=a("A_nrm", [128, 2 * D], BF16),
        nfm=a("A_nfm", [128, 8 * NTA], BF16),
        qf=a("A_qf", [128, 2 * NTA], BF16),
        kf=a("A_kf", [128, 2 * NTA], BF16),
        rt=a("A_rt", [128, 4 * NTA], BF16),
        qp=a("A_qp", [128, 8 * NTA], BF16),
        kp=a("A_kp", [128, 8 * NTA], BF16),
        v=a("A_v", [128, 2 * 2048], BF16),
        sg=a("A_sg", [128, 2 * 2048], BF16),
        P=[p(f"A_P{i}", [128, 512], F32) for i in range(6)],
        TPB=[p(f"A_TPB{i}", [128, 1024], BF16) for i in range(2)],
    ))

    def a1_sync(eng):
        eng.dma_start(out=A1["lnwbc"][:, :], in_=lnwbc_d).then_inc(sems.a_cst, 16)
        for i in range(NTILESA):
            if i >= 2:
                eng.wait_ge(sems.a_xn, i - 1)
            eng.dma_start(
                out=A1["x"][:, (i % 2) * 2 * D:(i % 2) * 2 * D + 2 * D]
                    .rearrange("p (s d) -> p s d", s=2),
                in_=x_d[i * NTA:(i + 1) * NTA, :].rearrange("(s p) d -> p s d", p=128),
            ).then_inc([sems.a_x0, sems.a_x1][i % 2], 16)

    def a1_pool(eng):
        eng.dma_start(out=A1["wqb"][:, :].rearrange("p (kc n) -> p kc n", kc=8),
                      in_=wq_d.rearrange("(kc p) n -> p kc n", p=128)
                      ).then_inc(sems.a_wstf, 16)
        eng.dma_start(out=A1["wkb"][:, :].rearrange("p (kc n) -> p kc n", kc=8),
                      in_=wk_d.rearrange("(kc p) n -> p kc n", p=128)
                      ).then_inc(sems.a_wstf, 16)
        eng.dma_start(out=A1["wvb"][:, :].rearrange("p (kc n) -> p kc n", kc=8),
                      in_=wv_d.rearrange("(kc p) n -> p kc n", p=128)
                      ).then_inc(sems.a_wstf, 16)
        eng.dma_start(out=A1["wgb"][:, :].rearrange("p (kc n) -> p kc n", kc=8),
                      in_=wg_d.rearrange("(kc p) n -> p kc n", p=128)
                      ).then_inc(sems.a_wstf, 16)
        for i in range(NTILESA):
            if i >= 2:
                eng.wait_ge(sems.a_rope, 8 * (i - 1))
            co = (i % 2) * 4 * NTA
            for j, srcd in enumerate([cos_d, sin_d, cosq_d, sinq_d]):
                eng.dma_start(out=A1["cs"][:, co + j * NTA:co + (j + 1) * NTA],
                              in_=srcd[:, i * NTA:(i + 1) * NTA]).then_inc(
                                  [sems.a_cs0, sems.a_cs1][i % 2], 16)
            eng.wait_ge(sems.a_rope, 8 * (i + 1))
            eng.dma_start(
                out=qp_d[:, :, i * NTA:(i + 1) * NTA].rearrange("kc p t -> p kc t"),
                in_=A1["qp"][:, :].rearrange("p (kc t) -> p kc t", kc=8),
            ).then_inc(sems.a_oq, 16)
            eng.dma_start(
                out=kp_d[:, :, i * NTA:(i + 1) * NTA].rearrange("kc p t -> p kc t"),
                in_=A1["kp"][:, :].rearrange("p (kc t) -> p kc t", kc=8),
            ).then_inc(sems.a_oq, 16)
            eng.wait_ge(sems.a_vge, 16 * (i + 1))
            eng.dma_start(
                out=vtm_d[i * NTA:(i + 1) * NTA, :].rearrange("(s p) n -> p s n", p=128),
                in_=A1["v"][:, :].rearrange("p (s n) -> p s n", s=2),
            ).then_inc(sems.a_ov, 16)
            eng.dma_start(
                out=sgtm_d[i * NTA:(i + 1) * NTA, :].rearrange("(s p) n -> p s n", p=128),
                in_=A1["sg"][:, :].rearrange("p (s n) -> p s n", s=2),
            ).then_inc(sems.a_ov, 16)
        eng.wait_ge(sems.a_oq, 16 * 2 * NTILESA)
        eng.wait_ge(sems.a_ov, 16 * 2 * NTILESA)

    def a1_tensor(eng):
        rows = ([("q", A1["wqb"], 1024, hx) for hx in range(2)]
                + [("k", A1["wkb"], 1024, hx) for hx in range(2)]
                + [("v", A1["wvb"], 2048, hx) for hx in range(4)]
                + [("g", A1["wgb"], 2048, hx) for hx in range(4)])
        for r, (kind, wb, wid, hx) in enumerate(rows):
            need = {"q": 16, "k": 32, "v": 48, "g": 64}[kind]
            eng.wait_ge(sems.a_wstf, need)
            if r >= 2:
                eng.wait_ge(sems.a_pree, r - 1)
            bank = A1["P"][5 if r % 2 == 0 else 4]
            for kc in range(8):
                a = mm(eng, bank[0:1, 0:512],
                       lnbcol_b[:, kc:kc + 1],
                       wb[:, kc * wid + hx * 512: kc * wid + (hx + 1) * 512],
                       start=(kc == 0), stop=(kc == 7))
            a.then_inc(sems.a_pre, 1)
        for j in range(16):
            eng.wait_ge(sems.a_pree, 12 + j)
            tp(eng, A1["P"][3][:, 0:1], A1["brow"][0:1, j * 128:(j + 1) * 128],
               identf[0:1, 0:1], start=True, stop=True).then_inc(sems.a_pre, 1)
        eng.wait_ge(sems.a_pree, 28)
        nfmv = A1["nfm"][:, :].rearrange("p (kc t) -> p kc t", kc=8)
        for i in range(NTILESA):
            for st in range(2):
                gt = 2 * i + st
                eng.wait_ge(sems.a_nrm, gt + 1)
                if gt >= 2:
                    eng.wait_ge(sems.a_nfm, gt - 1)
                for kc in range(8):
                    a = tp(eng, A1["TPB"][gt % 2][:, kc * 128:(kc + 1) * 128],
                           A1["nrm"][:, st * D + kc * 128: st * D + (kc + 1) * 128],
                           identb[:, :], start=(kc == 0), stop=(kc == 7))
                a.then_inc(sems.a_tp, 1)
            eng.wait_ge(sems.a_nfm, 2 * (i + 1))
            for hp in range(4):
                for qk in range(2):
                    wb = A1["wqb"] if qk == 0 else A1["wkb"]
                    if hp == 0 and qk == 0 and i >= 1:
                        eng.wait_ge(sems.a_vge, 16 * i)
                    w = 8 * i + (hp - 1) * 2 + qk + 1
                    if w > 0:
                        eng.wait_ge(sems.a_qke, w)
                    for half in range(2):
                        dq = 2 * hp + half
                        bank = A1["P"][qk * 2 + half]
                        for kc in range(8):
                            a = mm(eng, bank[:, 0:NTA],
                                   wb[:, kc * D + dq * 128: kc * D + (dq + 1) * 128],
                                   nfmv[:, kc, :], start=(kc == 0), stop=(kc == 7))
                    a.then_inc(sems.a_qk, 1)
            for st in range(2):
                for vg in range(2):
                    wb = A1["wvb"] if vg == 0 else A1["wgb"]
                    br = A1["bvrow"] if vg == 0 else A1["bgrow"]
                    for ng in range(4):
                        gv = 16 * i + st * 8 + vg * 4 + ng
                        if st == 0 and ng == 1:
                            eng.wait_ge(sems.a_qke, 8 * (i + 1))
                        if gv >= 2:
                            eng.wait_ge(sems.a_vge, gv - 1)
                        bank = A1["P"][(4 + vg) if ng % 2 == 0 else vg]
                        for kc in range(8):
                            mm(eng, bank[:, :],
                               nfmv[:, kc, st * 128:(st + 1) * 128],
                               wb[:, kc * 2048 + ng * 512: kc * 2048 + (ng + 1) * 512],
                               start=(kc == 0), stop=False)
                        mm(eng, bank[:, :], onesb[0:1, 0:128],
                           br[0:1, ng * 512:(ng + 1) * 512],
                           start=False, stop=True).then_inc(sems.a_vg, 1)

    def a1_act(eng):
        rows = (["q0", "q1", "k0", "k1"] + [f"v{j}" for j in range(4)]
                + [f"g{j}" for j in range(4)])
        for r, tag in enumerate(rows):
            eng.wait_ge(sems.a_pre, r + 1)
            bank = A1["P"][5 if r % 2 == 0 else 4]
            kind, hx = tag[0], int(tag[1])
            if kind == "q":
                dst = A1["brow"][0:1, hx * 512:(hx + 1) * 512]
            elif kind == "k":
                dst = A1["brow"][0:1, 1024 + hx * 512: 1024 + (hx + 1) * 512]
            elif kind == "v":
                dst = A1["bvrow"][0:1, hx * 512:(hx + 1) * 512]
            else:
                dst = A1["bgrow"][0:1, hx * 512:(hx + 1) * 512]
            eng.activation(out=dst, in_=bank[0:1, 0:512],
                           func=AF.Identity).then_inc(sems.a_pree, 1)
        for j in range(16):
            eng.wait_ge(sems.a_pre, 12 + j + 1)
            dst = A1["bqcol"][:, j:j + 1] if j < 8 else A1["bkcol"][:, j - 8:j - 7]
            eng.activation(out=dst, in_=A1["P"][3][:, 0:1],
                           func=AF.Identity).then_inc(sems.a_pree, 1)
        for i in range(NTILESA):
            for st in range(2):
                gt = 2 * i + st
                eng.wait_ge(sems.a_tp, gt + 1)
                eng.activation(out=A1["nfm"][:, :]
                               .rearrange("p (kc t) -> p kc t", kc=8)
                               [:, :, st * 128:(st + 1) * 128],
                               in_=A1["TPB"][gt % 2][:, :]
                               .rearrange("p (kc t) -> p kc t", kc=8),
                               func=AF.Identity).then_inc(sems.a_nfm, 1)
            for st in range(2):
                for vg in range(2):
                    for ng in range(4):
                        gv = 16 * i + st * 8 + vg * 4 + ng
                        eng.wait_ge(sems.a_vg, gv + 1)
                        if i > 0 and st == 0 and vg == 0 and ng == 0:
                            eng.wait_ge(sems.a_ov, 16 * 2 * i)
                        dstbuf = A1["v"] if vg == 0 else A1["sg"]
                        dst = dstbuf[:, st * 2048 + ng * 512: st * 2048 + (ng + 1) * 512]
                        bank = A1["P"][(4 + vg) if ng % 2 == 0 else vg]
                        eng.activation(out=dst, in_=bank[:, :],
                                       func=AF.Identity if vg == 0 else AF.Silu
                                       ).then_inc(sems.a_vge, 1)

    def a1_vector(eng):
        for i in range(NTILESA):
            eng.wait_ge([sems.a_x0, sems.a_x1][i % 2], 16 * (i // 2 + 1))
            eng.wait_ge(sems.a_cst, 16)
            xoff = (i % 2) * 2 * D
            for st in range(2):
                c = 2 * i + st
                a = eng.tensor_scalar(out=A1["xn"][:, st * D:(st + 1) * D],
                                      in0=A1["x"][:, xoff + st * D: xoff + (st + 1) * D],
                                      scalar1=negm1[:, c:c + 1], scalar2=rstd1[:, c:c + 1],
                                      op0=ALU.add, op1=ALU.mult)
            a.then_inc(sems.a_xn, 1)
            for st in range(2):
                c = 2 * i + st
                eng.tensor_scalar(out=A1["ntmp"][:, :],
                                  in0=A1["xn"][:, st * D:(st + 1) * D],
                                  scalar1=negm2[:, c:c + 1], scalar2=rstd2[:, c:c + 1],
                                  op0=ALU.add, op1=ALU.mult)
                eng.tensor_mul(out=A1["nrm"][:, st * D:(st + 1) * D],
                               in0=A1["ntmp"][:, :],
                               in1=A1["lnwbc"][:, :]).then_inc(sems.a_nrm, 1)
            eng.wait_ge([sems.a_cs0, sems.a_cs1][i % 2], 64 * (i // 2 + 1))
            co = (i % 2) * 4 * NTA
            for hp in range(4):
                for qk in range(2):
                    gq = 8 * i + hp * 2 + qk
                    eng.wait_ge(sems.a_qk, gq + 1)
                    if i > 0 and hp == 0 and qk == 0:
                        eng.wait_ge(sems.a_oq, 16 * 2 * i)
                    bcol = A1["bqcol"] if qk == 0 else A1["bkcol"]
                    fbuf = A1["qf"] if qk == 0 else A1["kf"]
                    obuf = A1["qp"] if qk == 0 else A1["kp"]
                    pA = A1["P"][qk * 2][:, 0:NTA]
                    pB = A1["P"][qk * 2 + 1][:, 0:NTA]
                    eng.tensor_scalar(out=fbuf[:, 0:NTA], in0=pA,
                                      scalar1=bcol[:, 2 * hp:2 * hp + 1], scalar2=None,
                                      op0=ALU.add)
                    a = eng.tensor_scalar(out=fbuf[:, NTA:2 * NTA], in0=pB,
                                          scalar1=bcol[:, 2 * hp + 1:2 * hp + 2],
                                          scalar2=None, op0=ALU.add)
                    a.then_inc(sems.a_qke, 1)
                    fA = fbuf[:, 0:NTA]
                    fB = fbuf[:, NTA:2 * NTA]
                    rt = A1["rt"]
                    jo = 2 * NTA if qk == 0 else 0
                    coss = A1["cs"][:, co + jo:co + jo + NTA]
                    sins = A1["cs"][:, co + jo + NTA:co + jo + 2 * NTA]
                    eng.tensor_mul(out=rt[:, 0:NTA], in0=fA, in1=coss)
                    eng.tensor_mul(out=rt[:, NTA:2 * NTA], in0=fB, in1=sins)
                    eng.tensor_mul(out=rt[:, 2 * NTA:3 * NTA], in0=fB, in1=coss)
                    eng.tensor_mul(out=rt[:, 3 * NTA:4 * NTA], in0=fA, in1=sins)
                    eng.tensor_sub(out=obuf[:, (2 * hp) * NTA:(2 * hp + 1) * NTA],
                                   in0=rt[:, 0:NTA], in1=rt[:, NTA:2 * NTA])
                    eng.tensor_add(out=obuf[:, (2 * hp + 1) * NTA:(2 * hp + 2) * NTA],
                                   in0=rt[:, 2 * NTA:3 * NTA],
                                   in1=rt[:, 3 * NTA:4 * NTA]).then_inc(sems.a_rope, 1)

    def a1_endstate():
        return [(sems.a_vg, 16 * NTILESA), (sems.a_rope, 8 * NTILESA),
                (sems.a_vge, 16 * NTILESA), (sems.a_wstf, 64),
                (sems.a_oq, 16 * 2 * NTILESA), (sems.a_ov, 16 * 2 * NTILESA),
                (sems.a_x0, 16 * NTILESA // 2), (sems.a_x1, 16 * NTILESA // 2),
                (sems.a_cs0, 16 * NTILESA), (sems.a_cs1, 16 * NTILESA),
                (sems.a_cst, 16), (sems.a_nfm, 2 * NTILESA),
                (sems.a_pree, 28), (sems.a_qke, 8 * NTILESA),
                (sems.a_xn, NTILESA), (sems.a_tp, 2 * NTILESA),
                (sems.a_qk, 8 * NTILESA), (sems.a_nrm, 2 * NTILESA)]

    def b_sync(eng):
        eng.dma_start(out=B["gnw"][:, :], in_=gnwcol_d).then_inc(sems.b_cst, 16)
        eng.dma_start(out=B["dmask"][:, :].rearrange("p (h i) -> p h i", h=H),
                      in_=dmaskt_d.rearrange("h p i -> p h i")).then_inc(sems.b_cst, 16)
        eng.dma_start(out=B["cross"][:, :].rearrange("p (h i) -> p h i", h=H),
                      in_=crossb_d.rearrange("h p i -> p h i")).then_inc(sems.b_cst, 16)
        eng.dma_start(out=B["kdec"][:, :], in_=kdeccol_d).then_inc(sems.b_cst, 16)
        for n in range(NCH):
            sl = n % 2
            if n >= 2:
                eng.wait_ge(sems.b_stt, 4 * (n - 1))
            eng.dma_start(
                out=B["q"][:, sl * 1024:(sl + 1) * 1024]
                    .rearrange("p (kc t) -> p kc t", kc=8),
                in_=qp_d[:, :, n * CB:(n + 1) * CB].rearrange("kc p t -> p kc t"),
            ).then_inc([sems.b_iq0, sems.b_iq1][sl], 16)
            eng.dma_start(
                out=B["k"][:, sl * 1024:(sl + 1) * 1024]
                    .rearrange("p (kc t) -> p kc t", kc=8),
                in_=kp_d[:, :, n * CB:(n + 1) * CB].rearrange("kc p t -> p kc t"),
            ).then_inc([sems.b_ik0, sems.b_ik1][sl], 16)
            eng.dma_start(
                out=B["v"][:, sl * 2048:(sl + 1) * 2048],
                in_=vtm_d[n * CB:(n + 1) * CB, :],
            ).then_inc([sems.b_iv0, sems.b_iv1][sl], 16)
            if n >= 2:
                eng.wait_ge(sems.b_og, 4 * (n - 1))
            eng.dma_start(
                out=B["sgs"][:, sl * 2048:(sl + 1) * 2048],
                in_=sgtm_d[n * CB:(n + 1) * CB, :],
            ).then_inc([sems.b_isg0, sems.b_isg1][sl], 16)
            if n >= 2:
                eng.wait_ge(sems.b_comb, n - 1)
            eng.dma_start(
                out=B["xs"][:, sl * D:(sl + 1) * D],
                in_=x_d[n * CB:(n + 1) * CB, :],
            ).then_inc([sems.b_ix0, sems.b_ix1][sl], 16)
            eng.dma_start(
                out=B["gfs"][:, sl * D:(sl + 1) * D],
                in_=gf_d[n * CB:(n + 1) * CB, :],
            ).then_inc([sems.b_igf0, sems.b_igf1][sl], 16)

    def b_pool(eng):
        eng.dma_start(out=B["wraw"][:, :].rearrange("p (kc n) -> p kc n", kc=16),
                      in_=wo_d.rearrange("(kc p) n -> p kc n", p=128)
                      ).then_inc(sems.b_wstf, 16)
        eng.memset(B["epsc"][:, :], EPS).then_inc(sems.b_rc1, 1)
        eng.memset(B["Sf"][:, :], 0.0)
        eng.memset(B["Sb"][:, :], 0.0).then_inc(sems.b_sbf, 4)
        for n in range(NCH):
            sl = n % 2
            eng.wait_ge(sems.b_woev, n + 1)
            eng.wait_ge([sems.b_ix0, sems.b_ix1][sl], 16 * (n // 2 + 1))
            if n >= 2:
                eng.wait_ge([sems.b_oo0, sems.b_oo1][sl], 16 * (n // 2))
            eng.tensor_add(out=B["outt"][:, sl * D:(sl + 1) * D],
                           in0=B["t1"][:, :],
                           in1=B["xs"][:, sl * D:(sl + 1) * D]).then_inc(sems.b_comb, 1)
            eng.wait_ge(sems.b_comb, n + 1)
            eng.dma_start(out=out_d[n * CB:(n + 1) * CB, :],
                          in_=B["outt"][:, sl * D:(sl + 1) * D]).then_inc(
                              [sems.b_oo0, sems.b_oo1][sl], 16)
        eng.wait_ge(sems.b_oo0, 16 * NCH // 2)
        eng.wait_ge(sems.b_oo1, 16 * NCH // 2)

    def b_tensor(eng):
        def kt_pass(m):
            sl2 = m % 2
            eng.wait_ge([sems.b_ik0, sems.b_ik1][sl2], 16 * (m // 2 + 1))
            for h in range(H):
                if 4 * m + h >= 1:
                    eng.wait_ge(sems.b_k2, 4 * m + h)
                for c in range(2):
                    a = tp(eng, B["KTB"][:, c * 128:(c + 1) * 128],
                           B["k"][:, sl2 * 1024 + (2 * h + c) * CB: sl2 * 1024 + (2 * h + c + 1) * CB],
                           identb[:, :], start=(c == 0), stop=(c == 1))
                a.then_inc(sems.b_pe, 1)

        def ogtp_wo(m):
            for w in range(2):
                eng.wait_ge(sems.b_og, 4 * m + 2 * (w + 1))
                if 2 * m + w >= 1:
                    eng.wait_ge(sems.b_oge, 2 * m + w)
                for j in range(8):
                    fc = w * 8 + j
                    a = tp(eng, B["OGB"][:, j * 128:(j + 1) * 128],
                           B["og"][:, fc * 128:(fc + 1) * 128],
                           identb[:, :], start=(j == 0), stop=(j == 7))
                a.then_inc(sems.b_ogt, 1)
            eng.wait_ge(sems.b_oge, 2 * m + 2)
            if m == 0:
                eng.wait_ge(sems.b_wsc, 16)
            else:
                eng.wait_ge(sems.b_woev, m)
            for ng in range(2):
                for fc in range(16):
                    a = mm(eng, B["P"][4 + ng][:, :],
                           B["ogfm"][:, fc * 128:(fc + 1) * 128],
                           B["wob"][:, fc * D + ng * 512: fc * D + ng * 512 + 512],
                           start=(fc == 0), stop=(fc == 15))
            a.then_inc(sems.b_wo, 1)

        kt_pass(0)
        for n in range(NCH):
            sl = n % 2
            eng.wait_ge([sems.b_iq0, sems.b_iq1][sl], 16 * (n // 2 + 1))
            for h in range(H):
                if h == 0 and n >= 1:
                    eng.wait_ge(sems.b_stt, 4 * n)
                if h >= 2:
                    eng.wait_ge(sems.b_mask, 4 * n + h - 1)
                for c in range(2):
                    a = mm(eng, B["P"][2 + h % 2][:, 0:CB],
                           B["k"][:, sl * 1024 + (2 * h + c) * CB: sl * 1024 + (2 * h + c + 1) * CB],
                           B["q"][:, sl * 1024 + (2 * h + c) * CB: sl * 1024 + (2 * h + c + 1) * CB],
                           start=(c == 0), stop=(c == 1))
                a.then_inc(sems.b_pe, 1)
            if n >= 1:
                ogtp_wo(n - 1)
            eng.wait_ge([sems.b_iv0, sems.b_iv1][sl], 16 * (n // 2 + 1))
            for h in range(H):
                gk = 4 * n + h
                eng.wait_ge(sems.b_mask, gk + 1)
                eng.wait_ge(sems.b_q2, gk + 1)
                eng.wait_ge(sems.b_sbf, 4 * n + h + 1)
                if gk >= 2:
                    eng.wait_ge(sems.b_og, gk - 1)
                mm(eng, B["P"][h % 2][:, :], B["at"][:, h * CB:(h + 1) * CB],
                   B["v"][:, sl * 2048 + h * DV: sl * 2048 + (h + 1) * DV],
                   start=True, stop=False)
                for c in range(2):
                    mm(eng, B["P"][h % 2][:, :],
                       B["q2"][:, (2 * h + c) * CB:(2 * h + c + 1) * CB],
                       B["Sb"][:, h * 1024 + c * 512: h * 1024 + (c + 1) * 512],
                       start=False, stop=(c == 1))
                if h == 0:
                    eng.wait_ge(sems.b_mask, 4 * n + 4)
                if gk >= 1:
                    eng.wait_ge(sems.b_stt, gk)
                eng.wait_ge(sems.b_k2, gk + 1)
                for c in range(2):
                    a = mm(eng, B["P"][2 + c][:, :],
                           B["k2"][:, h * DK + c * CB: h * DK + (c + 1) * CB],
                           B["v"][:, sl * 2048 + h * DV: sl * 2048 + (h + 1) * DV],
                           start=True, stop=True)
                a.then_inc(sems.b_pe, 1)
            if n + 1 < NCH:
                kt_pass(n + 1)
        ogtp_wo(NCH - 1)

    def b_act(eng):
        eng.wait_ge(sems.b_cst, 64)
        eng.wait_ge(sems.b_rc1, 1)
        def oge(m):
            for w in range(2):
                eng.wait_ge(sems.b_ogt, 2 * m + w + 1)
                if m >= 1 and w == 0:
                    eng.wait_ge(sems.b_wo, m)
                eng.activation(out=B["ogfm"][:, w * 1024:(w + 1) * 1024],
                               in_=B["OGB"][:, :],
                               func=AF.Identity).then_inc(sems.b_oge, 1)

        for n in range(NCH):
            for h in range(H):
                gk = 4 * n + h
                pe_kt = (h + 1) if n == 0 else (4 + 12 * (n - 1) + 9 + h)
                eng.wait_ge(sems.b_pe, pe_kt)
                eng.activation(out=B["k2"][:, h * DK:(h + 1) * DK],
                               in_=B["KTB"][:, 0:256],
                               func=AF.Identity,
                               scale=B["kdec"][:, h:h + 1]).then_inc(sems.b_k2, 1)
            if n >= 1:
                oge(n - 1)
            for h in range(H):
                gk = 4 * n + h
                eng.wait_ge(sems.b_pe, 4 + 12 * n + 5 + h)
                if gk >= 4:
                    eng.wait_ge(sems.b_rc3, gk - 3)
                eng.activation(out=B["sq"][:, :], in_=B["P"][h % 2][:, :],
                               func=AF.Square,
                               accum_out=B["cols"][:, h:h + 1]).then_inc(sems.b_sq, 1)
                eng.drain()
                eng.activation(out=B["cols"][:, 8 + h:9 + h], in_=B["cols"][:, h:h + 1],
                               func=AF.Sqrt, scale=1.0 / DV,
                               bias=B["epsc"][:, 0:1]).then_inc(sems.b_rc2, 1)
                eng.wait_ge(sems.b_stt, gk + 1)
                eng.activation(out=B["Sb"][:, h * 1024:(h + 1) * 1024],
                               in_=B["Sf"][:, (n % 2) * 4096 + h * 1024:
                                           (n % 2) * 4096 + (h + 1) * 1024],
                               func=AF.Identity).then_inc(sems.b_sbf, 1)
        oge(NCH - 1)

    def b_vector(eng):
        def so_t1(m):
            ms = m % 2
            eng.wait_ge(sems.b_wo, m + 1)
            if m >= 1:
                eng.wait_ge(sems.b_comb, m)
            eng.tensor_add(out=B["so"][:, 0:512], in0=B["P"][4][:, :],
                           in1=B["xn"][:, ms * D:ms * D + 512])
            eng.tensor_add(out=B["so"][:, 512:D], in0=B["P"][5][:, :],
                           in1=B["xn"][:, ms * D + 512:(ms + 1) * D])
            eng.wait_ge([sems.b_igf0, sems.b_igf1][ms], 16 * (m // 2 + 1))
            eng.tensor_mul(out=B["t1"][:, :], in0=B["so"][:, :],
                           in1=B["gfs"][:, ms * D:(ms + 1) * D]).then_inc(sems.b_woev, 1)

        eng.wait_ge(sems.b_cst, 64)
        eng.wait_ge(sems.b_wstf, 16)
        for kc in range(16):
            eng.tensor_scalar(out=B["wob"][:, kc * D:(kc + 1) * D],
                              in0=B["wraw"][:, kc * D:(kc + 1) * D],
                              scalar1=B["gnw"][:, kc % 4:kc % 4 + 1], scalar2=None,
                              op0=ALU.mult).then_inc(sems.b_wsc, 1)
        for n in range(NCH):
            sl = n % 2
            eng.wait_ge([sems.b_ix0, sems.b_ix1][sl], 16 * (n // 2 + 1))
            eng.tensor_scalar(out=B["xn"][:, sl * D:(sl + 1) * D],
                              in0=B["xs"][:, sl * D:(sl + 1) * D],
                              scalar1=negm1[:, n:n + 1], scalar2=rstd1[:, n:n + 1],
                              op0=ALU.add, op1=ALU.mult)
            for h in range(H):
                eng.wait_ge(sems.b_pe, 4 + 12 * n + h + 1)
                eng.tensor_mul(out=B["at"][:, h * CB:(h + 1) * CB],
                               in0=B["P"][2 + h % 2][:, 0:CB],
                               in1=B["dmask"][:, h * CB:(h + 1) * CB]
                               ).then_inc(sems.b_mask, 1)
            eng.wait_ge([sems.b_iq0, sems.b_iq1][sl], 16 * (n // 2 + 1))
            for h in range(H):
                for c in range(2):
                    a = eng.tensor_mul(
                        out=B["q2"][:, (2 * h + c) * CB:(2 * h + c + 1) * CB],
                        in0=B["q"][:, sl * 1024 + (2 * h + c) * CB: sl * 1024 + (2 * h + c + 1) * CB],
                        in1=B["cross"][:, h * CB:(h + 1) * CB])
                a.then_inc(sems.b_q2, 1)
            if n >= 1:
                so_t1(n - 1)
            for h in range(H):
                gk = 4 * n + h
                eng.wait_ge(sems.b_pe, 4 + 12 * n + 5 + h)
                if n == 0:
                    eng.wait_ge(sems.b_sbf, 4)
                for c in range(2):
                    a = eng.scalar_tensor_tensor(
                        out=B["Sf"][:, (n % 2) * 4096 + h * 1024 + c * 512:
                                    (n % 2) * 4096 + h * 1024 + (c + 1) * 512],
                        in0=B["Sf"][:, ((n + 1) % 2) * 4096 + h * 1024 + c * 512:
                                    ((n + 1) % 2) * 4096 + h * 1024 + (c + 1) * 512],
                        scalar=GC[h], in1=B["P"][2 + c][:, :],
                        op0=ALU.mult, op1=ALU.add)
                a.then_inc(sems.b_stt, 1)
                eng.wait_ge(sems.b_rc2, gk + 1)
                eng.reciprocal(out=B["cols"][:, 12 + h:13 + h],
                               in_=B["cols"][:, 8 + h:9 + h]).then_inc(sems.b_rc3, 1)
                eng.drain()
                eng.wait_ge([sems.b_isg0, sems.b_isg1][sl], 16 * (n // 2 + 1))
                eng.tensor_scalar(out=B["sgn"][:, :],
                                  in0=B["sgs"][:, sl * 2048 + h * DV: sl * 2048 + (h + 1) * DV],
                                  scalar1=B["cols"][:, 12 + h:13 + h], scalar2=None,
                                  op0=ALU.mult)
                if n >= 1:
                    eng.wait_ge(sems.b_ogt, 2 * n - 1 if h < 2 else 2 * n)
                eng.tensor_mul(out=B["og"][:, h * DV:(h + 1) * DV],
                               in0=B["P"][h % 2][:, :],
                               in1=B["sgn"][:, :]).then_inc(sems.b_og, 1)
        so_t1(NCH - 1)

    def c_endstate():
        nxn = 16 * NTILES + 16
        if "dispdbg" in dbg:
            nxn += 16
        if "statsdbg" in dbg:
            nxn += 64
        return [(sems.c_dtpe, 4 * NTILES), (sems.c_lge, 8 * NTILES),
                (sems.c_dtp, 4 * NTILES), (sems.s_wstgf, 80),
                (sems.s_oxn, nxn),
                (sems.s_osh0, 16 * 2 * NTILES), (sems.s_osh1, 16 * 2 * NTILES),
                (sems.s_cst, 64), (sems.s_in, 16 * NTILES)]

    def de_endstate(pi):
        t = DE_TOT
        out = [(sems.d_acc, pi * t["acc"]), (sems.d_ev, pi * t["ev"]),
               (sems.d_silu, pi * t["silu"]), (sems.d_wstf, pi * t["wstf"]),
               (sems.d_out0, pi * t["out"]), (sems.d_out1, pi * t["out"]),
               (sems.d_fm, pi * t["fm"]),
               (sems.d_dp0, pi * t["dp"]), (sems.d_dp1, pi * t["dp"]),
               (sems.d_st0, pi * t["st"]), (sems.d_st1, pi * t["st"]),
               (sems.d_hsc, pi * t["hsc"]), (sems.d_bc, pi * t["bc"]),
               (sems.d_h, pi * t["h"])]
        if pi >= 2:
            out.append((sems.d_gf, t["gf"]))
        return out

    def wait_list(eng, lst):
        for s, v in lst:
            eng.wait_ge(s, v)

    with nc.Block() as block:
        @block.sync
        def _(eng):
            if "C" in phases:
                c_sync(eng)
            if "D" in phases:
                wait_list(eng, c_endstate())
                de_sync(eng, de_bases(0), 0, sh_d)
            if "Ep" in phases:
                wait_list(eng, de_endstate(1))
                de_sync(eng, de_bases(1), 8, moeps_d)
            if "A1" in phases:
                wait_list(eng, de_endstate(2))
                a1_sync(eng)
            if "B" in phases:
                wait_list(eng, a1_endstate())
                b_sync(eng)

        @block.tensor
        def _(eng):
            if "C" in phases:
                c_tensor(eng)
            if "D" in phases:
                wait_list(eng, c_endstate())
                de_tensor(eng, de_bases(0), False)
            if "Ep" in phases:
                wait_list(eng, de_endstate(1))
                de_tensor(eng, de_bases(1), True)
            if "A1" in phases:
                wait_list(eng, de_endstate(2))
                a1_tensor(eng)
            if "B" in phases:
                wait_list(eng, a1_endstate())
                b_tensor(eng)

        @block.vector
        def _(eng):
            if "C" in phases:
                c_vector(eng)
            if "D" in phases:
                wait_list(eng, c_endstate())
                de_vector(eng, de_bases(0), False)
            if "Ep" in phases:
                wait_list(eng, de_endstate(1))
                de_vector(eng, de_bases(1), True)
            if "A1" in phases:
                wait_list(eng, de_endstate(2))
                a1_vector(eng)
            if "B" in phases:
                wait_list(eng, a1_endstate())
                b_vector(eng)

        @block.scalar
        def _(eng):
            if "C" in phases:
                c_act(eng)
            if "D" in phases:
                wait_list(eng, c_endstate())
                de_act(eng, de_bases(0), False)
            if "Ep" in phases:
                wait_list(eng, de_endstate(1))
                de_act(eng, de_bases(1), True)
            if "A1" in phases:
                wait_list(eng, de_endstate(2))
                a1_act(eng)
            if "B" in phases:
                wait_list(eng, a1_endstate())
                b_act(eng)

        @block.gpsimd
        def _(eng):
            if "C" in phases:
                c_pool(eng)
                c_gpsimd_out(eng)
            if "D" in phases:
                wait_list(eng, c_endstate())
                de_pool(eng, de_bases(0), False)
            if "Ep" in phases:
                wait_list(eng, de_endstate(1))
                de_pool(eng, de_bases(1), True)
            if "A1" in phases:
                wait_list(eng, de_endstate(2))
                a1_pool(eng)
            if "B" in phases:
                wait_list(eng, a1_endstate())
                b_pool(eng)

    es.close()
    return nc


_consts = None


def _in_maps(inputs):
    global _consts
    if _consts is None:
        _consts = _host_constants()
    lay = _host_layouts(inputs)
    x = np.ascontiguousarray(np.asarray(inputs["L0_attn_x"], dtype=np.float32))
    base = dict(lay)
    base.update(_consts)
    maps = []
    for c in range(8):
        m = dict(base)
        m["x"] = np.ascontiguousarray(x[c])
        maps.append(m)
    return maps


def kernel(**inputs):
    nc = build_program()
    maps = _in_maps(inputs)
    res = run_bass_kernel_spmd(nc, maps, core_ids=list(range(8)))
    return np.stack([res.results[c]["out"] for c in range(8)], axis=0)
